# version 02
# baseline (speedup 1.0000x reference)
import math
from contextlib import ExitStack

import numpy as np

import concourse.bass as bass
import concourse.mybir as mybir
from concourse.bass_utils import run_bass_kernel_spmd
from concourse.alu_op_type import AluOpType as OP

F32 = mybir.dt.float32
BF16 = mybir.dt.bfloat16
I32 = mybir.dt.int32
AF = mybir.ActivationFunctionType
AX = mybir.AxisListType

T = 4096
D = 1024
NT = T // 128
NC = D // 128
TC = 512
NTC = T // TC

ENGS = ("t", "v", "s", "g", "y")


class Step:
    __slots__ = ("eng", "fn", "deps", "inc", "val", "ch")

    def __init__(self, eng, fn, inc=1, ch=None):
        self.eng, self.fn, self.inc, self.ch = eng, fn, inc, ch
        self.deps = []
        self.val = None


class Em:

    def __init__(self, nc, ctx):
        self.nc = nc
        self.ctx = ctx
        self.steps = []
        self.sems = {e: ctx.enter_context(nc.semaphore(f"S_{e}")) for e in ENGS}
        self.chsems = {}
        self.chcount = {}
        self.writer = {}
        self.readers = {}

    def _dep_list(self, reads, writes):
        deps = []
        for b in reads:
            w = self.writer.get(b)
            if w is not None:
                deps.append(w)
        for b in writes:
            deps.extend(self.readers.get(b, ()))
            w = self.writer.get(b)
            if w is not None:
                deps.append(w)
        out = []
        seen = set()
        for d in deps:
            if id(d) not in seen:
                seen.add(id(d))
                if d.ch is not None:
                    out.append((d, self.chcount[d.ch]))
                else:
                    out.append((d, None))
        return out

    def _record(self, s, reads, writes):
        self.steps.append(s)
        for b in reads:
            self.readers.setdefault(b, []).append(s)
        for b in writes:
            self.writer[b] = s
            self.readers[b] = []

    def op(self, eng, fn, reads=(), writes=(), extra_deps=()):
        s = Step(eng, fn)
        s.deps = self._dep_list(reads, writes) + [(d, None) for d in extra_deps]
        self._record(s, reads, writes)
        return s

    def dma(self, ch, eng, out, in_, reads=(), writes=(), **kw):
        if ch not in self.chsems:
            self.chsems[ch] = self.ctx.enter_context(self.nc.semaphore(f"C_{ch}"))
            self.chcount[ch] = 0
            self.chlast = getattr(self, "chlast", {})
            self.chlast[ch] = None

        def fn(e, out=out, in_=in_, kw=kw):
            return e.dma_start(out, in_, **kw)

        s = Step(eng, fn, inc=16, ch=ch)
        s.deps = self._dep_list(reads, writes)
        prev = self.chlast.get(ch)
        if prev is not None:
            s.deps.append((prev, prev.val))
        self.chcount[ch] += 16
        s.val = self.chcount[ch]
        self.chlast[ch] = s
        self._record(s, reads, writes)
        return s

    def barrier(self):
        lasts = {}
        for s in self.steps:
            lasts[(s.eng, s.ch)] = s
        alll = [(d, self.chcount[d.ch] if d.ch is not None else None)
                for d in lasts.values()]
        newwriter = {}
        for e in ENGS:
            s = Step(e, None)
            s.deps = list(alll)
            self.steps.append(s)
            newwriter[("barrier", e)] = s
        self.writer = newwriter
        self.readers = {}

    def finalize(self, block):
        needed = set()
        for s in self.steps:
            s.deps = [(d, v) for (d, v) in s.deps
                      if not (d.ch is None and d.eng == s.eng == "t")]
            for d, v in s.deps:
                if d.ch is None:
                    needed.add(id(d))
        cnt = {e: 0 for e in ENGS}
        for s in self.steps:
            if s.ch is None and id(s) in needed and s.fn is not None:
                cnt[s.eng] += 1
                s.val = cnt[s.eng]
        final_ch = dict(self.chcount)
        final_eng = dict(cnt)

        def make_emitter(eng_name):
            mysteps = [s for s in self.steps if s.eng == eng_name]

            def emit(engobj):
                hw = {}
                for s in mysteps:
                    for d, vover in s.deps:
                        if d.ch is not None:
                            k, sem, v = ("ch", d.ch), self.chsems[d.ch], vover
                        else:
                            if d.val is None:
                                continue
                            k, sem, v = ("e", d.eng), self.sems[d.eng], d.val
                        if hw.get(k, 0) < v:
                            engobj.wait_ge(sem, v)
                            hw[k] = v
                    if s.fn is None:
                        continue
                    instr = s.fn(engobj)
                    if s.val is not None:
                        if s.ch is not None:
                            instr.then_inc(self.chsems[s.ch], 16)
                        else:
                            instr.then_inc(self.sems[s.eng], 1)
                if eng_name == "y":
                    for ch, v in final_ch.items():
                        if v > 0:
                            engobj.wait_ge(self.chsems[ch], v)
                    for e, v in final_eng.items():
                        if v > 0 and e != "y":
                            engobj.wait_ge(self.sems[e], v)
            return emit

        for e, section in (("t", block.tensor), ("v", block.vector), ("s", block.scalar),
                           ("g", block.gpsimd), ("y", block.sync)):
            section(make_emitter(e))



class Bundle:
    pass


def alloc_common(nc, ctx, em):
    B = Bundle()
    B.nc, B.em, B.ctx = nc, em, ctx
    B.ident = ctx.enter_context(nc.sbuf_tensor("ident_bf", [128, 128], BF16))
    B.iden_f32 = ctx.enter_context(nc.sbuf_tensor("iden_f32", [128, 128], F32))
    B.iota_fp = ctx.enter_context(nc.sbuf_tensor("iota_fp", [128, 512], F32))
    B.ones_t = ctx.enter_context(nc.sbuf_tensor("ones_t", [128, 128], BF16))
    B.consts = ctx.enter_context(nc.sbuf_tensor("consts", [128, 8], F32))
    B.iota_i = ctx.enter_context(nc.sbuf_tensor("iota_i", [128, 512], I32))
    B.xtile = ctx.enter_context(nc.sbuf_tensor("xtile", [128, D], F32))
    B.bn12 = ctx.enter_context(nc.sbuf_tensor("bn12", [128, 12], F32))
    B.mv = ctx.enter_context(nc.sbuf_tensor("mv", [128, 4], F32))
    B.xl = ctx.enter_context(nc.sbuf_tensor("xl", [128, D], BF16))
    B.ps_bf = ctx.enter_context(nc.psum_tensor("ps_bf", [128, 1024], BF16))
    B.ps = [ctx.enter_context(nc.psum_tensor(f"ps{i}", [128, 512], F32)) for i in range(6)]
    em.op("g", lambda g: g.iota(B.iota_i[:, :], [[1, 512]], base=0, channel_multiplier=-1),
          writes=[("iota_i",)])
    em.op("v", lambda v: v.tensor_copy(B.iota_fp[:, :], B.iota_i[:, :]),
          reads=[("iota_i",)], writes=[("iota_fp",)])
    em.op("v", lambda v: v.tensor_scalar(B.ident[:, :], B.iota_fp[:, 0:128], 0.0, None, OP.is_equal),
          reads=[("iota_fp",)], writes=[("ident",)])
    em.op("v", lambda v: v.tensor_scalar(B.iden_f32[:, :], B.iota_fp[:, 0:128], 0.0, None, OP.is_equal),
          reads=[("iota_fp",)], writes=[("iden_f32",)])
    em.op("v", lambda v: v.memset(B.ones_t[:, :], 1.0), writes=[("ones_t",)])
    B.ones_f = ctx.enter_context(nc.sbuf_tensor("ones_f", [128, 128], F32))
    em.op("v", lambda v: v.memset(B.ones_f[:, :], 1.0), writes=[("ones_f",)])
    em.op("v", lambda v: v.memset(B.consts[:, 0:1], 1e-5), writes=[("consts", 0)])
    em.op("v", lambda v: v.memset(B.consts[:, 1:2], 1e-6), writes=[("consts", 1)])
    em.op("v", lambda v: v.memset(B.consts[:, 2:3], 1.0), writes=[("consts", 2)])
    return B


def emit_ln_xlt(B, x_dram, xlt, tag):
    em = B.em
    kx, kbn, kmv, kxl, kps = ("xtile",), ("bn12",), ("mv",), ("xl",), ("psbf",)
    keys = []
    for t in range(NT):
        em.dma("xy", "y", B.xtile[:, :], x_dram[t * 128:(t + 1) * 128, :], writes=[kx])
        em.op("v", lambda v: v.bn_stats(B.bn12[:, 0:6], B.xtile[:, 0:512]),
              reads=[kx], writes=[(tag, "bn6a", t)])
        em.op("v", lambda v: v.bn_stats(B.bn12[:, 6:12], B.xtile[:, 512:1024]),
              reads=[kx], writes=[kbn], extra_deps=[em.writer[(tag, "bn6a", t)]])
        em.op("v", lambda v: v.bn_aggr(B.mv[:, 0:2], B.bn12[:, :]),
              reads=[kbn, (tag, "bn6a", t)], writes=[("mv", 0)])
        em.op("s", lambda s: s.activation(B.mv[:, 2:3], B.mv[:, 1:2], AF.Ln,
                                          bias=B.consts[:, 0:1], scale=1.0),
              reads=[("mv", 0), ("consts", 0)], writes=[("mv", 2)])
        em.op("s", lambda s: s.activation(B.mv[:, 3:4], B.mv[:, 2:3], AF.Exp, scale=-0.5),
              reads=[("mv", 2)], writes=[("mv", 3)])
        em.op("v", lambda v: v.tensor_scalar(B.xl[:, :], B.xtile[:, :],
                                             B.mv[:, 0:1], B.mv[:, 3:4],
                                             OP.subtract, OP.mult),
              reads=[kx, ("mv", 0), ("mv", 3)], writes=[kxl])
        for c in range(NC):
            em.op("t", lambda t_, c=c: t_.transpose(
                B.ps_bf[:, c * 128:(c + 1) * 128], B.xl[:, c * 128:(c + 1) * 128], B.ident[:, :]),
                reads=[kxl, ("ident",)], writes=[kps])
        k = (tag, "xlt", t)
        em.op("v", lambda v, t=t: v.tensor_copy(
            xlt[:, :].rearrange("k (c tt) -> k c tt", c=NC)[:, :, t * 128:(t + 1) * 128],
            B.ps_bf[:, :].rearrange("k (c m) -> k c m", c=NC)),
            reads=[kps], writes=[k])
        keys.append(k)
    return keys


def xlt_chunk(xlt, c, t0, n):
    return xlt[:, c * T + t0: c * T + t0 + n]



def emit_l1f(B, x_dram, wT_dram, xout_dram, tag="l1f"):
    em, nc, ctx = B.em, B.nc, B.ctx
    xlt = ctx.enter_context(nc.sbuf_tensor("l1f_xlt", [128, NC * T], BF16))
    w_f32 = ctx.enter_context(nc.sbuf_tensor("l1f_w", [128, NC * D], F32))
    w_bf = ctx.enter_context(nc.sbuf_tensor("l1f_wq", [128, NC * D], BF16))
    w_rt = ctx.enter_context(nc.sbuf_tensor("l1f_wrt", [128, 2 * D], F32))
    red = ctx.enter_context(nc.sbuf_tensor("l1f_red", [128, NC + 2], F32))
    sc = ctx.enter_context(nc.sbuf_tensor("l1f_sc", [128, 4], F32))
    scp = ctx.enter_context(nc.sbuf_tensor("l1f_scp", [1, 4], F32))
    out_sb = ctx.enter_context(nc.sbuf_tensor("l1f_out", [128, D], F32))
    xres = ctx.enter_context(nc.sbuf_tensor("l1f_xres", [128, D], F32))

    xlt_keys = emit_ln_xlt(B, x_dram, xlt, tag)

    wtr = wT_dram.rearrange("(c k) o -> k c o", c=NC)
    em.dma("wg", "g", w_f32[:, :].rearrange("k (c o) -> k c o", c=NC), wtr,
           writes=[("l1f_w",)])
    for c in range(NC):
        em.op("v", lambda v, c=c: v.reduce_sum(red[:, c:c + 1], w_f32[:, c * D:(c + 1) * D],
                                               axis=AX.X, apply_absolute_value=True),
              reads=[("l1f_w",)], writes=[("red", c)])
    em.op("v", lambda v: v.reduce_sum(red[:, NC:NC + 1], red[:, 0:NC], axis=AX.X),
          reads=[("red", c) for c in range(NC)], writes=[("redt",)])
    em.op("t", lambda t: t.matmul(B.ps[0][0:1, 0:128], red[:, NC:NC + 1],
                                  B.iden_f32[:, :], start=True, stop=True),
          reads=[("redt",), ("iden_f32",)], writes=[("ps", 0)])
    em.op("v", lambda v: v.reduce_sum(scp[0:1, 0:1], B.ps[0][0:1, 0:128], axis=AX.X),
          reads=[("ps", 0)], writes=[("scp", 0)])
    em.op("v", lambda v: v.tensor_scalar(scp[0:1, 1:2], scp[0:1, 0:1], 1.0 / (D * D), 1e-5,
                                         OP.mult, OP.max),
          reads=[("scp", 0)], writes=[("scp", 1)])
    em.op("v", lambda v: v.reciprocal(scp[0:1, 2:3], scp[0:1, 1:2]),
          reads=[("scp", 1)], writes=[("scp", 2)])
    em.op("t", lambda t: t.matmul(B.ps[1][:, 0:2], B.ones_f[0:1, 0:128], scp[0:1, 1:3],
                                  start=True, stop=True),
          reads=[("scp", 1), ("scp", 2), ("ones_f",)], writes=[("ps", 1)])
    em.op("v", lambda v: v.tensor_copy(sc[:, 0:2], B.ps[1][:, 0:2]),
          reads=[("ps", 1)], writes=[("sc",)])
    MAGIC = 12582912.0
    for c in range(NC):
        em.op("v", lambda v, c=c: v.tensor_scalar(w_rt[:, 0:D], w_f32[:, c * D:(c + 1) * D],
                                                  sc[:, 1:2], None, OP.mult),
              reads=[("l1f_w",), ("sc",)], writes=[("w_rt",)])
        em.op("v", lambda v, c=c: v.tensor_scalar(w_rt[:, D:2 * D], w_rt[:, 0:D],
                                                  MAGIC, MAGIC, OP.add, OP.subtract),
              reads=[("w_rt",)], writes=[("w_rt2",)])
        em.op("v", lambda v, c=c: v.tensor_scalar(w_bf[:, c * D:(c + 1) * D], w_rt[:, D:2 * D],
                                                  1.0, -1.0, OP.min, OP.max),
              reads=[("w_rt2",)], writes=[("wq_nm", c)])
        em.op("v", lambda v, c=c: v.tensor_scalar(w_bf[:, c * D:(c + 1) * D],
                                                  w_bf[:, c * D:(c + 1) * D],
                                                  sc[:, 0:1], None, OP.mult),
              reads=[("wq_nm", c), ("sc",)], writes=[("wq", c)])

    wq_keys = [("wq", c) for c in range(NC)]
    for t in range(NT):
        for half in range(2):
            for c in range(NC):
                em.op("t", lambda tt, t=t, c=c, half=half: tt.matmul(
                    B.ps[2 + half][:, :],
                    xlt_chunk(xlt, c, t * 128, 128),
                    w_bf[:, c * D + half * 512: c * D + half * 512 + 512],
                    start=(c == 0), stop=(c == NC - 1)),
                    reads=[xlt_keys[t]] + wq_keys, writes=[("ps", 2 + half)])
        em.dma("xy", "y", xres[:, :], x_dram[t * 128:(t + 1) * 128, :],
               writes=[("xres",)])
        for half in range(2):
            em.op("v", lambda v, t=t, half=half: v.tensor_tensor(
                out_sb[:, half * 512:half * 512 + 512], B.ps[2 + half][:, :],
                xres[:, half * 512:half * 512 + 512], OP.add),
                reads=[("ps", 2 + half), ("xres",)], writes=[("out", half)])
        em.dma("xo", "y", xout_dram[t * 128:(t + 1) * 128, :], out_sb[:, :],
               reads=[("out", 0), ("out", 1)])



def alloc_attn(B, tag):
    nc, ctx = B.nc, B.ctx
    A = Bundle()
    A.tag = tag
    A.qT = ctx.enter_context(nc.sbuf_tensor(f"{tag}_qT", [128, T], BF16))
    A.kT = ctx.enter_context(nc.sbuf_tensor(f"{tag}_kT", [128, T], BF16))
    A.vton = ctx.enter_context(nc.sbuf_tensor(f"{tag}_vton", [128, NT * 130], BF16))
    A.pt = [ctx.enter_context(nc.sbuf_tensor(f"{tag}_pt{i}", [128, 512], BF16)) for i in range(2)]
    A.rbc = ctx.enter_context(nc.sbuf_tensor(f"{tag}_rbc", [128, 512], BF16))
    A.rden_f = ctx.enter_context(nc.sbuf_tensor(f"{tag}_rdenf", [128, 512], F32))
    A.rden = ctx.enter_context(nc.sbuf_tensor(f"{tag}_rden", [128, 512], BF16))
    A.ot = ctx.enter_context(nc.sbuf_tensor(f"{tag}_ot", [128, 2 * T], BF16))
    A.nrm = ctx.enter_context(nc.sbuf_tensor(f"{tag}_nrm", [128, 24], F32))
    A.negb = ctx.enter_context(nc.sbuf_tensor(f"{tag}_negb", [128, 2], F32))
    A.indic = ctx.enter_context(nc.sbuf_tensor(f"{tag}_indic", [128, 128], BF16))
    A.masks = ctx.enter_context(nc.sbuf_tensor(f"{tag}_masks", [128, 4 * 512], BF16))
    em = B.em
    em.op("v", lambda v: v.memset(A.indic[:, :], 0.0), writes=[(tag, "indic")])
    em.op("v", lambda v: v.memset(A.indic[0:64, 0:1], 1.0),
          reads=[(tag, "indic")], writes=[(tag, "indic")])
    em.op("v", lambda v: v.memset(A.indic[64:128, 64:65], 1.0),
          reads=[(tag, "indic")], writes=[(tag, "indic")])
    for j in range(4):
        em.op("v", lambda v, j=j: v.tensor_scalar(A.masks[:, j * 512:(j + 1) * 512],
                                                  B.iota_fp[:, :], float(128 * j), None, OP.is_ge),
              reads=[("iota_fp",)], writes=[(tag, "mask", j)])
    em.op("v", lambda v: v.memset(
        A.vton[:, :].rearrange("p (t c) -> p t c", c=130)[:, :, 64:65], 1.0),
        writes=[(tag, "vton_ones")])
    em.op("v", lambda v: v.memset(
        A.vton[:, :].rearrange("p (t c) -> p t c", c=130)[:, :, 129:130], 1.0),
        writes=[(tag, "vton_ones2")])
    return A


def emit_bounds(B, A, scale, extra_col=None):
    em, tag = B.em, A.tag
    for which, src, key in (("q", A.qT, (tag, "qT")), ("k", A.kT, (tag, "kT"))):
        base = 0 if which == "q" else 8
        for tc in range(8):
            em.op("s", lambda s, src=src, tc=tc: s.activation(
                B.xl[:, 0:512], src[:, tc * 512:(tc + 1) * 512], AF.Square),
                reads=[key], writes=[("xl",)])
            em.op("t", lambda t, tc=tc: t.matmul(B.ps[3][:, :], A.indic[:, :],
                                                 B.xl[:, 0:512],
                                                 start=True, stop=True),
                  reads=[("xl",), (tag, "indic")], writes=[("ps", 3)])
            for r in (0, 64):
                em.op("v", lambda v, tc=tc, r=r, base=base: v.reduce_max(
                    A.nrm[r:r + 1, base + tc:base + tc + 1], B.ps[3][r:r + 1, :], axis=AX.X),
                    reads=[("ps", 3)], writes=[(tag, "nrm", which, tc, r)])
    for r in (0, 64):
        em.op("v", lambda v, r=r: v.reduce_max(A.nrm[r:r + 1, 16:17], A.nrm[r:r + 1, 0:8], axis=AX.X),
              reads=[(tag, "nrm", "q", tc, r) for tc in range(8)], writes=[(tag, "nq", r)])
        em.op("v", lambda v, r=r: v.reduce_max(A.nrm[r:r + 1, 17:18], A.nrm[r:r + 1, 8:16], axis=AX.X),
              reads=[(tag, "nrm", "k", tc, r) for tc in range(8)], writes=[(tag, "nk", r)])
        em.op("v", lambda v, r=r: v.tensor_tensor(A.nrm[r:r + 1, 18:19], A.nrm[r:r + 1, 16:17],
                                                  A.nrm[r:r + 1, 17:18], OP.mult),
              reads=[(tag, "nq", r), (tag, "nk", r)], writes=[(tag, "prod", r)])
        em.op("s", lambda s, r=r: s.activation(A.nrm[r:r + 1, 19:20], A.nrm[r:r + 1, 18:19], AF.Sqrt),
              reads=[(tag, "prod", r)], writes=[(tag, "sqb", r)])
        if extra_col is not None:
            et, ekey, ecol = extra_col
            em.op("v", lambda v, r=r: v.tensor_tensor(
                A.nrm[r:r + 1, 19:20], A.nrm[r:r + 1, 19:20],
                et[r:r + 1, ecol:ecol + 1], OP.add),
                reads=[(tag, "sqb", r), ekey], writes=[(tag, "sqb", r)])
        em.op("v", lambda v, r=r: v.tensor_scalar(A.nrm[r:r + 1, 20:21], A.nrm[r:r + 1, 19:20],
                                                  -scale, None, OP.mult),
              reads=[(tag, "sqb", r)], writes=[(tag, "negb1", r)])
        h = 0 if r == 0 else 1
        em.op("t", lambda t, r=r: t.matmul(B.ps[3][:, 0:1], B.ones_f[r:r + 1, 0:128],
                                           A.nrm[r:r + 1, 20:21], start=True, stop=True),
              reads=[(tag, "negb1", r), ("ones_f",)], writes=[("ps", 3)])
        em.op("v", lambda v, h=h: v.tensor_copy(A.negb[:, h:h + 1], B.ps[3][:, 0:1]),
              reads=[("ps", 3)], writes=[(tag, "negb", h)])


def emit_attn_core(B, A, pair, causal=True, use_exp=True, scale=1.0, dbg_cb=None,
                   s_extra=None):
    em, tag = B.em, A.tag
    nktiles = NT
    for h in range(2):
        hb = 64 * h
        for qc in range(NTC):
            rng = list(range(0, 4 * qc + 4)) if causal else list(range(nktiles))
            for i, ki in enumerate(rng):
                if dbg_cb is not None:
                    dbg_cb("pre_s", pair, h, qc, ki)
                sb = 4 + (ki % 2)
                em.op("t", lambda t, ki=ki, sb=sb, hb=hb, qc=qc: t.matmul(
                    B.ps[sb][:, :],
                    A.kT[hb:hb + 64, ki * 128:(ki + 1) * 128],
                    A.qT[hb:hb + 64, qc * 512:(qc + 1) * 512],
                    start=True, stop=(s_extra is None)),
                    reads=[(tag, "kT"), (tag, "qT")], writes=[("ps", sb)])
                if s_extra is not None:
                    s_extra(h, qc, ki, sb)
                pk = ("pt", ki % 2)
                if use_exp:
                    em.op("s", lambda s, ki=ki, sb=sb, h=h: s.activation(
                        A.pt[ki % 2][:, :], B.ps[sb][:, :], AF.Exp,
                        bias=A.negb[:, h:h + 1], scale=scale),
                        reads=[("ps", sb), (tag, "negb", h)], writes=[pk])
                else:
                    em.op("s", lambda s, ki=ki, sb=sb: s.activation(
                        A.pt[ki % 2][:, :], B.ps[sb][:, :], AF.Copy, scale=scale),
                        reads=[("ps", sb)], writes=[pk])
                if causal and ki >= 4 * qc:
                    j = ki - 4 * qc
                    em.op("v", lambda v, ki=ki, j=j: v.tensor_tensor(
                        A.pt[ki % 2][:, :], A.pt[ki % 2][:, :],
                        A.masks[:, j * 512:(j + 1) * 512], OP.mult),
                        reads=[pk, (tag, "mask", j)], writes=[pk])
                if dbg_cb is not None:
                    dbg_cb("post_mask", pair, h, qc, ki)
                em.op("t", lambda t, ki=ki, h=h, i=i, last=len(rng) - 1: t.matmul(
                    B.ps[2][0:65, :],
                    A.vton[:, ki * 130 + 65 * h: ki * 130 + 65 * h + 65],
                    A.pt[ki % 2][:, :],
                    start=(i == 0), stop=(i == last)),
                    reads=[pk, (tag, "vton")], writes=[("ps", 2)])
            em.op("v", lambda v: v.reciprocal(A.rden_f[64:65, :], B.ps[2][64:65, :]),
                  reads=[("ps", 2)], writes=[(tag, "rdenf")])
            em.op("v", lambda v: v.tensor_copy(A.rden[64:65, :], A.rden_f[64:65, :]),
                  reads=[(tag, "rdenf")], writes=[(tag, "rden")])
            em.op("t", lambda t: t.matmul(B.ps[3][:, :], B.ones_t[64:65, 0:128],
                                          A.rden[64:65, :], start=True, stop=True),
                  reads=[(tag, "rden"), ("ones_t",)], writes=[("ps", 3)])
            em.op("v", lambda v: v.tensor_copy(A.rbc[0:64, :], B.ps[3][0:64, :]),
                  reads=[("ps", 3)], writes=[(tag, "rbc")])
            if dbg_cb is not None:
                dbg_cb("post_pv", pair, h, qc, -1)
            em.op("v", lambda v, h=h, qc=qc: v.tensor_tensor(
                A.ot[0:64, h * T + qc * 512: h * T + (qc + 1) * 512],
                B.ps[2][0:64, :], A.rbc[0:64, :], OP.mult),
                reads=[("ps", 2), (tag, "rbc")], writes=[(tag, "ot", h, qc)])
    for h in range(2):
        em.dma("ot", "y", A.ot_hbm[128 * pair + 64 * h:128 * pair + 64 * h + 64, :],
               A.ot[0:64, h * T:(h + 1) * T],
               reads=[(tag, "ot", h, qc) for qc in range(NTC)],
               writes=[(tag, "ot_hbm", pair, h)])


def emit_proj_T(B, xlt, xlt_keys, w_sb, w_key, dst, dst_key, bias_col=None,
                bias_key=None, n_dout=128, nc_in=NC, src_chunk=None):
    em = B.em
    assert n_dout <= 128
    if src_chunk is None:
        src_chunk = lambda c, t0, n: xlt_chunk(xlt, c, t0, n)
    for tc in range(NTC):
        em.op("t", lambda t, tc=tc: [
            t.matmul(B.ps[0][0:n_dout, :],
                     w_sb[:, c * n_dout:(c + 1) * n_dout],
                     src_chunk(c, tc * 512, 512),
                     start=(c == 0), stop=(c == nc_in - 1)) for c in range(nc_in)][-1],
            reads=[w_key] + xlt_keys, writes=[("ps", 0)])
        if bias_col is not None:
            em.op("v", lambda v, tc=tc: v.tensor_scalar(
                dst[0:n_dout, tc * 512:(tc + 1) * 512], B.ps[0][0:n_dout, :],
                bias_col, None, OP.add),
                reads=[("ps", 0), bias_key], writes=[dst_key])
        else:
            em.op("v", lambda v, tc=tc: v.tensor_copy(
                dst[0:n_dout, tc * 512:(tc + 1) * 512], B.ps[0][0:n_dout, :]),
                reads=[("ps", 0)], writes=[dst_key])



def emit_mha(B, x_dram, xout_dram, tag, pair_loads, wprojT, bproj,
             causal=True, use_exp=True, scale=1.0, softplus_qk=False, dbg=False):
    em, nc, ctx = B.em, B.nc, B.ctx
    xlt = ctx.enter_context(nc.sbuf_tensor(f"{tag}_xlt", [128, NC * T], BF16))
    A = alloc_attn(B, tag)
    if dbg:
        A.ot_hbm = nc.dram_tensor(f"{tag}_othbm", [D, T], BF16, kind="ExternalOutput").ap()
    else:
        A.ot_hbm = nc.dram_tensor(f"{tag}_othbm", [D, T], BF16).ap()
    wq = ctx.enter_context(nc.sbuf_tensor(f"{tag}_wq", [128, NC * 128], BF16))
    wk = ctx.enter_context(nc.sbuf_tensor(f"{tag}_wk", [128, NC * 128], BF16))
    wv = ctx.enter_context(nc.sbuf_tensor(f"{tag}_wv", [128, NC * 128], BF16))
    bcols = ctx.enter_context(nc.sbuf_tensor(f"{tag}_bcols", [128, 2], F32))
    vbrow = ctx.enter_context(nc.sbuf_tensor(f"{tag}_vbrow", [1, 128], F32))
    vbbc = ctx.enter_context(nc.sbuf_tensor(f"{tag}_vbbc", [128, 128], F32))
    wpr = ctx.enter_context(nc.sbuf_tensor(f"{tag}_wpr", [128, NC * D], BF16))
    bb = ctx.enter_context(nc.sbuf_tensor(f"{tag}_bb", [128, D], F32))
    otl = ctx.enter_context(nc.sbuf_tensor(f"{tag}_otl", [128, NC * 128], BF16))
    out_sb = ctx.enter_context(nc.sbuf_tensor(f"{tag}_out", [128, D], F32))
    xres = ctx.enter_context(nc.sbuf_tensor(f"{tag}_xres", [128, D], F32))
    if softplus_qk:
        spa = ctx.enter_context(nc.sbuf_tensor(f"{tag}_spa", [128, T], F32))
        spb = ctx.enter_context(nc.sbuf_tensor(f"{tag}_spb", [128, T], F32))

    xlt_keys = emit_ln_xlt(B, x_dram, xlt, tag)

    em.dma("wg", "g", wpr[:, :].rearrange("k (c o) -> k c o", c=NC),
           wprojT.rearrange("(c k) o -> k c o", c=NC), writes=[("wpr",)])
    for c in range(NC):
        em.dma("wg", "g", vbrow[0:1, 0:128],
               bproj[c * 128:(c + 1) * 128].rearrange("(o p) -> o p", o=1),
               writes=[("bprow",)])
        em.op("t", lambda t: t.matmul(B.ps[3][:, 0:128], B.ones_f[0:1, 0:128],
                                      vbrow[0:1, 0:128], start=True, stop=True),
              reads=[("bprow",), ("ones_f",)], writes=[("ps", 3)])
        em.op("v", lambda v, c=c: v.tensor_copy(bb[:, c * 128:(c + 1) * 128], B.ps[3][:, 0:128]),
              reads=[("ps", 3)], writes=[("bb", c)])

    def softplus_inplace(dst, key):
        em.op("s", lambda s: s.activation(spa[:, :], dst[:, :], AF.Abs),
              reads=[key], writes=[(tag, "spa")])
        em.op("s", lambda s: s.activation(spa[:, :], spa[:, :], AF.Exp, scale=-1.0),
              reads=[(tag, "spa")], writes=[(tag, "spa")])
        em.op("s", lambda s: s.activation(spa[:, :], spa[:, :], AF.Ln,
                                          bias=B.consts[:, 2:3], scale=1.0),
              reads=[(tag, "spa"), ("consts", 2)], writes=[(tag, "spa")])
        em.op("v", lambda v: v.tensor_scalar(spb[:, :], dst[:, :], 0.0, None, OP.max),
              reads=[key], writes=[(tag, "spb")])
        em.op("v", lambda v: v.tensor_tensor(dst[:, :], spa[:, :], spb[:, :], OP.add),
              reads=[(tag, "spa"), (tag, "spb")], writes=[key])

    for pair in range(8):
        pair_loads(em, pair, wq, wk, wv, bcols, vbrow)
        em.op("t", lambda t: t.matmul(B.ps[3][:, 0:128], B.ones_f[0:1, 0:128],
                                      vbrow[0:1, 0:128], start=True, stop=True),
              reads=[("bprow",), ("ones_f",)], writes=[("ps", 3)])
        em.op("v", lambda v: v.tensor_copy(vbbc[:, :], B.ps[3][:, 0:128]),
              reads=[("ps", 3)], writes=[("vbbc",)])
        emit_proj_T(B, xlt, xlt_keys, wq, ("wq",), A.qT, (tag, "qT"),
                    bias_col=bcols[:, 0:1], bias_key=("bcol", 0))
        emit_proj_T(B, xlt, xlt_keys, wk, ("wk",), A.kT, (tag, "kT"),
                    bias_col=bcols[:, 1:2], bias_key=("bcol", 1))
        if softplus_qk:
            softplus_inplace(A.qT, (tag, "qT"))
            softplus_inplace(A.kT, (tag, "kT"))
        for t in range(NT):
            em.op("t", lambda tt, t=t: [
                tt.matmul(B.ps[1][:, 0:128],
                          xlt_chunk(xlt, c, t * 128, 128),
                          wv[:, c * 128:(c + 1) * 128],
                          start=(c == 0), stop=(c == NC - 1)) for c in range(NC)][-1],
                reads=[("wv",)] + xlt_keys, writes=[("ps", 1)])
            em.op("v", lambda v, t=t: v.tensor_tensor(
                A.vton[:, t * 130:t * 130 + 130].rearrange("p (h e) -> p h e", h=2)[:, :, 0:64],
                B.ps[1][:, 0:128].rearrange("p (h e) -> p h e", h=2),
                vbbc[:, :].rearrange("p (h e) -> p h e", h=2), OP.add),
                reads=[("ps", 1), ("vbbc",)], writes=[(tag, "vton")])
        if use_exp:
            emit_bounds(B, A, scale)
        emit_attn_core(B, A, pair, causal=causal, use_exp=use_exp, scale=scale,
                       dbg_cb=getattr(B, "dbg_cb", None))

    emit_outproj(B, A, tag, x_dram, xout_dram, wpr, bb, otl, out_sb, xres,
                 with_bias=True)


def emit_outproj(B, A, tag, x_dram, xout_dram, wpr, bb, otl, out_sb, xres,
                 with_bias=True):
    em = B.em
    othbm_keys = [(tag, "ot_hbm", p, h) for p in range(8) for h in range(2)]
    for t in range(NT):
        em.dma("oc", "y", otl[:, :].rearrange("k (c o) -> k c o", c=NC),
               A.ot_hbm[:, t * 128:(t + 1) * 128].rearrange("(c k) o -> k c o", c=NC),
               reads=othbm_keys, writes=[("otl",)])
        for half in range(2):
            em.op("t", lambda tt, t=t, half=half: [
                tt.matmul(B.ps[4 + half][:, :],
                          otl[:, c * 128:(c + 1) * 128],
                          wpr[:, c * D + half * 512: c * D + half * 512 + 512],
                          start=(c == 0), stop=(c == NC - 1)) for c in range(NC)][-1],
                reads=[("otl",), ("wpr",)], writes=[("ps", 4 + half)])
        em.dma("xy", "y", xres[:, :], x_dram[t * 128:(t + 1) * 128, :], writes=[("xres",)])
        for half in range(2):
            if with_bias:
                em.op("v", lambda v, half=half: v.tensor_tensor(
                    out_sb[:, half * 512:half * 512 + 512], B.ps[4 + half][:, :],
                    bb[:, half * 512:half * 512 + 512], OP.add),
                    reads=[("ps", 4 + half)] + [("bb", c) for c in range(NC)],
                    writes=[("osb", half)])
                em.op("v", lambda v, half=half: v.tensor_tensor(
                    out_sb[:, half * 512:half * 512 + 512],
                    out_sb[:, half * 512:half * 512 + 512],
                    xres[:, half * 512:half * 512 + 512], OP.add),
                    reads=[("osb", half), ("xres",)], writes=[("osb", half)])
            else:
                em.op("v", lambda v, half=half: v.tensor_tensor(
                    out_sb[:, half * 512:half * 512 + 512], B.ps[4 + half][:, :],
                    xres[:, half * 512:half * 512 + 512], OP.add),
                    reads=[("ps", 4 + half), ("xres",)], writes=[("osb", half)])
        em.dma("xo", "y", xout_dram[t * 128:(t + 1) * 128, :], out_sb[:, :],
               reads=[("osb", 0), ("osb", 1)])


def emit_l0a(B, x_dram, wqkvT, bqkv, wprojT, bproj, xout_dram, tag="l0a", dbg=False):
    def pair_loads(em, pair, wq, wk, wv, bcols, vbrow):
        for wsb, key, col0 in ((wq, ("wq",), pair * 128), (wk, ("wk",), D + pair * 128),
                               (wv, ("wv",), 2 * D + pair * 128)):
            em.dma("wg", "g", wsb[:, :].rearrange("k (c o) -> k c o", c=NC),
                   wqkvT[:, col0:col0 + 128].rearrange("(c k) o -> k c o", c=NC),
                   writes=[key])
        em.dma("wg", "g", bcols[:, 0:1],
               bqkv[pair * 128:pair * 128 + 128].rearrange("(p o) -> p o", o=1),
               writes=[("bcol", 0)])
        em.dma("wg", "g", bcols[:, 1:2],
               bqkv[D + pair * 128:D + pair * 128 + 128].rearrange("(p o) -> p o", o=1),
               writes=[("bcol", 1)])
        em.dma("wg", "g", vbrow[0:1, 0:128],
               bqkv[2 * D + pair * 128:2 * D + pair * 128 + 128].rearrange("(o p) -> o p", o=1),
               writes=[("bprow",)])

    emit_mha(B, x_dram, xout_dram, tag, pair_loads, wprojT, bproj,
             causal=True, use_exp=True, scale=1.0 / 8.0, dbg=dbg)


def emit_l2a(B, x_dram, wqT, bq, wkT, bk, wvT, bv, woT, bo, xout_dram, tag="l2a"):
    def pair_loads(em, pair, wq, wk, wv, bcols, vbrow):
        for wsb, key, src in ((wq, ("wq",), wqT), (wk, ("wk",), wkT), (wv, ("wv",), wvT)):
            em.dma("wg", "g", wsb[:, :].rearrange("k (c o) -> k c o", c=NC),
                   src[:, pair * 128:(pair + 1) * 128].rearrange("(c k) o -> k c o", c=NC),
                   writes=[key])
        em.dma("wg", "g", bcols[:, 0:1],
               bq[pair * 128:pair * 128 + 128].rearrange("(p o) -> p o", o=1),
               writes=[("bcol", 0)])
        em.dma("wg", "g", bcols[:, 1:2],
               bk[pair * 128:pair * 128 + 128].rearrange("(p o) -> p o", o=1),
               writes=[("bcol", 1)])
        em.dma("wg", "g", vbrow[0:1, 0:128],
               bv[pair * 128:pair * 128 + 128].rearrange("(o p) -> o p", o=1),
               writes=[("bprow",)])

    emit_mha(B, x_dram, xout_dram, tag, pair_loads, woT, bo,
             causal=True, use_exp=False, scale=1.0, softplus_qk=True)


def emit_l3a(B, x_dram, wqkvT, bqkv, woutT, bout, xout_dram, tag="l3a"):
    wr = wqkvT.rearrange("(c k) (g e) -> k c g e", c=NC, e=192)
    br = bqkv.rearrange("(g e) -> g e", e=192)

    def pair_loads(em, pair, wq, wk, wv, bcols, vbrow):
        for wsb, key, off in ((wq, ("wq",), 0), (wk, ("wk",), 64), (wv, ("wv",), 128)):
            for g in range(2):
                em.dma("wg", "g",
                       wsb[:, :].rearrange("k (c o) -> k c o", c=NC)[:, :, 64 * g:64 * g + 64],
                       wr[:, :, 2 * pair + g, off:off + 64],
                       writes=[key])
        for g in range(2):
            h = 2 * pair + g
            em.dma("wg", "g", bcols[64 * g:64 * g + 64, 0:1],
                   bqkv[192 * h:192 * h + 64].rearrange("(p o) -> p o", o=1),
                   writes=[("bcol", 0)])
            em.dma("wg", "g", bcols[64 * g:64 * g + 64, 1:2],
                   bqkv[192 * h + 64:192 * h + 128].rearrange("(p o) -> p o", o=1),
                   writes=[("bcol", 1)])
            em.dma("wg", "g", vbrow[0:1, 64 * g:64 * g + 64],
                   bqkv[192 * h + 128:192 * h + 192].rearrange("(o p) -> o p", o=1),
                   writes=[("bprow",)])

    emit_mha(B, x_dram, xout_dram, tag, pair_loads, woutT, bout,
             causal=False, use_exp=True, scale=1.0 / 8.0)



def emit_l1a(B, x_dram, W_kv, g_kv, W_kdec, W_vdec, W_q, W_qdec, W_krope, W_qrope,
             W_o, cos4, sin4, rot_permT, xout_dram, tag="l1a"):
    em, nc, ctx = B.em, B.nc, B.ctx
    RSC = 1.0 / math.sqrt(96.0)
    qcT = ctx.enter_context(nc.sbuf_tensor(f"{tag}_qcT", [128, 4 * T], BF16))
    kvT = ctx.enter_context(nc.sbuf_tensor(f"{tag}_kvT", [128, 2 * T], BF16))
    krT = ctx.enter_context(nc.sbuf_tensor(f"{tag}_krT", [128, 4 * T], BF16))
    qrT = ctx.enter_context(nc.sbuf_tensor(f"{tag}_qrT", [128, 4 * T], BF16))
    ropeb = ctx.enter_context(nc.sbuf_tensor(f"{tag}_ropeb", [128, 12], F32))

    with ExitStack() as ctx2:
        xlt = ctx2.enter_context(nc.sbuf_tensor(f"{tag}_xlt", [128, NC * T], BF16))
        cosb = ctx2.enter_context(nc.sbuf_tensor(f"{tag}_cos", [128, 512], BF16))
        sinb = ctx2.enter_context(nc.sbuf_tensor(f"{tag}_sin", [128, 512], BF16))
        rotp = ctx2.enter_context(nc.sbuf_tensor(f"{tag}_rotp", [128, 128], BF16))
        wkv = ctx2.enter_context(nc.sbuf_tensor(f"{tag}_wkv", [128, NC * 256], BF16))
        wbig = ctx2.enter_context(nc.sbuf_tensor(f"{tag}_wbig", [128, NC * 128], BF16))
        gkvr = ctx2.enter_context(nc.sbuf_tensor(f"{tag}_gkvr", [1, 256], F32))
        gkvb = ctx2.enter_context(nc.sbuf_tensor(f"{tag}_gkvb", [128, 256], F32))
        kvtok = ctx2.enter_context(nc.sbuf_tensor(f"{tag}_kvtok", [128, 256], BF16))
        rcol = ctx2.enter_context(nc.sbuf_tensor(f"{tag}_rcol", [128, 4], F32))
        kvsq = B.xl[:, 0:256]
        rsc1 = B.xtile[:, 0:512]
        rsc2 = B.xtile[:, 512:1024]

        xlt_keys = emit_ln_xlt(B, x_dram, xlt, tag)
        em.dma("wg", "g", rotp[:, :], rot_permT, writes=[("rotp",)])
        em.dma("wg", "g", wkv[:, :].rearrange("k (c o) -> k c o", c=NC),
               W_kv.rearrange("(c k) o -> k c o", c=NC), writes=[("wkv",)])
        em.dma("wg", "g", gkvr[0:1, :], g_kv.rearrange("(o p) -> o p", o=1),
               writes=[("gkvr",)])
        em.op("t", lambda t: t.matmul(B.ps[3][:, 0:256], B.ones_f[0:1, 0:128],
                                      gkvr[0:1, :], start=True, stop=True),
              reads=[("gkvr",), ("ones_f",)], writes=[("ps", 3)])
        em.op("v", lambda v: v.tensor_copy(gkvb[:, :], B.ps[3][:, 0:256]),
              reads=[("ps", 3)], writes=[("gkvb",)])

        for t in range(NT):
            em.op("t", lambda tt, t=t: [
                tt.matmul(B.ps[0][:, 0:256], xlt_chunk(xlt, c, t * 128, 128),
                          wkv[:, c * 256:(c + 1) * 256],
                          start=(c == 0), stop=(c == NC - 1)) for c in range(NC)][-1],
                reads=[("wkv",)] + xlt_keys, writes=[("ps", 0)])
            em.op("s", lambda s: s.activation(kvsq, B.ps[0][:, 0:256], AF.Square,
                                              accum_out=rcol[:, 0:1]),
                  reads=[("ps", 0)], writes=[("xl",), ("rcol", 0)])
            em.op("s", lambda s: s.activation(rcol[:, 1:2], rcol[:, 0:1], AF.Ln,
                                              bias=B.consts[:, 1:2], scale=1.0 / 256),
                  reads=[("rcol", 0), ("consts", 1)], writes=[("rcol", 1)])
            em.op("s", lambda s: s.activation(rcol[:, 2:3], rcol[:, 1:2], AF.Exp, scale=-0.5),
                  reads=[("rcol", 1)], writes=[("rcol", 2)])
            em.op("v", lambda v: v.scalar_tensor_tensor(kvtok[:, :], B.ps[0][:, 0:256],
                                                        rcol[:, 2:3], gkvb[:, :],
                                                        OP.mult, OP.mult),
                  reads=[("ps", 0), ("rcol", 2), ("gkvb",)], writes=[("kvtok",)])
            em.op("t", lambda tt, t=t: [
                tt.transpose(B.ps_bf[:, r * 128:(r + 1) * 128], kvtok[:, r * 128:(r + 1) * 128],
                             B.ident[:, :]) for r in range(2)][-1],
                reads=[("kvtok",), ("ident",)], writes=[("psbf",)])
            em.op("v", lambda v, t=t: v.tensor_copy(
                kvT[:, :].rearrange("k (r tt) -> k r tt", r=2)[:, :, t * 128:(t + 1) * 128],
                B.ps_bf[:, 0:256].rearrange("k (r m) -> k r m", r=2)),
                reads=[("psbf",)], writes=[(tag, "kvT")])

        for i in range(4):
            em.dma("wg", "g", wbig[:, :].rearrange("k (c o) -> k c o", c=NC),
                   W_q[:, i * 128:(i + 1) * 128].rearrange("(c k) o -> k c o", c=NC),
                   writes=[("wbig",)])
            emit_proj_T(B, xlt, xlt_keys, wbig, ("wbig",),
                        qcT[:, i * T:(i + 1) * T], (tag, "qcT"))
        for i in range(4):
            em.dma("wg", "g", wbig[:, :].rearrange("k (c o) -> k c o", c=NC),
                   W_krope[:, i * 128:(i + 1) * 128].rearrange("(c k) o -> k c o", c=NC),
                   writes=[("wbig",)])
            emit_proj_T(B, xlt, xlt_keys, wbig, ("wbig",),
                        krT[:, i * T:(i + 1) * T], (tag, "krT"))
        for i in range(4):
            em.dma("wg", "g", wbig[:, 0:4 * 128].rearrange("k (c o) -> k c o", c=4),
                   W_qrope[:, i * 128:(i + 1) * 128].rearrange("(c k) o -> k c o", c=4),
                   writes=[("wbig",)])
            emit_proj_T(B, xlt, [(tag, "qcT")], wbig, ("wbig",),
                        qrT[:, i * T:(i + 1) * T], (tag, "qrT"), nc_in=4,
                        src_chunk=lambda c, t0, n: qcT[:, c * T + t0:c * T + t0 + n])

        for tc in range(NTC):
            em.dma("wg", "g", cosb[:, :], cos4[:, tc * 512:(tc + 1) * 512], writes=[("cosb",)])
            em.dma("wg", "g", sinb[:, :], sin4[:, tc * 512:(tc + 1) * 512], writes=[("sinb",)])
            for name, ten, key in (("kr", krT, (tag, "krT")), ("qr", qrT, (tag, "qrT"))):
                for i in range(4):
                    sl = slice(i * T + tc * 512, i * T + tc * 512 + 512)
                    em.op("t", lambda t, sl=sl, ten=ten: t.matmul(
                        B.ps[0][:, :], rotp[:, :], ten[:, sl], start=True, stop=True),
                        reads=[key, ("rotp",)], writes=[("ps", 0)])
                    em.op("v", lambda v, sl=sl, ten=ten: v.tensor_tensor(
                        rsc1, ten[:, sl], cosb[:, :], OP.mult),
                        reads=[key, ("cosb",)], writes=[("xtile",)])
                    em.op("v", lambda v: v.tensor_tensor(
                        rsc2, B.ps[0][:, :], sinb[:, :], OP.mult),
                        reads=[("ps", 0), ("sinb",)], writes=[("xtile2",)])
                    em.op("v", lambda v, sl=sl, ten=ten: v.tensor_tensor(
                        ten[:, sl], rsc1, rsc2, OP.add),
                        reads=[("xtile",), ("xtile2",)], writes=[key])

        em.op("v", lambda v: v.memset(ropeb[:, :], 0.0), writes=[(tag, "ropeb")])
        sqs = B.xl[:, 0:512]
        for which, ten, key, col in (("q", qrT, (tag, "qrT"), 0), ("k", krT, (tag, "krT"), 4)):
            for i in range(4):
                for tc in range(NTC):
                    em.op("s", lambda s, ten=ten, i=i, tc=tc: s.activation(
                        sqs, ten[:, i * T + tc * 512:i * T + tc * 512 + 512], AF.Square),
                        reads=[key], writes=[("xl",)])
                    em.op("t", lambda t: t.matmul(B.ps[3][0:1, :], B.ones_t[:, 0:1],
                                                  sqs, start=True, stop=True),
                          reads=[("xl",), ("ones_t",)], writes=[("ps", 3)])
                    em.op("v", lambda v, col=col, i=i, tc=tc: v.reduce_max(
                        rsc2[0:1, 8 * i + tc + (256 if col else 0):8 * i + tc + 1 + (256 if col else 0)],
                        B.ps[3][0:1, :], axis=AX.X),
                        reads=[("ps", 3)], writes=[(tag, "rnm", col, i, tc)])
        for i in range(4):
            em.op("v", lambda v, i=i: v.reduce_max(ropeb[0:1, i:i + 1],
                                                   rsc2[0:1, 8 * i:8 * i + 8], axis=AX.X),
                  reads=[(tag, "rnm", 0, i, tc) for tc in range(NTC)],
                  writes=[(tag, "ropebq", i)])
            em.op("v", lambda v, i=i: v.reduce_max(ropeb[0:1, 4 + i:5 + i],
                                                   rsc2[0:1, 256 + 8 * i:256 + 8 * i + 8], axis=AX.X),
                  reads=[(tag, "rnm", 4, i, tc) for tc in range(NTC)],
                  writes=[(tag, "ropebk", i)])
            em.op("v", lambda v, i=i: v.tensor_tensor(ropeb[0:1, 8 + i:9 + i],
                                                      ropeb[0:1, i:i + 1],
                                                      ropeb[0:1, 4 + i:5 + i], OP.mult),
                  reads=[(tag, "ropebq", i), (tag, "ropebk", i)], writes=[(tag, "ropebp", i)])
            em.op("s", lambda s, i=i: s.activation(ropeb[0:1, 8 + i:9 + i],
                                                   ropeb[0:1, 8 + i:9 + i], AF.Sqrt),
                  reads=[(tag, "ropebp", i)], writes=[(tag, "ropebp", i)])
        em.op("t", lambda t: t.matmul(B.ps[3][:, 0:4], B.ones_f[0:1, 0:128],
                                      ropeb[0:1, 8:12], start=True, stop=True),
              reads=[(tag, "ropebp", i) for i in range(4)] + [("ones_f",)],
              writes=[("ps", 3)])
        em.op("v", lambda v: v.tensor_copy(ropeb[:, 8:12], B.ps[3][:, 0:4]),
              reads=[("ps", 3)], writes=[(tag, "ropebb")])

    em.barrier()

    A = alloc_attn(B, tag)
    A.ot_hbm = nc.dram_tensor(f"{tag}_othbm", [D, T], BF16).ap()
    wdec = ctx.enter_context(nc.sbuf_tensor(f"{tag}_wdec", [128, 4 * 128], BF16))
    wkdc = ctx.enter_context(nc.sbuf_tensor(f"{tag}_wkdc", [128, 2 * 128], BF16))
    wvdc = ctx.enter_context(nc.sbuf_tensor(f"{tag}_wvdc", [128, 2 * 128], BF16))
    wpr = ctx.enter_context(nc.sbuf_tensor(f"{tag}_wpr", [128, NC * D], BF16))
    otl = ctx.enter_context(nc.sbuf_tensor(f"{tag}_otl", [128, NC * 128], BF16))
    out_sb = ctx.enter_context(nc.sbuf_tensor(f"{tag}_out", [128, D], F32))
    xres = ctx.enter_context(nc.sbuf_tensor(f"{tag}_xres", [128, D], F32))
    em.dma("wg", "g", wpr[:, :].rearrange("k (c o) -> k c o", c=NC),
           W_o.rearrange("(c k) o -> k c o", c=NC), writes=[("wpr",)])

    def rope_rows(h):
        i, j = h // 4, h % 4
        return i, 32 * j

    for pair in range(8):
        em.dma("wg", "g", wdec[:, :].rearrange("k (c o) -> k c o", c=4),
               W_qdec[:, pair * 128:(pair + 1) * 128].rearrange("(c k) o -> k c o", c=4),
               writes=[("wdec",)])
        em.dma("wg", "g", wkdc[:, :].rearrange("k (c o) -> k c o", c=2),
               W_kdec[:, pair * 128:(pair + 1) * 128].rearrange("(c k) o -> k c o", c=2),
               writes=[("wkdc",)])
        em.dma("wg", "g", wvdc[:, :].rearrange("k (c o) -> k c o", c=2),
               W_vdec[:, pair * 128:(pair + 1) * 128].rearrange("(c k) o -> k c o", c=2),
               writes=[("wvdc",)])
        emit_proj_T(B, None, [(tag, "qcT")], wdec, ("wdec",), A.qT, (tag, "qT"),
                    nc_in=4, src_chunk=lambda c, t0, n: qcT[:, c * T + t0:c * T + t0 + n])
        emit_proj_T(B, None, [(tag, "kvT")], wkdc, ("wkdc",), A.kT, (tag, "kT"),
                    nc_in=2, src_chunk=lambda c, t0, n: kvT[:, c * T + t0:c * T + t0 + n])
        for t in range(NT):
            em.op("t", lambda tt, t=t: [
                tt.matmul(B.ps[1][:, 0:128], kvT[:, c * T + t * 128:c * T + t * 128 + 128],
                          wvdc[:, c * 128:(c + 1) * 128],
                          start=(c == 0), stop=(c == 1)) for c in range(2)][-1],
                reads=[("wvdc",), (tag, "kvT")], writes=[("ps", 1)])
            em.op("v", lambda v, t=t: v.tensor_copy(
                A.vton[:, t * 130:t * 130 + 130].rearrange("p (h e) -> p h e", h=2)[:, :, 0:64],
                B.ps[1][:, 0:128].rearrange("p (h e) -> p h e", h=2)),
                reads=[("ps", 1)], writes=[(tag, "vton")])
        emit_bounds(B, A, RSC, extra_col=(ropeb, (tag, "ropebb"), 8 + pair // 2))
        def s_extra(h, qc, ki, sbank, pair=pair):
            hh = pair * 2 + h
            i, r0 = rope_rows(hh)
            tp = {"tile_position": (r0, 0)} if r0 == 96 else {}
            em.op("t", lambda t, h=h, qc=qc, ki=ki, sbank=sbank, i=i, r0=r0, tp=tp: t.matmul(
                B.ps[sbank][:, :],
                krT[r0:r0 + 32, i * T + ki * 128:i * T + (ki + 1) * 128],
                qrT[r0:r0 + 32, i * T + qc * 512:i * T + (qc + 1) * 512],
                start=False, stop=True, **tp),
                reads=[(tag, "krT"), (tag, "qrT")], writes=[("ps", sbank)])
        emit_attn_core(B, A, pair, causal=True, use_exp=True, scale=RSC,
                       s_extra=s_extra)

    emit_outproj(B, A, tag, x_dram, xout_dram, wpr, None, otl, out_sb, xres,
                 with_bias=False)



L0F_VAR = set()


def emit_l0f(B, x_dram, w_gate, w1, w3, w2, xout_dram, tag="l0f"):
    VAR = L0F_VAR
    em, nc, ctx = B.em, B.nc, B.ctx
    E, H, NH = 8, 2048, 16
    xlt = ctx.enter_context(nc.sbuf_tensor(f"{tag}_xlt", [128, NC * T], BF16))
    y_hbm = nc.dram_tensor(f"{tag}_yhbm", [T, D], F32).ap()
    wg_sb = ctx.enter_context(nc.sbuf_tensor(f"{tag}_wg", [128, NC * E], BF16))
    w1sb = ctx.enter_context(nc.sbuf_tensor(f"{tag}_w1", [128, NC * H], BF16))
    w3sb = ctx.enter_context(nc.sbuf_tensor(f"{tag}_w3", [128, NC * H], BF16))
    w2sb = ctx.enter_context(nc.sbuf_tensor(f"{tag}_w2", [128, NH * D], BF16))
    hsb = ctx.enter_context(nc.sbuf_tensor(f"{tag}_h", [128, NH * 512], BF16))
    hsil = ctx.enter_context(nc.sbuf_tensor(f"{tag}_hsil", [128, 512], F32))
    comb = ctx.enter_context(nc.sbuf_tensor(f"{tag}_comb", [128, NT * E], F32))
    yscr = ctx.enter_context(nc.sbuf_tensor(f"{tag}_yscr", [128, D], F32))
    r_lall = ctx.enter_context(nc.sbuf_tensor(f"{tag}_rlall", [128, NT * E], F32))
    r_t1 = ctx.enter_context(nc.sbuf_tensor(f"{tag}_rt1", [128, NT * E], F32))
    r_m = ctx.enter_context(nc.sbuf_tensor(f"{tag}_rm", [128, 2 * NT], F32))
    xres = ctx.enter_context(nc.sbuf_tensor(f"{tag}_xres", [128, D], F32))
    out_sb = ctx.enter_context(nc.sbuf_tensor(f"{tag}_out", [128, D], F32))

    xlt_keys = emit_ln_xlt(B, x_dram, xlt, tag)

    em.dma("wg", "g", wg_sb[:, :].rearrange("k (c e) -> k c e", c=NC),
           w_gate.rearrange("(c k) e -> k c e", c=NC), writes=[("wgsb",)])
    for t in range(NT):
        em.op("t", lambda tt, t=t: [
            tt.matmul(B.ps[0][:, 0:E], xlt_chunk(xlt, c, t * 128, 128),
                      wg_sb[:, c * E:(c + 1) * E],
                      start=(c == 0), stop=(c == NC - 1)) for c in range(NC)][-1],
            reads=[("wgsb",)] + xlt_keys, writes=[("ps", 0)])
        em.op("v", lambda v, t=t: v.tensor_copy(r_lall[:, t * E:(t + 1) * E], B.ps[0][:, 0:E]),
              reads=[("ps", 0)], writes=[("r_lall",)])

    def v3(ten):
        return ten[:, :].rearrange("p (t e) -> p t e", e=E)

    def bc(ten):
        return bass.AP(ten, 0, [[2 * NT, 128], [1, NT], [0, E]])

    em.op("v", lambda v: v.tensor_reduce(r_m[:, 0:NT], v3(r_lall), axis=AX.X, op=OP.max),
          reads=[("r_lall",)], writes=[("r_m",)])
    em.op("v", lambda v: v.tensor_tensor(v3(r_t1), v3(r_lall), bc(r_m), OP.subtract),
          reads=[("r_lall",), ("r_m",)], writes=[("r_t1",)])
    em.op("s", lambda s: s.activation(r_lall[:, :], r_t1[:, :], AF.Exp),
          reads=[("r_t1",), ("r_lall",)], writes=[("r_lall",)])
    em.op("v", lambda v: v.tensor_scalar(comb[:, :], r_t1[:, :], 0.0, 1e30,
                                         OP.is_ge, OP.mult),
          reads=[("r_t1",)], writes=[("r_comb_t",)])
    em.op("v", lambda v: v.tensor_tensor(comb[:, :], r_t1[:, :], comb[:, :], OP.subtract),
          reads=[("r_t1",), ("r_comb_t",)], writes=[("r_comb_t",)])
    em.op("v", lambda v: v.tensor_reduce(r_m[:, NT:2 * NT], v3(comb), axis=AX.X, op=OP.max),
          reads=[("r_comb_t",)], writes=[("r_m2",)])
    em.op("v", lambda v: v.tensor_tensor(v3(comb), v3(r_t1),
                                         bass.AP(r_m, NT, [[2 * NT, 128], [1, NT], [0, E]]),
                                         OP.is_ge),
          reads=[("r_t1",), ("r_m2",)], writes=[("r_comb_t",)])
    em.op("v", lambda v: v.tensor_tensor(r_t1[:, :], r_lall[:, :], comb[:, :], OP.mult),
          reads=[("r_lall",), ("r_comb_t",)], writes=[("r_t1",)])
    em.op("v", lambda v: v.tensor_reduce(r_m[:, 0:NT], v3(r_t1), axis=AX.X, op=OP.add),
          reads=[("r_t1",)], writes=[("r_m",)])
    em.op("v", lambda v: v.reciprocal(r_m[:, NT:2 * NT], r_m[:, 0:NT]),
          reads=[("r_m",)], writes=[("r_m2",)])
    em.op("v", lambda v: v.tensor_tensor(v3(comb), v3(r_t1),
                                         bass.AP(r_m, NT, [[2 * NT, 128], [1, NT], [0, E]]),
                                         OP.mult),
          reads=[("r_t1",), ("r_m2",)],
          writes=[(tag, "comb", t) for t in range(NT)])

    comb_keys = [(tag, "comb", t) for t in range(NT)]

    ya_keys = []
    n_experts = 1 if "oneexp" in VAR else E
    for e in range(n_experts):
        em.dma("wg", "g", w1sb[:, :].rearrange("k (c h) -> k c h", c=NC),
               w1[e].rearrange("(c k) h -> k c h", c=NC), writes=[("w1sb",)])
        em.dma("wg", "g", w3sb[:, :].rearrange("k (c h) -> k c h", c=NC),
               w3[e].rearrange("(c k) h -> k c h", c=NC), writes=[("w3sb",)])
        em.dma("wg", "g", w2sb[:, :].rearrange("k (hc d) -> k hc d", hc=NH),
               w2[e].rearrange("(hc k) d -> k hc d", hc=NH), writes=[("w2sb",)])
        for tc in range(NTC):
            for hc in range(NH):
                em.op("t", lambda tt, tc=tc, hc=hc: [
                    tt.matmul(B.ps[4][:, :], w1sb[:, c * H + hc * 128: c * H + hc * 128 + 128],
                              xlt_chunk(xlt, c, tc * 512, 512),
                              start=(c == 0), stop=(c == NC - 1)) for c in range(NC)][-1],
                    reads=[("w1sb",)] + xlt_keys, writes=[("ps", 4)])
                em.op("t", lambda tt, tc=tc, hc=hc: [
                    tt.matmul(B.ps[5][:, :], w3sb[:, c * H + hc * 128: c * H + hc * 128 + 128],
                              xlt_chunk(xlt, c, tc * 512, 512),
                              start=(c == 0), stop=(c == NC - 1)) for c in range(NC)][-1],
                    reads=[("w3sb",)] + xlt_keys, writes=[("ps", 5)])
                em.op("s", lambda s: s.activation(hsil[:, :], B.ps[4][:, :],
                                                  AF.Copy if "nosilu" in VAR else AF.Silu),
                      reads=[("ps", 4)], writes=[("hsil",)])
                em.op("v", lambda v, hc=hc: v.tensor_tensor(
                    hsb[:, hc * 512:(hc + 1) * 512], hsil[:, :], B.ps[5][:, :], OP.mult),
                    reads=[("hsil",), ("ps", 5)], writes=[("hsb", hc)])
            for ti in range(4):
                t = tc * 4 + ti
                for half in range(2):
                    em.op("t", lambda tt, t=t, ti=ti, half=half: [
                        tt.matmul(B.ps[2 + half][:, :],
                                  hsb[:, hc * 512 + ti * 128: hc * 512 + ti * 128 + 128],
                                  w2sb[:, hc * D + half * 512: hc * D + half * 512 + 512],
                                  start=(hc == 0), stop=(hc == NH - 1)) for hc in range(NH)][-1],
                        reads=[("hsb", hc) for hc in range(NH)] + [("w2sb",)],
                        writes=[("ps", 2 + half)])
                    em.op("v", lambda v, t=t, e=e, half=half: v.tensor_scalar(
                        yscr[:, half * 512:half * 512 + 512], B.ps[2 + half][:, :],
                        comb[:, t * E + e:t * E + e + 1], None, OP.mult),
                        reads=[("ps", 2 + half), (tag, "comb", t)], writes=[("ysc", half)])
                kw = {} if (e == 0 or "noacc" in VAR) else {"accum_op": OP.add}
                s = em.dma("ya", "g", y_hbm[t * 128:(t + 1) * 128, :], yscr[:, :],
                           reads=[("ysc", 0), ("ysc", 1)],
                           writes=[(tag, "yhbm", t) if e == n_experts - 1 else (tag, "yhbm_p", t, e)],
                           **kw)
                if e == n_experts - 1:
                    ya_keys.append((tag, "yhbm", t))

    for t in range(NT):
        em.dma("xy", "y", xres[:, :], x_dram[t * 128:(t + 1) * 128, :], writes=[("xres",)])
        em.dma("yl", "y", yscr[:, :], y_hbm[t * 128:(t + 1) * 128, :],
               reads=[(tag, "yhbm", t), ("ysc", 0), ("ysc", 1)], writes=[("yscr_f",)])
        em.op("v", lambda v: v.tensor_tensor(out_sb[:, :], yscr[:, :], xres[:, :], OP.add),
              reads=[("yscr_f",), ("xres",)], writes=[("osb", 0)])
        em.dma("xo", "y", xout_dram[t * 128:(t + 1) * 128, :], out_sb[:, :],
               reads=[("osb", 0)])



def row_to_cols(B, em, dst, dst_key, row_sb, row_key, n):
    em.op("t", lambda t: [
        t.matmul(B.ps[1][:, j:j + 1], row_sb[0:1, j * 128:(j + 1) * 128],
                 B.ones_f[0:1, 0:1], start=(j == 0), stop=(j == n - 1)) for j in range(n)][-1],
        reads=[row_key, ("ones_f",)], writes=[("ps", 1)])
    em.op("v", lambda v: v.tensor_copy(dst[:, 0:n], B.ps[1][:, 0:n]),
          reads=[("ps", 1)], writes=[dst_key])


def emit_l3f(B, x_dram, gate_wT, fc1_wT, fc1_b, geglu, fc2_wT, b16,
             s_fc1_wT, s_fc1_b, s_geglu, s_fc2_wT, xout_dram, tag="l3f"):
    em, nc, ctx = B.em, B.nc, B.ctx
    E = 15
    xlt = ctx.enter_context(nc.sbuf_tensor(f"{tag}_xlt", [128, NC * T], BF16))
    y_hbm = nc.dram_tensor(f"{tag}_yhbm", [T, D], F32).ap()
    wg_sb = ctx.enter_context(nc.sbuf_tensor(f"{tag}_wg", [128, NC * E], BF16))
    comb = ctx.enter_context(nc.sbuf_tensor(f"{tag}_comb", [128, NT * 16], F32))
    r_lall = ctx.enter_context(nc.sbuf_tensor(f"{tag}_rlall", [128, NT * 15], F32))
    r_t1 = ctx.enter_context(nc.sbuf_tensor(f"{tag}_rt1", [128, NT * 15], F32))
    r_t2 = ctx.enter_context(nc.sbuf_tensor(f"{tag}_rt2", [128, NT * 15], F32))
    r_m = ctx.enter_context(nc.sbuf_tensor(f"{tag}_rm", [128, 2 * NT], F32))
    yscr = ctx.enter_context(nc.sbuf_tensor(f"{tag}_yscr", [128, D], F32))
    gel = ctx.enter_context(nc.sbuf_tensor(f"{tag}_gel", [128, 512], F32))
    fbrow = ctx.enter_context(nc.sbuf_tensor(f"{tag}_fbrow", [1, 4096], F32))
    fb = ctx.enter_context(nc.sbuf_tensor(f"{tag}_fb", [128, 32], F32))
    gg = ctx.enter_context(nc.sbuf_tensor(f"{tag}_gg", [128, 16], F32))
    b16sb = ctx.enter_context(nc.sbuf_tensor(f"{tag}_b16", [16, D], BF16))
    combt = ctx.enter_context(nc.sbuf_tensor(f"{tag}_combt", [16, 128], BF16))
    ytile = ctx.enter_context(nc.sbuf_tensor(f"{tag}_yt", [128, D], F32))
    xres = ctx.enter_context(nc.sbuf_tensor(f"{tag}_xres", [128, D], F32))
    out_sb = ctx.enter_context(nc.sbuf_tensor(f"{tag}_out", [128, D], F32))

    xlt_keys = emit_ln_xlt(B, x_dram, xlt, tag)

    em.dma("wg", "g", wg_sb[:, :].rearrange("k (c e) -> k c e", c=NC),
           gate_wT.rearrange("(c k) e -> k c e", c=NC), writes=[("wgsb",)])
    em.dma("wg", "g", b16sb[0:16, :], b16, writes=[("b16sb",)])
    em.op("v", lambda v: v.memset(
        comb[:, :].rearrange("k (t e) -> k t e", e=16)[:, :, 15:16], 1.0),
        writes=[(tag, "comb1",)])
    for t in range(NT):
        em.op("t", lambda tt, t=t: [
            tt.matmul(B.ps[0][:, 0:E], xlt_chunk(xlt, c, t * 128, 128),
                      wg_sb[:, c * E:(c + 1) * E],
                      start=(c == 0), stop=(c == NC - 1)) for c in range(NC)][-1],
            reads=[("wgsb",)] + xlt_keys, writes=[("ps", 0)])
        em.op("v", lambda v, t=t: v.tensor_copy(r_lall[:, t * E:(t + 1) * E], B.ps[0][:, 0:E]),
              reads=[("ps", 0)], writes=[("r_lall",)])

    def v3(ten):
        return ten[:, :].rearrange("p (t e) -> p t e", e=E)

    def bc(off):
        return bass.AP(r_m, off, [[2 * NT, 128], [1, NT], [0, E]])

    em.op("v", lambda v: v.tensor_reduce(r_m[:, 0:NT], v3(r_lall), axis=AX.X, op=OP.max),
          reads=[("r_lall",)], writes=[("r_m",)])
    em.op("v", lambda v: v.tensor_tensor(v3(r_t1), v3(r_lall), bc(0), OP.subtract),
          reads=[("r_lall",), ("r_m",)], writes=[("r_t1",)])
    em.op("s", lambda s: s.activation(r_lall[:, :], r_t1[:, :], AF.Exp),
          reads=[("r_t1",), ("r_lall",)], writes=[("r_lall",)])
    em.op("v", lambda v: v.tensor_reduce(r_m[:, 0:NT], v3(r_lall), axis=AX.X, op=OP.add),
          reads=[("r_lall",)], writes=[("r_m",)])
    em.op("v", lambda v: v.reciprocal(r_m[:, NT:2 * NT], r_m[:, 0:NT]),
          reads=[("r_m",)], writes=[("r_m2",)])
    em.op("v", lambda v: v.tensor_tensor(v3(r_t1), v3(r_lall), bc(NT), OP.mult),
          reads=[("r_lall",), ("r_m2",)], writes=[("r_t1",)])
    em.op("v", lambda v: v.tensor_reduce(r_m[:, 0:NT], v3(r_t1), axis=AX.X, op=OP.max),
          reads=[("r_t1",)], writes=[("r_m",)])
    em.op("v", lambda v: v.tensor_tensor(v3(r_lall), v3(r_t1), bc(0), OP.is_ge),
          reads=[("r_t1",), ("r_m",)], writes=[("r_lall",)])
    em.op("v", lambda v: v.tensor_tensor(r_lall[:, :], r_t1[:, :], r_lall[:, :], OP.subtract),
          reads=[("r_t1",), ("r_lall",)], writes=[("r_lall",)])
    em.op("v", lambda v: v.tensor_reduce(r_m[:, 0:NT], v3(r_lall), axis=AX.X, op=OP.max),
          reads=[("r_lall",)], writes=[("r_m",)])
    em.op("v", lambda v: v.tensor_tensor(v3(r_t2), v3(r_lall), bc(0), OP.is_ge),
          reads=[("r_lall",), ("r_m",)], writes=[("r_t2",)])
    em.op("v", lambda v: v.tensor_tensor(r_lall[:, :], r_lall[:, :], r_t2[:, :], OP.subtract),
          reads=[("r_lall",), ("r_t2",)], writes=[("r_lall",)])
    em.op("v", lambda v: v.tensor_reduce(r_m[:, 0:NT], v3(r_lall), axis=AX.X, op=OP.max),
          reads=[("r_lall",)], writes=[("r_m",)])
    em.op("v", lambda v: v.tensor_tensor(v3(r_lall), v3(r_t1), bc(0), OP.is_ge),
          reads=[("r_t1",), ("r_m",)], writes=[("r_lall",)])
    em.op("v", lambda v: v.tensor_tensor(r_t1[:, :], r_t1[:, :], r_lall[:, :], OP.mult),
          reads=[("r_t1",), ("r_lall",)], writes=[("r_t1",)])
    em.op("v", lambda v: v.tensor_reduce(r_m[:, 0:NT], v3(r_t1), axis=AX.X, op=OP.add),
          reads=[("r_t1",)], writes=[("r_m",)])
    em.op("v", lambda v: v.tensor_scalar(r_m[:, 0:NT], r_m[:, 0:NT], 1e-20, None, OP.add),
          reads=[("r_m",)], writes=[("r_m",)])
    em.op("v", lambda v: v.reciprocal(r_m[:, NT:2 * NT], r_m[:, 0:NT]),
          reads=[("r_m",)], writes=[("r_m2",)])
    em.op("v", lambda v: v.tensor_tensor(
        comb[:, :].rearrange("k (t g) -> k t g", g=16)[:, :, 0:E],
        v3(r_t1), bc(NT), OP.mult),
        reads=[("r_t1",), ("r_m2",)],
        writes=[(tag, "comb", t) for t in range(NT)])

    comb_keys = [(tag, "comb", t) for t in range(NT)] + [(tag, "comb1",)]

    with ExitStack() as ctx2:
        w1sb = ctx2.enter_context(nc.sbuf_tensor(f"{tag}_w1", [128, NC * 2048], BF16))
        w2sb = ctx2.enter_context(nc.sbuf_tensor(f"{tag}_w2", [128, NC * D], BF16))
        xh_sb = ctx2.enter_context(nc.sbuf_tensor(f"{tag}_xh", [128, 8 * 512], BF16))
        a_sb = ctx2.enter_context(nc.sbuf_tensor(f"{tag}_a", [128, 8 * 512], BF16))
        for e in range(E):
            em.dma("wg", "g", w1sb[:, :].rearrange("k (c h) -> k c h", c=NC),
                   fc1_wT[e].rearrange("(c k) h -> k c h", c=NC), writes=[("w1sb",)])
            em.dma("wg", "g", w2sb[:, :].rearrange("k (fc d) -> k fc d", fc=NC),
                   fc2_wT[e].rearrange("(fc k) d -> k fc d", fc=NC), writes=[("w2sb",)])
            em.dma("wg", "g", fbrow[0:1, 0:2048],
                   fc1_b[e].rearrange("(o p) -> o p", o=1), writes=[("fbrow",)])
            row_to_cols(B, em, fb, ("fb",), fbrow, ("fbrow",), 16)
            em.dma("wg", "g", fbrow[0:1, 0:1024],
                   geglu[e].rearrange("(o p) -> o p", o=1), writes=[("fbrow",)])
            row_to_cols(B, em, gg, ("gg",), fbrow, ("fbrow",), 8)
            for tc in range(NTC):
                for j in range(8):
                    em.op("t", lambda tt, tc=tc, j=j: [
                        tt.matmul(B.ps[4][:, :],
                                  w1sb[:, c * 2048 + j * 128: c * 2048 + j * 128 + 128],
                                  xlt_chunk(xlt, c, tc * 512, 512),
                                  start=(c == 0), stop=(c == NC - 1)) for c in range(NC)][-1],
                        reads=[("w1sb",)] + xlt_keys, writes=[("ps", 4)])
                    em.op("t", lambda tt, tc=tc, j=j: [
                        tt.matmul(B.ps[5][:, :],
                                  w1sb[:, c * 2048 + 1024 + j * 128: c * 2048 + 1024 + j * 128 + 128],
                                  xlt_chunk(xlt, c, tc * 512, 512),
                                  start=(c == 0), stop=(c == NC - 1)) for c in range(NC)][-1],
                        reads=[("w1sb",)] + xlt_keys, writes=[("ps", 5)])
                    em.op("v", lambda v, j=j: v.tensor_scalar(
                        xh_sb[:, j * 512:(j + 1) * 512], B.ps[4][:, :],
                        fb[:, j:j + 1], None, OP.add),
                        reads=[("ps", 4), ("fb",)], writes=[("xh", j)])
                    em.op("s", lambda s, j=j: s.activation(gel[:, :], B.ps[5][:, :], AF.Gelu,
                                                           bias=fb[:, 8 + j:9 + j], scale=1.0),
                          reads=[("ps", 5), ("fb",)], writes=[("gel",)])
                    em.op("v", lambda v, j=j: v.scalar_tensor_tensor(
                        a_sb[:, j * 512:(j + 1) * 512], gel[:, :], gg[:, j:j + 1],
                        xh_sb[:, j * 512:(j + 1) * 512], OP.mult, OP.mult),
                        reads=[("gel",), ("gg",), ("xh", j)], writes=[("a", j)])
                for ti in range(4):
                    t = tc * 4 + ti
                    for half in range(2):
                        em.op("t", lambda tt, ti=ti, half=half: [
                            tt.matmul(B.ps[2 + half][:, :],
                                      a_sb[:, j * 512 + ti * 128: j * 512 + ti * 128 + 128],
                                      w2sb[:, j * D + half * 512: j * D + half * 512 + 512],
                                      start=(j == 0), stop=(j == 7)) for j in range(8)][-1],
                            reads=[("a", j) for j in range(8)] + [("w2sb",)],
                            writes=[("ps", 2 + half)])
                        em.op("v", lambda v, t=t, e=e, half=half: v.tensor_scalar(
                            yscr[:, half * 512:half * 512 + 512], B.ps[2 + half][:, :],
                            comb[:, t * 16 + e:t * 16 + e + 1], None, OP.mult),
                            reads=[("ps", 2 + half), (tag, "comb", t)], writes=[("ysc", half)])
                    kw = {} if e == 0 else {"accum_op": OP.add}
                    em.dma("ya", "g", y_hbm[t * 128:(t + 1) * 128, :], yscr[:, :],
                           reads=[("ysc", 0), ("ysc", 1)],
                           writes=[(tag, "yhbm_p", t, e)], **kw)
    em.barrier()

    with ExitStack() as ctx2:
        s1sb = ctx2.enter_context(nc.sbuf_tensor(f"{tag}_s1", [128, NC * 2048], BF16))
        s2sb = ctx2.enter_context(nc.sbuf_tensor(f"{tag}_s2", [128, 8 * D], BF16))
        xh2 = ctx2.enter_context(nc.sbuf_tensor(f"{tag}_xh2", [128, 8 * 512], BF16))
        a2 = ctx2.enter_context(nc.sbuf_tensor(f"{tag}_a2", [128, 8 * 512], BF16))
        s1r = s_fc1_wT.rearrange("(c k) h -> k c h", c=NC)
        s2r = s_fc2_wT.rearrange("(fc k) d -> k fc d", fc=16)
        for fh in range(2):
            em.dma("wg", "g",
                   s1sb[:, :].rearrange("k (c h) -> k c h", c=NC)[:, :, 0:1024],
                   s1r[:, :, fh * 1024:fh * 1024 + 1024], writes=[("s1sb",)])
            em.dma("wg", "g",
                   s1sb[:, :].rearrange("k (c h) -> k c h", c=NC)[:, :, 1024:2048],
                   s1r[:, :, 2048 + fh * 1024:2048 + fh * 1024 + 1024], writes=[("s1sb",)])
            em.dma("wg", "g", s2sb[:, :].rearrange("k (fc d) -> k fc d", fc=8),
                   s2r[:, fh * 8:fh * 8 + 8, :], writes=[("s2sb",)])
            em.dma("wg", "g", fbrow[0:1, 0:1024],
                   s_fc1_b[fh * 1024:fh * 1024 + 1024].rearrange("(o p) -> o p", o=1),
                   writes=[("fbrow",)])
            em.dma("wg", "g", fbrow[0:1, 1024:2048],
                   s_fc1_b[2048 + fh * 1024:2048 + fh * 1024 + 1024].rearrange("(o p) -> o p", o=1),
                   writes=[("fbrow",)])
            row_to_cols(B, em, fb, ("fb",), fbrow, ("fbrow",), 16)
            em.dma("wg", "g", fbrow[0:1, 0:1024],
                   s_geglu[fh * 1024:fh * 1024 + 1024].rearrange("(o p) -> o p", o=1),
                   writes=[("fbrow",)])
            row_to_cols(B, em, gg, ("gg",), fbrow, ("fbrow",), 8)
            for tc in range(NTC):
                for j in range(8):
                    em.op("t", lambda tt, tc=tc, j=j: [
                        tt.matmul(B.ps[4][:, :],
                                  s1sb[:, c * 2048 + j * 128: c * 2048 + j * 128 + 128],
                                  xlt_chunk(xlt, c, tc * 512, 512),
                                  start=(c == 0), stop=(c == NC - 1)) for c in range(NC)][-1],
                        reads=[("s1sb",)] + xlt_keys, writes=[("ps", 4)])
                    em.op("t", lambda tt, tc=tc, j=j: [
                        tt.matmul(B.ps[5][:, :],
                                  s1sb[:, c * 2048 + 1024 + j * 128: c * 2048 + 1024 + j * 128 + 128],
                                  xlt_chunk(xlt, c, tc * 512, 512),
                                  start=(c == 0), stop=(c == NC - 1)) for c in range(NC)][-1],
                        reads=[("s1sb",)] + xlt_keys, writes=[("ps", 5)])
                    em.op("v", lambda v, j=j: v.tensor_scalar(
                        xh2[:, j * 512:(j + 1) * 512], B.ps[4][:, :],
                        fb[:, j:j + 1], None, OP.add),
                        reads=[("ps", 4), ("fb",)], writes=[("xh2", j)])
                    em.op("s", lambda s, j=j: s.activation(gel[:, :], B.ps[5][:, :], AF.Gelu,
                                                           bias=fb[:, 8 + j:9 + j], scale=1.0),
                          reads=[("ps", 5), ("fb",)], writes=[("gel",)])
                    em.op("v", lambda v, j=j: v.scalar_tensor_tensor(
                        a2[:, j * 512:(j + 1) * 512], gel[:, :], gg[:, j:j + 1],
                        xh2[:, j * 512:(j + 1) * 512], OP.mult, OP.mult),
                        reads=[("gel",), ("gg",), ("xh2", j)], writes=[("a2", j)])
                for ti in range(4):
                    t = tc * 4 + ti
                    for half in range(2):
                        em.op("t", lambda tt, ti=ti, half=half: [
                            tt.matmul(B.ps[2 + half][:, :],
                                      a2[:, j * 512 + ti * 128: j * 512 + ti * 128 + 128],
                                      s2sb[:, j * D + half * 512: j * D + half * 512 + 512],
                                      start=(j == 0), stop=(j == 7)) for j in range(8)][-1],
                            reads=[("a2", j) for j in range(8)] + [("s2sb",)],
                            writes=[("ps", 2 + half)])
                        em.op("v", lambda v, half=half: v.tensor_copy(
                            yscr[:, half * 512:half * 512 + 512], B.ps[2 + half][:, :]),
                            reads=[("ps", 2 + half)], writes=[("ysc", half)])
                    em.dma("ya", "g", y_hbm[t * 128:(t + 1) * 128, :], yscr[:, :],
                           reads=[("ysc", 0), ("ysc", 1)],
                           writes=[(tag, "yhbm", t) if fh == 1 else (tag, "yhbm_s", t)],
                           accum_op=OP.add)

    for t in range(NT):
        em.op("t", lambda tt, t=t: tt.matmul(B.ps[0][0:16, 0:128],
                                             comb[:, t * 16:(t + 1) * 16],
                                             B.iden_f32[:, :], start=True, stop=True),
              reads=[(tag, "comb", t), (tag, "comb1",), ("iden_f32",)], writes=[("ps", 0)])
        em.op("v", lambda v: v.tensor_copy(combt[0:16, :], B.ps[0][0:16, 0:128]),
              reads=[("ps", 0)], writes=[("combt",)])
        for half in range(2):
            em.op("t", lambda tt, half=half: tt.matmul(
                B.ps[2 + half][:, :], combt[0:16, :],
                b16sb[0:16, half * 512:half * 512 + 512], start=True, stop=True),
                reads=[("combt",), ("b16sb",)], writes=[("ps", 2 + half)])
        em.dma("xy", "y", xres[:, :], x_dram[t * 128:(t + 1) * 128, :], writes=[("xres",)])
        em.dma("yl", "y", ytile[:, :], y_hbm[t * 128:(t + 1) * 128, :],
               reads=[(tag, "yhbm", t)], writes=[("ytile",)])
        for half in range(2):
            em.op("v", lambda v, half=half: v.tensor_tensor(
                out_sb[:, half * 512:half * 512 + 512], B.ps[2 + half][:, :],
                ytile[:, half * 512:half * 512 + 512], OP.add),
                reads=[("ps", 2 + half), ("ytile",)], writes=[("osb", half)])
            em.op("v", lambda v, half=half: v.tensor_tensor(
                out_sb[:, half * 512:half * 512 + 512],
                out_sb[:, half * 512:half * 512 + 512],
                xres[:, half * 512:half * 512 + 512], OP.add),
                reads=[("osb", half), ("xres",)], writes=[("osb", half)])
        em.dma("xo", "y", xout_dram[t * 128:(t + 1) * 128, :], out_sb[:, :],
               reads=[("osb", 0), ("osb", 1)])



def emit_l2f(B, x_dram, W_expT, b_exp, up_w, up_b, down_w, W_con, b_con, down_b,
             xout_dram, tag="l2f"):
    em, nc, ctx = B.em, B.nc, B.ctx
    F = 4096
    MuT = ctx.enter_context(nc.sbuf_tensor(f"{tag}_MuT", [128, 32 * 128], BF16))
    MdT = ctx.enter_context(nc.sbuf_tensor(f"{tag}_MdT", [128, 32 * 128], BF16))
    GuT = ctx.enter_context(nc.sbuf_tensor(f"{tag}_GuT", [128, 32 * 128], BF16))
    GdT = ctx.enter_context(nc.sbuf_tensor(f"{tag}_GdT", [128, 32 * 128], BF16))
    b1sh = ctx.enter_context(nc.sbuf_tensor(f"{tag}_b1sh", [128, 32], F32))
    upbsh = ctx.enter_context(nc.sbuf_tensor(f"{tag}_upbsh", [128, 32], F32))

    gdram = nc.dram_tensor(f"{tag}_gdram", [F, 512], BF16).ap()
    zdram = nc.dram_tensor(f"{tag}_zdram", [F, 512], BF16).ap()
    h5dram = nc.dram_tensor(f"{tag}_h5dram", [F, T], BF16).ap()
    y_hbm = nc.dram_tensor(f"{tag}_yhbm", [T, D], F32).ap()
    b1dram = nc.dram_tensor(f"{tag}_b1d", [F], F32).ap()

    with ExitStack() as ctx2:
        masks = ctx2.enter_context(nc.sbuf_tensor(f"{tag}_masks", [128, 14 * 128], BF16))
        stg = ctx2.enter_context(nc.sbuf_tensor(f"{tag}_stg", [1, 3 * F], F32))
        wstage = ctx2.enter_context(nc.sbuf_tensor(f"{tag}_wst", [1, 8192], F32))
        vb = [ctx2.enter_context(nc.sbuf_tensor(f"{tag}_vb{i}", [128, 32 * 128], BF16))
              for i in range(2)]
        acc = ctx2.enter_context(nc.sbuf_tensor(f"{tag}_acc", [128, 128], BF16))
        tmp = ctx2.enter_context(nc.sbuf_tensor(f"{tag}_tmp", [128, 128], BF16))

        for mi in range(7):
            for sign in range(2):
                d = (2 ** mi) * (1 if sign == 0 else -1)
                em.op("v", lambda v, mi=mi, sign=sign, d=d: v.tensor_scalar(
                    masks[:, (2 * mi + sign) * 128:(2 * mi + sign + 1) * 128],
                    B.iota_fp[:, 0:128], float(d), None, OP.is_equal),
                    reads=[("iota_fp",)], writes=[(tag, "mask", mi, sign)])

        def build_chain(w_dram, big, MT, MT_key):
            nstages = 5 if big else 7
            for si in range(nstages):
                s_abs = (7 + si) if big else si
                em.dma("wg", "g", wstage[0:1, :],
                       w_dram[s_abs].rearrange("k a b -> (k a b)").rearrange("(o p) -> o p", o=1),
                       writes=[("wstage",)])
                em.op("v", lambda v: v.memset(stg[0:1, :], 0.0), writes=[("stg",)])
                if not big:
                    s2 = 2 ** si
                    q = 2048 // s2
                    for base, off in ((0, 0), (s2, 3), (F, 1), (2 * F + s2, 2)):
                        em.op("v", lambda v, s2=s2, q=q, off=off, base=base: v.tensor_copy(
                            bass.AP(stg, base, [[3 * F, 1], [2 * s2, q], [1, s2]]),
                            bass.AP(wstage, off, [[8192, 1], [4 * s2, q], [4, s2]])),
                            reads=[("wstage",)], writes=[("stg",)])
                    mi = si
                else:
                    sg = 2 ** si
                    nhi = 32 // (2 * sg)
                    for part, bit, off in ((0, 0, 0), (0, 1, 3), (F, 0, 1), (2 * F, 1, 2)):
                        for r in range(4):
                            em.op("v", lambda v, sg=sg, nhi=nhi, part=part, bit=bit,
                                  off=off, r=r: v.tensor_copy(
                                bass.AP(stg, part + 4 * sg * bit + r,
                                        [[3 * F, 1], [8 * sg, nhi], [4, sg], [128, 32]]),
                                bass.AP(wstage, off + 4 * r,
                                        [[8192, 1], [512 * sg, nhi], [512, sg], [16, 32]])),
                                reads=[("wstage",)], writes=[("stg",)])
                    mi = si + 2
                for b in range(32):
                    em.op("t", lambda t, b=b: t.matmul(
                        B.ps[0][:, 0:128], B.ones_f[0:1, 0:128], stg[0:1, b * 128:(b + 1) * 128],
                        start=True, stop=True),
                        reads=[("stg",), ("ones_f",)], writes=[("ps", 0)])
                    em.op("v", lambda v: v.tensor_tensor(acc[:, :], B.ps[0][:, 0:128],
                                                         B.ident[:, :], OP.mult),
                          reads=[("ps", 0), ("ident",)], writes=[("acc",)])
                    em.op("t", lambda t, b=b: t.matmul(
                        B.ps[0][:, 0:128], B.ones_f[0:1, 0:128],
                        stg[0:1, F + b * 128:F + (b + 1) * 128],
                        start=True, stop=True),
                        reads=[("stg",), ("ones_f",)], writes=[("ps", 0)])
                    em.op("v", lambda v, mi=mi: v.tensor_tensor(
                        tmp[:, :], B.ps[0][:, 0:128],
                        masks[:, (2 * mi + 1) * 128:(2 * mi + 2) * 128], OP.mult),
                        reads=[("ps", 0), (tag, "mask", mi, 1)], writes=[("tmp",)])
                    em.op("v", lambda v: v.tensor_tensor(acc[:, :], acc[:, :], tmp[:, :], OP.add),
                          reads=[("acc",), ("tmp",)], writes=[("acc",)])
                    em.op("t", lambda t, b=b: t.matmul(
                        B.ps[0][:, 0:128], B.ones_f[0:1, 0:128],
                        stg[0:1, 2 * F + b * 128:2 * F + (b + 1) * 128],
                        start=True, stop=True),
                        reads=[("stg",), ("ones_f",)], writes=[("ps", 0)])
                    em.op("v", lambda v, mi=mi: v.tensor_tensor(
                        tmp[:, :], B.ps[0][:, 0:128],
                        masks[:, (2 * mi) * 128:(2 * mi + 1) * 128], OP.mult),
                        reads=[("ps", 0), (tag, "mask", mi, 0)], writes=[("tmp",)])
                    em.op("v", lambda v: v.tensor_tensor(acc[:, :], acc[:, :], tmp[:, :], OP.add),
                          reads=[("acc",), ("tmp",)], writes=[("acc",)])
                    src = B.ident[:, :] if si == 0 else vb[(si + 1) % 2][:, b * 128:(b + 1) * 128]
                    src_key = ("ident",) if si == 0 else ("vb", (si + 1) % 2, b)
                    em.op("t", lambda t, src=src: t.matmul(
                        B.ps[1][:, 0:128], acc[:, :], src, start=True, stop=True),
                        reads=[("acc",), src_key], writes=[("ps", 1)])
                    em.op("v", lambda v, b=b, si=si: v.tensor_copy(
                        vb[si % 2][:, b * 128:(b + 1) * 128], B.ps[1][:, 0:128]),
                        reads=[("ps", 1)], writes=[("vb", si % 2, b)])
            last = (nstages - 1) % 2
            for b in range(32):
                em.op("t", lambda t, b=b, last=last: t.matmul(
                    B.ps[1][:, 0:128], vb[last][:, b * 128:(b + 1) * 128], B.ident[:, :],
                    start=True, stop=True),
                    reads=[("vb", last, b), ("ident",)], writes=[("ps", 1)])
                em.op("v", lambda v, b=b: v.tensor_copy(MT[:, b * 128:(b + 1) * 128],
                                                        B.ps[1][:, 0:128]),
                      reads=[("ps", 1)], writes=[MT_key])

        build_chain(up_w, False, MuT, (tag, "MuT"))
        build_chain(up_w, True, GuT, (tag, "GuT"))
        build_chain(down_w, False, MdT, (tag, "MdT"))
        build_chain(down_w, True, GdT, (tag, "GdT"))
    em.barrier()

    xlt = ctx.enter_context(nc.sbuf_tensor(f"{tag}_xlt", [128, NC * T], BF16))
    w1eff = ctx.enter_context(nc.sbuf_tensor(f"{tag}_w1eff", [128, NC * F], BF16))
    bb = ctx.enter_context(nc.sbuf_tensor(f"{tag}_bb", [128, D], F32))
    hstage = ctx.enter_context(nc.sbuf_tensor(f"{tag}_hst", [128, 512], BF16))
    gstage = ctx.enter_context(nc.sbuf_tensor(f"{tag}_gst", [128, 512], BF16))
    zstage = ctx.enter_context(nc.sbuf_tensor(f"{tag}_zst", [128, 512], BF16))
    gnorm = ctx.enter_context(nc.sbuf_tensor(f"{tag}_gn", [128, 512], BF16))
    zshuf = ctx.enter_context(nc.sbuf_tensor(f"{tag}_zs", [128, 512], BF16))
    h5l = ctx.enter_context(nc.sbuf_tensor(f"{tag}_h5l", [128, 4 * 128], BF16))
    wcs = ctx.enter_context(nc.sbuf_tensor(f"{tag}_wcs", [128, 4 * D], BF16))
    yscr = ctx.enter_context(nc.sbuf_tensor(f"{tag}_yscr", [128, D], F32))
    xres = ctx.enter_context(nc.sbuf_tensor(f"{tag}_xres", [128, D], F32))
    out_sb = ctx.enter_context(nc.sbuf_tensor(f"{tag}_out", [128, D], F32))
    brow = xres[0:1, :]
    brow2 = yscr[0:1, :]

    xlt_keys = emit_ln_xlt(B, x_dram, xlt, tag)

    with ExitStack() as ctx3:
        bcol = ctx3.enter_context(nc.sbuf_tensor(f"{tag}_bcol", [128, 32], BF16))
        bce = ctx3.enter_context(nc.sbuf_tensor(f"{tag}_bce", [128, 128], F32))
        wexpb = ctx3.enter_context(nc.sbuf_tensor(f"{tag}_wex", [128, D], BF16))
        for b in range(32):
            em.dma("wg", "g", wexpb[:, :],
                   W_expT[b * 128:(b + 1) * 128, :], writes=[("wexpb",)])
            for c in range(NC):
                em.op("t", lambda t, b=b, c=c: t.matmul(
                    B.ps[2][:, 0:128], wexpb[:, c * 128:(c + 1) * 128],
                    MuT[:, b * 128:(b + 1) * 128], start=True, stop=True),
                    reads=[("wexpb",), (tag, "MuT")], writes=[("ps", 2)])
                em.op("v", lambda v, b=b, c=c: v.tensor_copy(
                    bass.AP(w1eff, c * F + 4 * b, [[NC * F, 128], [128, 32], [1, 4]]),
                    B.ps[2][:, 0:128].rearrange("k (g r) -> k g r", r=4)),
                    reads=[("ps", 2)], writes=[("w1eff",)])
        for b in range(32):
            em.dma("wg", "g", bcol[:, b:b + 1],
                   b_exp[b * 128:(b + 1) * 128].rearrange("(p o) -> p o", o=1),
                   writes=[("bcol",)])
        em.op("t", lambda t: [
            t.matmul(B.ps[2][:, b:b + 1], MuT[:, b * 128:(b + 1) * 128], bcol[:, b:b + 1],
                     start=(b == 0), stop=(b == 31)) for b in range(32)][-1],
            reads=[(tag, "MuT"), ("bcol",)], writes=[("ps", 2)])
        em.op("v", lambda v: v.tensor_copy(bce[:, 0:32], B.ps[2][:, 0:32]),
              reads=[("ps", 2)], writes=[("bce",)])
        em.op("t", lambda t: t.matmul(B.ps[3][0:32, 0:128], bce[:, 0:32], B.iden_f32[:, :],
                                      start=True, stop=True),
              reads=[("bce",), ("iden_f32",)], writes=[("ps", 3)])
        em.op("v", lambda v: v.tensor_copy(bce[0:32, 0:128], B.ps[3][0:32, 0:128]),
              reads=[("ps", 3)], writes=[("bceT",)])
        em.dma("bf", "y", b1dram.rearrange("(b k) -> b k", k=128), bce[0:32, 0:128],
               reads=[("bceT",)], writes=[(tag, "b1d")])
        for g in range(32):
            em.dma("bf", "y", b1sh[:, g:g + 1],
                   bass.AP(b1dram.tensor, 4 * g, [[128, 32], [1, 4]]),
                   reads=[(tag, "b1d")], writes=[(tag, "b1sh")])
            em.dma("wg", "g", upbsh[:, g:g + 1],
                   bass.AP(up_b.tensor, 4 * g, [[128, 32], [1, 4]]),
                   writes=[(tag, "upbsh")])

    em.dma("wg", "g", brow[0:1, 0:1024], b_con.rearrange("(o p) -> o p", o=1),
           writes=[("brow",)])
    em.dma("wg", "g", brow2[0:1, 0:1024], down_b.rearrange("(o p) -> o p", o=1),
           writes=[("brow2",)])
    em.op("v", lambda v: v.tensor_tensor(brow[0:1, :], brow[0:1, :], brow2[0:1, :], OP.add),
          reads=[("brow",), ("brow2",)], writes=[("brow",)])
    for half in range(2):
        em.op("t", lambda t, half=half: t.matmul(
            B.ps[3][:, :], B.ones_f[0:1, 0:128], brow[0:1, half * 512:half * 512 + 512],
            start=True, stop=True),
            reads=[("brow",), ("ones_f",)], writes=[("ps", 3)])
        em.op("v", lambda v, half=half: v.tensor_copy(bb[:, half * 512:half * 512 + 512],
                                                      B.ps[3][:, :]),
              reads=[("ps", 3)], writes=[("bb", half)])

    for tc in range(NTC):
        for g in range(32):
            em.op("t", lambda t, tc=tc, g=g: [
                t.matmul(B.ps[4][:, :], w1eff[:, c * F + g * 128: c * F + (g + 1) * 128],
                         xlt_chunk(xlt, c, tc * 512, 512),
                         start=(c == 0), stop=(c == NC - 1)) for c in range(NC)][-1],
                reads=[("w1eff",)] + xlt_keys, writes=[("ps", 4)])
            em.op("v", lambda v, g=g: v.tensor_scalar(hstage[:, :], B.ps[4][:, :],
                                                      b1sh[:, g:g + 1], None, OP.add),
                  reads=[("ps", 4), (tag, "b1sh")], writes=[("hstage",)])
            em.op("t", lambda t, g=g: t.matmul(B.ps[5][:, :], GuT[:, g * 128:(g + 1) * 128],
                                               hstage[:, :], start=True, stop=True),
                  reads=[("hstage",), (tag, "GuT")], writes=[("ps", 5)])
            em.op("s", lambda s, g=g: s.activation(gstage[:, :], B.ps[5][:, :], AF.Gelu,
                                                   bias=upbsh[:, g:g + 1], scale=1.0),
                  reads=[("ps", 5), (tag, "upbsh")], writes=[("gstage",)])
            em.dma("bf", "y",
                   bass.AP(gdram.tensor, 4 * g * 512, [[128 * 512, 32], [512, 4], [1, 512]]),
                   gstage[:, :], reads=[("gstage",)], writes=[(tag, "gd", g)])
        for b in range(32):
            em.dma("bf2", "s", gnorm[:, :], gdram[b * 128:(b + 1) * 128, :],
                   reads=[(tag, "gd", gg2) for gg2 in range(32)], writes=[("gnorm",)])
            em.op("t", lambda t, b=b: t.matmul(B.ps[2][:, :], MdT[:, b * 128:(b + 1) * 128],
                                               gnorm[:, :], start=True, stop=True),
                  reads=[("gnorm",), (tag, "MdT")], writes=[("ps", 2)])
            em.op("v", lambda v: v.tensor_copy(zstage[:, :], B.ps[2][:, :]),
                  reads=[("ps", 2)], writes=[("zstage",)])
            em.dma("bf", "y", zdram[b * 128:(b + 1) * 128, :], zstage[:, :],
                   reads=[("zstage",)], writes=[(tag, "zd", b)])
        for g in range(32):
            em.dma("bf2", "s", zshuf[:, :],
                   bass.AP(zdram.tensor, 4 * g * 512, [[128 * 512, 32], [512, 4], [1, 512]]),
                   reads=[(tag, "zd", bb2) for bb2 in range(32)], writes=[("zshuf",)])
            em.op("t", lambda t, g=g: t.matmul(B.ps[3][:, :], GdT[:, g * 128:(g + 1) * 128],
                                               zshuf[:, :], start=True, stop=True),
                  reads=[("zshuf",), (tag, "GdT")], writes=[("ps", 3)])
            em.op("v", lambda v: v.tensor_copy(zstage[:, :], B.ps[3][:, :]),
                  reads=[("ps", 3)], writes=[("zstage",)])
            em.dma("bf", "y", h5dram[g * 128:(g + 1) * 128, tc * 512:(tc + 1) * 512],
                   zstage[:, :], reads=[("zstage",)], writes=[(tag, "h5", tc, g)])

    h5keys = [(tag, "h5", tc, g) for tc in range(NTC) for g in range(32)]
    for bat in range(8):
        for gi in range(4):
            g = bat * 4 + gi
            em.dma("wg", "g", wcs[:, gi * D:(gi + 1) * D],
                   bass.AP(W_con.tensor, 4 * g * D, [[128 * D, 32], [D, 4], [1, D]]),
                   writes=[("wcs", gi)])
        for t in range(NT):
            em.dma("oc", "y", h5l[:, :],
                   bass.AP(h5dram.tensor, bat * 4 * 128 * T + t * 128,
                           [[T, 128], [128 * T, 4], [1, 128]]),
                   reads=h5keys, writes=[("h5l",)])
            for half in range(2):
                em.op("t", lambda tt, half=half: [
                    tt.matmul(B.ps[2 + half][:, :], h5l[:, gi * 128:(gi + 1) * 128],
                              wcs[:, gi * D + half * 512: gi * D + half * 512 + 512],
                              start=(gi == 0), stop=(gi == 3)) for gi in range(4)][-1],
                    reads=[("h5l",)] + [("wcs", gi) for gi in range(4)],
                    writes=[("ps", 2 + half)])
                em.op("v", lambda v, half=half: v.tensor_copy(
                    yscr[:, half * 512:half * 512 + 512], B.ps[2 + half][:, :]),
                    reads=[("ps", 2 + half)], writes=[("ysc", half)])
            kw = {} if bat == 0 else {"accum_op": OP.add}
            em.dma("ya", "g", y_hbm[t * 128:(t + 1) * 128, :], yscr[:, :],
                   reads=[("ysc", 0), ("ysc", 1)],
                   writes=[(tag, "yhbm", t) if bat == 7 else (tag, "yhbm_p", t, bat)], **kw)

    for t in range(NT):
        em.dma("xy", "y", xres[:, :], x_dram[t * 128:(t + 1) * 128, :], writes=[("xres",)])
        em.dma("yl", "y", yscr[:, :], y_hbm[t * 128:(t + 1) * 128, :],
               reads=[(tag, "yhbm", t), ("ysc", 0), ("ysc", 1)], writes=[("yscr_f",)])
        for half in range(2):
            em.op("v", lambda v, half=half: v.tensor_tensor(
                out_sb[:, half * 512:half * 512 + 512], yscr[:, half * 512:half * 512 + 512],
                bb[:, half * 512:half * 512 + 512], OP.add),
                reads=[("yscr_f",), ("bb", half)], writes=[("osb", half)])
            em.op("v", lambda v, half=half: v.tensor_tensor(
                out_sb[:, half * 512:half * 512 + 512],
                out_sb[:, half * 512:half * 512 + 512],
                xres[:, half * 512:half * 512 + 512], OP.add),
                reads=[("osb", half), ("xres",)], writes=[("osb", half)])
        em.dma("xo", "y", xout_dram[t * 128:(t + 1) * 128, :], out_sb[:, :],
               reads=[("osb", 0), ("osb", 1)])



def build_l2a_program():
    nc = bass.Bass("TRN2", target_bir_lowering=False)
    x = nc.dram_tensor("x", [T, D], F32, kind="ExternalInput")
    names = ["wqT", "bq", "wkT", "bk", "wvT", "bv", "woT", "bo"]
    hs = []
    for n in names:
        if n.startswith("w"):
            hs.append(nc.dram_tensor(n, [D, D], F32, kind="ExternalInput"))
        else:
            hs.append(nc.dram_tensor(n, [D], F32, kind="ExternalInput"))
    xout = nc.dram_tensor("xout", [T, D], F32, kind="ExternalOutput")
    with ExitStack() as ctx:
        em = Em(nc, ctx)
        block = ctx.enter_context(nc.Block())
        B = alloc_common(nc, ctx, em)
        emit_l2a(B, x.ap(), *[h.ap() for h in hs], xout.ap())
        em.finalize(block)
    return nc


def build_l3a_program():
    nc = bass.Bass("TRN2", target_bir_lowering=False)
    x = nc.dram_tensor("x", [T, D], F32, kind="ExternalInput")
    wqkvT = nc.dram_tensor("wqkvT", [D, 3 * D], F32, kind="ExternalInput")
    bqkv = nc.dram_tensor("bqkv", [3 * D], F32, kind="ExternalInput")
    woutT = nc.dram_tensor("woutT", [D, D], F32, kind="ExternalInput")
    bout = nc.dram_tensor("bout", [D], F32, kind="ExternalInput")
    xout = nc.dram_tensor("xout", [T, D], F32, kind="ExternalOutput")
    with ExitStack() as ctx:
        em = Em(nc, ctx)
        block = ctx.enter_context(nc.Block())
        B = alloc_common(nc, ctx, em)
        emit_l3a(B, x.ap(), wqkvT.ap(), bqkv.ap(), woutT.ap(), bout.ap(), xout.ap())
        em.finalize(block)
    return nc


def build_l2f_program():
    nc = bass.Bass("TRN2", target_bir_lowering=False)
    x = nc.dram_tensor("x", [T, D], F32, kind="ExternalInput")
    WexpT = nc.dram_tensor("WexpT", [4096, D], F32, kind="ExternalInput")
    b_exp = nc.dram_tensor("b_exp", [4096], F32, kind="ExternalInput")
    up_w = nc.dram_tensor("up_w", [12, 2048, 2, 2], F32, kind="ExternalInput")
    up_b = nc.dram_tensor("up_b", [4096], F32, kind="ExternalInput")
    down_w = nc.dram_tensor("down_w", [12, 2048, 2, 2], F32, kind="ExternalInput")
    W_con = nc.dram_tensor("W_con", [4096, D], F32, kind="ExternalInput")
    b_con = nc.dram_tensor("b_con", [D], F32, kind="ExternalInput")
    down_b = nc.dram_tensor("down_b", [D], F32, kind="ExternalInput")
    xout = nc.dram_tensor("xout", [T, D], F32, kind="ExternalOutput")
    with ExitStack() as ctx:
        em = Em(nc, ctx)
        block = ctx.enter_context(nc.Block())
        B = alloc_common(nc, ctx, em)
        emit_l2f(B, x.ap(), WexpT.ap(), b_exp.ap(), up_w.ap(), up_b.ap(),
                 down_w.ap(), W_con.ap(), b_con.ap(), down_b.ap(), xout.ap())
        em.finalize(block)
    return nc


def rope_tables_np():
    dim = 32
    inv_freq = 1.0 / 10000.0 ** (np.arange(0, dim, 2, dtype=np.float32) / dim)
    tt = np.arange(T, dtype=np.float32)
    freqs = np.outer(tt, inv_freq)
    emb = np.concatenate([freqs, freqs], -1)
    cos = np.cos(emb).astype(np.float32)
    sin = np.sin(emb).astype(np.float32)
    cos4 = np.ascontiguousarray(np.tile(cos.T, (4, 1)))
    sin4 = np.ascontiguousarray(np.tile(sin.T, (4, 1)))
    RT = np.zeros((128, 128), dtype=np.float32)
    for b in range(0, 128, 32):
        for j in range(16):
            RT[b + j + 16, b + j] = -1.0
            RT[b + j, b + j + 16] = 1.0
    return cos4, sin4, RT


def build_l1a_program():
    nc = bass.Bass("TRN2", target_bir_lowering=False)
    x = nc.dram_tensor("x", [T, D], F32, kind="ExternalInput")
    names = [("W_kv", [D, 256]), ("g_kv", [256]), ("W_kdec", [256, D]),
             ("W_vdec", [256, D]), ("W_q", [D, 512]), ("W_qdec", [512, D]),
             ("W_krope", [D, 512]), ("W_qrope", [512, 512]), ("W_o", [D, D]),
             ("cos4", [128, T]), ("sin4", [128, T]), ("rot_permT", [128, 128])]
    hs = {n: nc.dram_tensor(n, sh, F32, kind="ExternalInput") for n, sh in names}
    xout = nc.dram_tensor("xout", [T, D], F32, kind="ExternalOutput")
    with ExitStack() as ctx:
        em = Em(nc, ctx)
        block = ctx.enter_context(nc.Block())
        B = alloc_common(nc, ctx, em)
        emit_l1a(B, x.ap(), *[hs[n].ap() for n, _ in names], xout.ap())
        em.finalize(block)
    return nc


def build_l1f_program():
    nc = bass.Bass("TRN2", target_bir_lowering=False)
    x = nc.dram_tensor("x", [T, D], F32, kind="ExternalInput")
    wT = nc.dram_tensor("wT", [D, D], F32, kind="ExternalInput")
    xout = nc.dram_tensor("xout", [T, D], F32, kind="ExternalOutput")
    with ExitStack() as ctx:
        em = Em(nc, ctx)
        block = ctx.enter_context(nc.Block())
        B = alloc_common(nc, ctx, em)
        emit_l1f(B, x.ap(), wT.ap(), xout.ap())
        em.finalize(block)
    return nc


def build_l0a_program(dbg=False):
    nc = bass.Bass("TRN2", target_bir_lowering=False)
    x = nc.dram_tensor("x", [T, D], F32, kind="ExternalInput")
    wqkvT = nc.dram_tensor("wqkvT", [D, 3 * D], F32, kind="ExternalInput")
    bqkv = nc.dram_tensor("bqkv", [3 * D], F32, kind="ExternalInput")
    wprojT = nc.dram_tensor("wprojT", [D, D], F32, kind="ExternalInput")
    bproj = nc.dram_tensor("bproj", [D], F32, kind="ExternalInput")
    xout = nc.dram_tensor("xout", [T, D], F32, kind="ExternalOutput")
    with ExitStack() as ctx:
        em = Em(nc, ctx)
        block = ctx.enter_context(nc.Block())
        B = alloc_common(nc, ctx, em)
        emit_l0a(B, x.ap(), wqkvT.ap(), bqkv.ap(), wprojT.ap(), bproj.ap(), xout.ap(), dbg=dbg)
        em.finalize(block)
    return nc


def build_l0f_program():
    nc = bass.Bass("TRN2", target_bir_lowering=False)
    x = nc.dram_tensor("x", [T, D], F32, kind="ExternalInput")
    wg = nc.dram_tensor("wg", [D, 8], F32, kind="ExternalInput")
    w1 = nc.dram_tensor("w1", [8, D, 2048], F32, kind="ExternalInput")
    w3 = nc.dram_tensor("w3", [8, D, 2048], F32, kind="ExternalInput")
    w2 = nc.dram_tensor("w2", [8, 2048, D], F32, kind="ExternalInput")
    xout = nc.dram_tensor("xout", [T, D], F32, kind="ExternalOutput")
    with ExitStack() as ctx:
        em = Em(nc, ctx)
        block = ctx.enter_context(nc.Block())
        B = alloc_common(nc, ctx, em)
        emit_l0f(B, x.ap(), wg.ap(), [w1.ap()[e] for e in range(8)],
                 [w3.ap()[e] for e in range(8)], [w2.ap()[e] for e in range(8)], xout.ap())
        em.finalize(block)
    return nc


def build_l3f_program():
    nc = bass.Bass("TRN2", target_bir_lowering=False)
    x = nc.dram_tensor("x", [T, D], F32, kind="ExternalInput")
    gwT = nc.dram_tensor("gwT", [D, 15], F32, kind="ExternalInput")
    fc1T = nc.dram_tensor("fc1T", [15, D, 2048], F32, kind="ExternalInput")
    fc1b = nc.dram_tensor("fc1b", [15, 2048], F32, kind="ExternalInput")
    gg = nc.dram_tensor("gg", [15, D], F32, kind="ExternalInput")
    fc2T = nc.dram_tensor("fc2T", [15, D, D], F32, kind="ExternalInput")
    b16 = nc.dram_tensor("b16", [16, D], F32, kind="ExternalInput")
    s1T = nc.dram_tensor("s1T", [D, 4096], F32, kind="ExternalInput")
    s1b = nc.dram_tensor("s1b", [4096], F32, kind="ExternalInput")
    sgg = nc.dram_tensor("sgg", [2048], F32, kind="ExternalInput")
    s2T = nc.dram_tensor("s2T", [2048, D], F32, kind="ExternalInput")
    xout = nc.dram_tensor("xout", [T, D], F32, kind="ExternalOutput")
    with ExitStack() as ctx:
        em = Em(nc, ctx)
        block = ctx.enter_context(nc.Block())
        B = alloc_common(nc, ctx, em)
        emit_l3f(B, x.ap(), gwT.ap(), [fc1T.ap()[e] for e in range(15)], fc1b.ap(),
                 gg.ap(), [fc2T.ap()[e] for e in range(15)], b16.ap(),
                 s1T.ap(), s1b.ap(), sgg.ap(), s2T.ap(), xout.ap())
        em.finalize(block)
    return nc



def _prep_stages(inputs):
    i = {k: np.ascontiguousarray(v) for k, v in inputs.items()}
    ct = np.ascontiguousarray
    cos4, sin4, rotT = rope_tables_np()
    stages = [
        ("l0a", build_l0a_program,
         {"wqkvT": ct(i["L0_attn_c_attn_kernel"].T), "bqkv": i["L0_attn_c_attn_bias"],
          "wprojT": ct(i["L0_attn_c_proj_kernel"].T), "bproj": i["L0_attn_c_proj_bias"]}),
        ("l0f", build_l0f_program,
         {"wg": i["L0_ffn_w_gate"], "w1": i["L0_ffn_w1"], "w3": i["L0_ffn_w3"],
          "w2": i["L0_ffn_w2"]}),
        ("l1a", build_l1a_program,
         {"W_kv": i["L1_attn_W_kv"], "g_kv": i["L1_attn_g_kv"],
          "W_kdec": i["L1_attn_W_kdec"], "W_vdec": i["L1_attn_W_vdec"],
          "W_q": i["L1_attn_W_q"], "W_qdec": i["L1_attn_W_qdec"],
          "W_krope": i["L1_attn_W_krope"], "W_qrope": i["L1_attn_W_qrope"],
          "W_o": i["L1_attn_W_o"], "cos4": cos4, "sin4": sin4, "rot_permT": rotT}),
        ("l1f", build_l1f_program, {"wT": ct(i["L1_ffn_weight"].T)}),
        ("l2a", build_l2a_program,
         {"wqT": ct(i["L2_attn_Wq"].T), "bq": i["L2_attn_bq"],
          "wkT": ct(i["L2_attn_Wk"].T), "bk": i["L2_attn_bk"],
          "wvT": ct(i["L2_attn_Wv"].T), "bv": i["L2_attn_bv"],
          "woT": ct(i["L2_attn_Wo"].T), "bo": i["L2_attn_bo"]}),
        ("l2f", build_l2f_program,
         {"WexpT": ct(i["L2_ffn_W_exp"].T), "b_exp": i["L2_ffn_b_exp"],
          "up_w": i["L2_ffn_up_weights"], "up_b": i["L2_ffn_up_bias"],
          "down_w": i["L2_ffn_down_weights"], "W_con": i["L2_ffn_W_con"],
          "b_con": i["L2_ffn_b_con"], "down_b": i["L2_ffn_down_bias"]}),
        ("l3a", build_l3a_program,
         {"wqkvT": ct(i["L3_attn_W_qkv"].T), "bqkv": i["L3_attn_b_qkv"],
          "woutT": ct(i["L3_attn_W_out"].T), "bout": i["L3_attn_b_out"]}),
        ("l3f", build_l3f_program,
         {"gwT": ct(i["L3_ffn_gate_w"].T), "fc1T": ct(i["L3_ffn_fc1_w"].transpose(0, 2, 1)),
          "fc1b": i["L3_ffn_fc1_b"], "gg": i["L3_ffn_geglu_mult"],
          "fc2T": ct(i["L3_ffn_fc2_w"].transpose(0, 2, 1)),
          "b16": ct(np.concatenate([i["L3_ffn_fc2_b"], i["L3_ffn_s_fc2_b"][None]], 0)),
          "s1T": ct(i["L3_ffn_s_fc1_w"].T), "s1b": i["L3_ffn_s_fc1_b"],
          "sgg": i["L3_ffn_s_geglu_mult"], "s2T": ct(i["L3_ffn_s_fc2_w"].T)}),
    ]
    return stages


def _launch_retry(build, in_maps, ncores, attempts=3):
    import time as _time
    last = None
    for att in range(attempts):
        nc = build()
        try:
            return run_bass_kernel_spmd(nc, in_maps, core_ids=list(range(ncores)))
        except Exception as ex:
            last = ex
            _time.sleep(10)
    raise last


def kernel(**inputs):
    x = np.ascontiguousarray(np.asarray(inputs["L0_attn_hidden_states"], dtype=np.float32))
    ncores = x.shape[0]
    shards = [np.ascontiguousarray(x[c]) for c in range(ncores)]
    for name, build, wmap in _prep_stages(inputs):
        in_maps = [dict(wmap, x=shards[c]) for c in range(ncores)]
        res = _launch_retry(build, in_maps, ncores)
        shards = [np.asarray(res.results[c]["xout"]) for c in range(ncores)]
    return np.stack(shards, axis=0)


# version 03 (final)
# speedup vs baseline: 1.0065x; 1.0065x over previous
import math
from contextlib import ExitStack

import numpy as np

import concourse.bass as bass
import concourse.mybir as mybir
from concourse.bass_utils import run_bass_kernel_spmd
from concourse.alu_op_type import AluOpType as OP

F32 = mybir.dt.float32
BF16 = mybir.dt.bfloat16
I32 = mybir.dt.int32
AF = mybir.ActivationFunctionType
AX = mybir.AxisListType

T = 4096
D = 1024
NT = T // 128
NC = D // 128
TC = 512
NTC = T // TC

ENGS = ("t", "v", "s", "g", "y")


class Step:
    __slots__ = ("eng", "fn", "deps", "inc", "val", "ch")

    def __init__(self, eng, fn, inc=1, ch=None):
        self.eng, self.fn, self.inc, self.ch = eng, fn, inc, ch
        self.deps = []
        self.val = None


class Em:

    def __init__(self, nc, ctx):
        self.nc = nc
        self.ctx = ctx
        self.steps = []
        self.sems = {e: ctx.enter_context(nc.semaphore(f"S_{e}")) for e in ENGS}
        self.chsems = {}
        self.chcount = {}
        self.writer = {}
        self.readers = {}

    def _dep_list(self, reads, writes):
        deps = []
        for b in reads:
            w = self.writer.get(b)
            if w is not None:
                deps.append(w)
        for b in writes:
            deps.extend(self.readers.get(b, ()))
            w = self.writer.get(b)
            if w is not None:
                deps.append(w)
        out = []
        seen = set()
        for d in deps:
            if id(d) not in seen:
                seen.add(id(d))
                if d.ch is not None:
                    out.append((d, self.chcount[d.ch]))
                else:
                    out.append((d, None))
        return out

    def _record(self, s, reads, writes):
        self.steps.append(s)
        for b in reads:
            self.readers.setdefault(b, []).append(s)
        for b in writes:
            self.writer[b] = s
            self.readers[b] = []

    def op(self, eng, fn, reads=(), writes=(), extra_deps=()):
        s = Step(eng, fn)
        s.deps = self._dep_list(reads, writes) + [(d, None) for d in extra_deps]
        self._record(s, reads, writes)
        return s

    def dma(self, ch, eng, out, in_, reads=(), writes=(), **kw):
        if ch not in self.chsems:
            self.chsems[ch] = self.ctx.enter_context(self.nc.semaphore(f"C_{ch}"))
            self.chcount[ch] = 0
            self.chlast = getattr(self, "chlast", {})
            self.chlast[ch] = None

        def fn(e, out=out, in_=in_, kw=kw):
            return e.dma_start(out, in_, **kw)

        s = Step(eng, fn, inc=16, ch=ch)
        s.deps = self._dep_list(reads, writes)
        prev = self.chlast.get(ch)
        if prev is not None:
            s.deps.append((prev, prev.val))
        self.chcount[ch] += 16
        s.val = self.chcount[ch]
        self.chlast[ch] = s
        self._record(s, reads, writes)
        return s

    def barrier(self):
        lasts = {}
        for s in self.steps:
            lasts[(s.eng, s.ch)] = s
        alll = [(d, self.chcount[d.ch] if d.ch is not None else None)
                for d in lasts.values()]
        newwriter = {}
        for e in ENGS:
            s = Step(e, None)
            s.deps = list(alll)
            self.steps.append(s)
            newwriter[("barrier", e)] = s
        self.writer = newwriter
        self.readers = {}

    def finalize(self, block):
        needed = set()
        for s in self.steps:
            s.deps = [(d, v) for (d, v) in s.deps
                      if not (d.ch is None and d.eng == s.eng == "t")]
            for d, v in s.deps:
                if d.ch is None:
                    needed.add(id(d))
        cnt = {e: 0 for e in ENGS}
        for s in self.steps:
            if s.ch is None and id(s) in needed and s.fn is not None:
                cnt[s.eng] += 1
                s.val = cnt[s.eng]
        final_ch = dict(self.chcount)
        final_eng = dict(cnt)

        def make_emitter(eng_name):
            mysteps = [s for s in self.steps if s.eng == eng_name]

            def emit(engobj):
                hw = {}
                for s in mysteps:
                    for d, vover in s.deps:
                        if d.ch is not None:
                            k, sem, v = ("ch", d.ch), self.chsems[d.ch], vover
                        else:
                            if d.val is None:
                                continue
                            k, sem, v = ("e", d.eng), self.sems[d.eng], d.val
                        if hw.get(k, 0) < v:
                            engobj.wait_ge(sem, v)
                            hw[k] = v
                    if s.fn is None:
                        continue
                    instr = s.fn(engobj)
                    if s.val is not None:
                        if s.ch is not None:
                            instr.then_inc(self.chsems[s.ch], 16)
                        else:
                            instr.then_inc(self.sems[s.eng], 1)
                if eng_name == "y":
                    for ch, v in final_ch.items():
                        if v > 0:
                            engobj.wait_ge(self.chsems[ch], v)
                    for e, v in final_eng.items():
                        if v > 0 and e != "y":
                            engobj.wait_ge(self.sems[e], v)
            return emit

        for e, section in (("t", block.tensor), ("v", block.vector), ("s", block.scalar),
                           ("g", block.gpsimd), ("y", block.sync)):
            section(make_emitter(e))



class Bundle:
    pass


def alloc_common(nc, ctx, em):
    B = Bundle()
    B.nc, B.em, B.ctx = nc, em, ctx
    B.ident = ctx.enter_context(nc.sbuf_tensor("ident_bf", [128, 128], BF16))
    B.iden_f32 = ctx.enter_context(nc.sbuf_tensor("iden_f32", [128, 128], F32))
    B.iota_fp = ctx.enter_context(nc.sbuf_tensor("iota_fp", [128, 512], F32))
    B.ones_t = ctx.enter_context(nc.sbuf_tensor("ones_t", [128, 128], BF16))
    B.consts = ctx.enter_context(nc.sbuf_tensor("consts", [128, 8], F32))
    B.iota_i = ctx.enter_context(nc.sbuf_tensor("iota_i", [128, 512], I32))
    B.xtile = ctx.enter_context(nc.sbuf_tensor("xtile", [128, D], F32))
    B.bn12 = ctx.enter_context(nc.sbuf_tensor("bn12", [128, 12], F32))
    B.mv = ctx.enter_context(nc.sbuf_tensor("mv", [128, 4], F32))
    B.xl = ctx.enter_context(nc.sbuf_tensor("xl", [128, D], BF16))
    B.ps_bf = ctx.enter_context(nc.psum_tensor("ps_bf", [128, 1024], BF16))
    B.ps = [ctx.enter_context(nc.psum_tensor(f"ps{i}", [128, 512], F32)) for i in range(6)]
    em.op("g", lambda g: g.iota(B.iota_i[:, :], [[1, 512]], base=0, channel_multiplier=-1),
          writes=[("iota_i",)])
    em.op("v", lambda v: v.tensor_copy(B.iota_fp[:, :], B.iota_i[:, :]),
          reads=[("iota_i",)], writes=[("iota_fp",)])
    em.op("v", lambda v: v.tensor_scalar(B.ident[:, :], B.iota_fp[:, 0:128], 0.0, None, OP.is_equal),
          reads=[("iota_fp",)], writes=[("ident",)])
    em.op("v", lambda v: v.tensor_scalar(B.iden_f32[:, :], B.iota_fp[:, 0:128], 0.0, None, OP.is_equal),
          reads=[("iota_fp",)], writes=[("iden_f32",)])
    em.op("v", lambda v: v.memset(B.ones_t[:, :], 1.0), writes=[("ones_t",)])
    B.ones_f = ctx.enter_context(nc.sbuf_tensor("ones_f", [128, 128], F32))
    em.op("v", lambda v: v.memset(B.ones_f[:, :], 1.0), writes=[("ones_f",)])
    em.op("v", lambda v: v.memset(B.consts[:, 0:1], 1e-5), writes=[("consts", 0)])
    em.op("v", lambda v: v.memset(B.consts[:, 1:2], 1e-6), writes=[("consts", 1)])
    em.op("v", lambda v: v.memset(B.consts[:, 2:3], 1.0), writes=[("consts", 2)])
    return B


def emit_ln_xlt(B, x_dram, xlt, tag):
    em = B.em
    kx, kbn, kmv, kxl, kps = ("xtile",), ("bn12",), ("mv",), ("xl",), ("psbf",)
    keys = []
    for t in range(NT):
        em.dma("xy", "y", B.xtile[:, :], x_dram[t * 128:(t + 1) * 128, :], writes=[kx])
        em.op("v", lambda v: v.bn_stats(B.bn12[:, 0:6], B.xtile[:, 0:512]),
              reads=[kx], writes=[(tag, "bn6a", t)])
        em.op("v", lambda v: v.bn_stats(B.bn12[:, 6:12], B.xtile[:, 512:1024]),
              reads=[kx], writes=[kbn], extra_deps=[em.writer[(tag, "bn6a", t)]])
        em.op("v", lambda v: v.bn_aggr(B.mv[:, 0:2], B.bn12[:, :]),
              reads=[kbn, (tag, "bn6a", t)], writes=[("mv", 0)])
        em.op("s", lambda s: s.activation(B.mv[:, 2:3], B.mv[:, 1:2], AF.Ln,
                                          bias=B.consts[:, 0:1], scale=1.0),
              reads=[("mv", 0), ("consts", 0)], writes=[("mv", 2)])
        em.op("s", lambda s: s.activation(B.mv[:, 3:4], B.mv[:, 2:3], AF.Exp, scale=-0.5),
              reads=[("mv", 2)], writes=[("mv", 3)])
        em.op("v", lambda v: v.tensor_scalar(B.xl[:, :], B.xtile[:, :],
                                             B.mv[:, 0:1], B.mv[:, 3:4],
                                             OP.subtract, OP.mult),
              reads=[kx, ("mv", 0), ("mv", 3)], writes=[kxl])
        for c in range(NC):
            em.op("t", lambda t_, c=c: t_.transpose(
                B.ps_bf[:, c * 128:(c + 1) * 128], B.xl[:, c * 128:(c + 1) * 128], B.ident[:, :]),
                reads=[kxl, ("ident",)], writes=[kps])
        k = (tag, "xlt", t)
        em.op("v", lambda v, t=t: v.tensor_copy(
            xlt[:, :].rearrange("k (c tt) -> k c tt", c=NC)[:, :, t * 128:(t + 1) * 128],
            B.ps_bf[:, :].rearrange("k (c m) -> k c m", c=NC)),
            reads=[kps], writes=[k])
        keys.append(k)
    return keys


def xlt_chunk(xlt, c, t0, n):
    return xlt[:, c * T + t0: c * T + t0 + n]



def emit_l1f(B, x_dram, wT_dram, xout_dram, tag="l1f"):
    em, nc, ctx = B.em, B.nc, B.ctx
    xlt = ctx.enter_context(nc.sbuf_tensor("l1f_xlt", [128, NC * T], BF16))
    w_f32 = ctx.enter_context(nc.sbuf_tensor("l1f_w", [128, NC * D], F32))
    w_bf = ctx.enter_context(nc.sbuf_tensor("l1f_wq", [128, NC * D], BF16))
    w_rt = ctx.enter_context(nc.sbuf_tensor("l1f_wrt", [128, 2 * D], F32))
    red = ctx.enter_context(nc.sbuf_tensor("l1f_red", [128, NC + 2], F32))
    sc = ctx.enter_context(nc.sbuf_tensor("l1f_sc", [128, 4], F32))
    scp = ctx.enter_context(nc.sbuf_tensor("l1f_scp", [1, 4], F32))
    out_sb = ctx.enter_context(nc.sbuf_tensor("l1f_out", [128, D], F32))
    xres = ctx.enter_context(nc.sbuf_tensor("l1f_xres", [128, D], F32))

    xlt_keys = emit_ln_xlt(B, x_dram, xlt, tag)

    wtr = wT_dram.rearrange("(c k) o -> k c o", c=NC)
    em.dma("wg", "g", w_f32[:, :].rearrange("k (c o) -> k c o", c=NC), wtr,
           writes=[("l1f_w",)])
    for c in range(NC):
        em.op("v", lambda v, c=c: v.reduce_sum(red[:, c:c + 1], w_f32[:, c * D:(c + 1) * D],
                                               axis=AX.X, apply_absolute_value=True),
              reads=[("l1f_w",)], writes=[("red", c)])
    em.op("v", lambda v: v.reduce_sum(red[:, NC:NC + 1], red[:, 0:NC], axis=AX.X),
          reads=[("red", c) for c in range(NC)], writes=[("redt",)])
    em.op("t", lambda t: t.matmul(B.ps[0][0:1, 0:128], red[:, NC:NC + 1],
                                  B.iden_f32[:, :], start=True, stop=True),
          reads=[("redt",), ("iden_f32",)], writes=[("ps", 0)])
    em.op("v", lambda v: v.reduce_sum(scp[0:1, 0:1], B.ps[0][0:1, 0:128], axis=AX.X),
          reads=[("ps", 0)], writes=[("scp", 0)])
    em.op("v", lambda v: v.tensor_scalar(scp[0:1, 1:2], scp[0:1, 0:1], 1.0 / (D * D), 1e-5,
                                         OP.mult, OP.max),
          reads=[("scp", 0)], writes=[("scp", 1)])
    em.op("v", lambda v: v.reciprocal(scp[0:1, 2:3], scp[0:1, 1:2]),
          reads=[("scp", 1)], writes=[("scp", 2)])
    em.op("t", lambda t: t.matmul(B.ps[1][:, 0:2], B.ones_f[0:1, 0:128], scp[0:1, 1:3],
                                  start=True, stop=True),
          reads=[("scp", 1), ("scp", 2), ("ones_f",)], writes=[("ps", 1)])
    em.op("v", lambda v: v.tensor_copy(sc[:, 0:2], B.ps[1][:, 0:2]),
          reads=[("ps", 1)], writes=[("sc",)])
    MAGIC = 12582912.0
    for c in range(NC):
        em.op("v", lambda v, c=c: v.tensor_scalar(w_rt[:, 0:D], w_f32[:, c * D:(c + 1) * D],
                                                  sc[:, 1:2], None, OP.mult),
              reads=[("l1f_w",), ("sc",)], writes=[("w_rt",)])
        em.op("v", lambda v, c=c: v.tensor_scalar(w_rt[:, D:2 * D], w_rt[:, 0:D],
                                                  MAGIC, MAGIC, OP.add, OP.subtract),
              reads=[("w_rt",)], writes=[("w_rt2",)])
        em.op("v", lambda v, c=c: v.tensor_scalar(w_bf[:, c * D:(c + 1) * D], w_rt[:, D:2 * D],
                                                  1.0, -1.0, OP.min, OP.max),
              reads=[("w_rt2",)], writes=[("wq_nm", c)])
        em.op("v", lambda v, c=c: v.tensor_scalar(w_bf[:, c * D:(c + 1) * D],
                                                  w_bf[:, c * D:(c + 1) * D],
                                                  sc[:, 0:1], None, OP.mult),
              reads=[("wq_nm", c), ("sc",)], writes=[("wq", c)])

    wq_keys = [("wq", c) for c in range(NC)]
    for t in range(NT):
        for half in range(2):
            for c in range(NC):
                em.op("t", lambda tt, t=t, c=c, half=half: tt.matmul(
                    B.ps[2 + half][:, :],
                    xlt_chunk(xlt, c, t * 128, 128),
                    w_bf[:, c * D + half * 512: c * D + half * 512 + 512],
                    start=(c == 0), stop=(c == NC - 1)),
                    reads=[xlt_keys[t]] + wq_keys, writes=[("ps", 2 + half)])
        em.dma("xy", "y", xres[:, :], x_dram[t * 128:(t + 1) * 128, :],
               writes=[("xres",)])
        for half in range(2):
            em.op("v", lambda v, t=t, half=half: v.tensor_tensor(
                out_sb[:, half * 512:half * 512 + 512], B.ps[2 + half][:, :],
                xres[:, half * 512:half * 512 + 512], OP.add),
                reads=[("ps", 2 + half), ("xres",)], writes=[("out", half)])
        em.dma("xo", "y", xout_dram[t * 128:(t + 1) * 128, :], out_sb[:, :],
               reads=[("out", 0), ("out", 1)])



def alloc_attn(B, tag):
    nc, ctx = B.nc, B.ctx
    A = Bundle()
    A.tag = tag
    A.qT = ctx.enter_context(nc.sbuf_tensor(f"{tag}_qT", [128, T], BF16))
    A.kT = ctx.enter_context(nc.sbuf_tensor(f"{tag}_kT", [128, T], BF16))
    A.vton = ctx.enter_context(nc.sbuf_tensor(f"{tag}_vton", [128, NT * 130], BF16))
    A.pt = [ctx.enter_context(nc.sbuf_tensor(f"{tag}_pt{i}", [128, 512], BF16)) for i in range(2)]
    A.rbc = ctx.enter_context(nc.sbuf_tensor(f"{tag}_rbc", [128, 512], BF16))
    A.rden_f = ctx.enter_context(nc.sbuf_tensor(f"{tag}_rdenf", [128, 512], F32))
    A.rden = ctx.enter_context(nc.sbuf_tensor(f"{tag}_rden", [128, 512], BF16))
    A.ot = ctx.enter_context(nc.sbuf_tensor(f"{tag}_ot", [128, 2 * T], BF16))
    A.nrm = ctx.enter_context(nc.sbuf_tensor(f"{tag}_nrm", [128, 24], F32))
    A.negb = ctx.enter_context(nc.sbuf_tensor(f"{tag}_negb", [128, 2], F32))
    A.indic = ctx.enter_context(nc.sbuf_tensor(f"{tag}_indic", [128, 128], BF16))
    A.masks = ctx.enter_context(nc.sbuf_tensor(f"{tag}_masks", [128, 4 * 512], BF16))
    em = B.em
    em.op("v", lambda v: v.memset(A.indic[:, :], 0.0), writes=[(tag, "indic")])
    em.op("v", lambda v: v.memset(A.indic[0:64, 0:1], 1.0),
          reads=[(tag, "indic")], writes=[(tag, "indic")])
    em.op("v", lambda v: v.memset(A.indic[64:128, 64:65], 1.0),
          reads=[(tag, "indic")], writes=[(tag, "indic")])
    for j in range(4):
        em.op("v", lambda v, j=j: v.tensor_scalar(A.masks[:, j * 512:(j + 1) * 512],
                                                  B.iota_fp[:, :], float(128 * j), None, OP.is_ge),
              reads=[("iota_fp",)], writes=[(tag, "mask", j)])
    em.op("v", lambda v: v.memset(
        A.vton[:, :].rearrange("p (t c) -> p t c", c=130)[:, :, 64:65], 1.0),
        writes=[(tag, "vton_ones")])
    em.op("v", lambda v: v.memset(
        A.vton[:, :].rearrange("p (t c) -> p t c", c=130)[:, :, 129:130], 1.0),
        writes=[(tag, "vton_ones2")])
    return A


def emit_bounds(B, A, scale, extra_col=None):
    em, tag = B.em, A.tag
    for which, src, key in (("q", A.qT, (tag, "qT")), ("k", A.kT, (tag, "kT"))):
        base = 0 if which == "q" else 8
        for tc in range(8):
            em.op("s", lambda s, src=src, tc=tc: s.activation(
                B.xl[:, 0:512], src[:, tc * 512:(tc + 1) * 512], AF.Square),
                reads=[key], writes=[("xl",)])
            em.op("t", lambda t, tc=tc: t.matmul(B.ps[3][:, :], A.indic[:, :],
                                                 B.xl[:, 0:512],
                                                 start=True, stop=True),
                  reads=[("xl",), (tag, "indic")], writes=[("ps", 3)])
            for r in (0, 64):
                em.op("v", lambda v, tc=tc, r=r, base=base: v.reduce_max(
                    A.nrm[r:r + 1, base + tc:base + tc + 1], B.ps[3][r:r + 1, :], axis=AX.X),
                    reads=[("ps", 3)], writes=[(tag, "nrm", which, tc, r)])
    for r in (0, 64):
        em.op("v", lambda v, r=r: v.reduce_max(A.nrm[r:r + 1, 16:17], A.nrm[r:r + 1, 0:8], axis=AX.X),
              reads=[(tag, "nrm", "q", tc, r) for tc in range(8)], writes=[(tag, "nq", r)])
        em.op("v", lambda v, r=r: v.reduce_max(A.nrm[r:r + 1, 17:18], A.nrm[r:r + 1, 8:16], axis=AX.X),
              reads=[(tag, "nrm", "k", tc, r) for tc in range(8)], writes=[(tag, "nk", r)])
        em.op("v", lambda v, r=r: v.tensor_tensor(A.nrm[r:r + 1, 18:19], A.nrm[r:r + 1, 16:17],
                                                  A.nrm[r:r + 1, 17:18], OP.mult),
              reads=[(tag, "nq", r), (tag, "nk", r)], writes=[(tag, "prod", r)])
        em.op("s", lambda s, r=r: s.activation(A.nrm[r:r + 1, 19:20], A.nrm[r:r + 1, 18:19], AF.Sqrt),
              reads=[(tag, "prod", r)], writes=[(tag, "sqb", r)])
        if extra_col is not None:
            et, ekey, ecol = extra_col
            em.op("v", lambda v, r=r: v.tensor_tensor(
                A.nrm[r:r + 1, 19:20], A.nrm[r:r + 1, 19:20],
                et[r:r + 1, ecol:ecol + 1], OP.add),
                reads=[(tag, "sqb", r), ekey], writes=[(tag, "sqb", r)])
        em.op("v", lambda v, r=r: v.tensor_scalar(A.nrm[r:r + 1, 20:21], A.nrm[r:r + 1, 19:20],
                                                  -scale, None, OP.mult),
              reads=[(tag, "sqb", r)], writes=[(tag, "negb1", r)])
        h = 0 if r == 0 else 1
        em.op("t", lambda t, r=r: t.matmul(B.ps[3][:, 0:1], B.ones_f[r:r + 1, 0:128],
                                           A.nrm[r:r + 1, 20:21], start=True, stop=True),
              reads=[(tag, "negb1", r), ("ones_f",)], writes=[("ps", 3)])
        em.op("v", lambda v, h=h: v.tensor_copy(A.negb[:, h:h + 1], B.ps[3][:, 0:1]),
              reads=[("ps", 3)], writes=[(tag, "negb", h)])


def emit_attn_core(B, A, pair, causal=True, use_exp=True, scale=1.0, dbg_cb=None,
                   s_extra=None):
    em, tag = B.em, A.tag
    nktiles = NT
    for h in range(2):
        hb = 64 * h
        for qc in range(NTC):
            rng = list(range(0, 4 * qc + 4)) if causal else list(range(nktiles))
            for i, ki in enumerate(rng):
                if dbg_cb is not None:
                    dbg_cb("pre_s", pair, h, qc, ki)
                sb = 4 + (ki % 2)
                em.op("t", lambda t, ki=ki, sb=sb, hb=hb, qc=qc: t.matmul(
                    B.ps[sb][:, :],
                    A.kT[hb:hb + 64, ki * 128:(ki + 1) * 128],
                    A.qT[hb:hb + 64, qc * 512:(qc + 1) * 512],
                    start=True, stop=(s_extra is None)),
                    reads=[(tag, "kT"), (tag, "qT")], writes=[("ps", sb)])
                if s_extra is not None:
                    s_extra(h, qc, ki, sb)
                pk = ("pt", ki % 2)
                if use_exp:
                    em.op("s", lambda s, ki=ki, sb=sb, h=h: s.activation(
                        A.pt[ki % 2][:, :], B.ps[sb][:, :], AF.Exp,
                        bias=A.negb[:, h:h + 1], scale=scale),
                        reads=[("ps", sb), (tag, "negb", h)], writes=[pk])
                else:
                    em.op("s", lambda s, ki=ki, sb=sb: s.activation(
                        A.pt[ki % 2][:, :], B.ps[sb][:, :], AF.Copy, scale=scale),
                        reads=[("ps", sb)], writes=[pk])
                if causal and ki >= 4 * qc:
                    j = ki - 4 * qc
                    em.op("v", lambda v, ki=ki, j=j: v.tensor_tensor(
                        A.pt[ki % 2][:, :], A.pt[ki % 2][:, :],
                        A.masks[:, j * 512:(j + 1) * 512], OP.mult),
                        reads=[pk, (tag, "mask", j)], writes=[pk])
                if dbg_cb is not None:
                    dbg_cb("post_mask", pair, h, qc, ki)
                em.op("t", lambda t, ki=ki, h=h, i=i, last=len(rng) - 1: t.matmul(
                    B.ps[2][0:65, :],
                    A.vton[:, ki * 130 + 65 * h: ki * 130 + 65 * h + 65],
                    A.pt[ki % 2][:, :],
                    start=(i == 0), stop=(i == last)),
                    reads=[pk, (tag, "vton")], writes=[("ps", 2)])
            em.op("v", lambda v: v.reciprocal(A.rden_f[64:65, :], B.ps[2][64:65, :]),
                  reads=[("ps", 2)], writes=[(tag, "rdenf")])
            em.op("v", lambda v: v.tensor_copy(A.rden[64:65, :], A.rden_f[64:65, :]),
                  reads=[(tag, "rdenf")], writes=[(tag, "rden")])
            em.op("t", lambda t: t.matmul(B.ps[3][:, :], B.ones_t[64:65, 0:128],
                                          A.rden[64:65, :], start=True, stop=True),
                  reads=[(tag, "rden"), ("ones_t",)], writes=[("ps", 3)])
            em.op("v", lambda v: v.tensor_copy(A.rbc[0:64, :], B.ps[3][0:64, :]),
                  reads=[("ps", 3)], writes=[(tag, "rbc")])
            if dbg_cb is not None:
                dbg_cb("post_pv", pair, h, qc, -1)
            em.op("v", lambda v, h=h, qc=qc: v.tensor_tensor(
                A.ot[0:64, h * T + qc * 512: h * T + (qc + 1) * 512],
                B.ps[2][0:64, :], A.rbc[0:64, :], OP.mult),
                reads=[("ps", 2), (tag, "rbc")], writes=[(tag, "ot", h, qc)])
    for h in range(2):
        em.dma("ot", "y", A.ot_hbm[128 * pair + 64 * h:128 * pair + 64 * h + 64, :],
               A.ot[0:64, h * T:(h + 1) * T],
               reads=[(tag, "ot", h, qc) for qc in range(NTC)],
               writes=[(tag, "ot_hbm", pair, h)])


def emit_proj_T(B, xlt, xlt_keys, w_sb, w_key, dst, dst_key, bias_col=None,
                bias_key=None, n_dout=128, nc_in=NC, src_chunk=None):
    em = B.em
    assert n_dout <= 128
    if src_chunk is None:
        src_chunk = lambda c, t0, n: xlt_chunk(xlt, c, t0, n)
    for tc in range(NTC):
        em.op("t", lambda t, tc=tc: [
            t.matmul(B.ps[0][0:n_dout, :],
                     w_sb[:, c * n_dout:(c + 1) * n_dout],
                     src_chunk(c, tc * 512, 512),
                     start=(c == 0), stop=(c == nc_in - 1)) for c in range(nc_in)][-1],
            reads=[w_key] + xlt_keys, writes=[("ps", 0)])
        if bias_col is not None:
            em.op("v", lambda v, tc=tc: v.tensor_scalar(
                dst[0:n_dout, tc * 512:(tc + 1) * 512], B.ps[0][0:n_dout, :],
                bias_col, None, OP.add),
                reads=[("ps", 0), bias_key], writes=[dst_key])
        else:
            em.op("v", lambda v, tc=tc: v.tensor_copy(
                dst[0:n_dout, tc * 512:(tc + 1) * 512], B.ps[0][0:n_dout, :]),
                reads=[("ps", 0)], writes=[dst_key])



def emit_mha(B, x_dram, xout_dram, tag, pair_loads, wprojT, bproj,
             causal=True, use_exp=True, scale=1.0, softplus_qk=False, dbg=False):
    em, nc, ctx = B.em, B.nc, B.ctx
    xlt = ctx.enter_context(nc.sbuf_tensor(f"{tag}_xlt", [128, NC * T], BF16))
    A = alloc_attn(B, tag)
    if dbg:
        A.ot_hbm = nc.dram_tensor(f"{tag}_othbm", [D, T], BF16, kind="ExternalOutput").ap()
    else:
        A.ot_hbm = nc.dram_tensor(f"{tag}_othbm", [D, T], BF16).ap()
    wq = ctx.enter_context(nc.sbuf_tensor(f"{tag}_wq", [128, NC * 128], BF16))
    wk = ctx.enter_context(nc.sbuf_tensor(f"{tag}_wk", [128, NC * 128], BF16))
    wv = ctx.enter_context(nc.sbuf_tensor(f"{tag}_wv", [128, NC * 128], BF16))
    bcols = ctx.enter_context(nc.sbuf_tensor(f"{tag}_bcols", [128, 2], F32))
    vbrow = ctx.enter_context(nc.sbuf_tensor(f"{tag}_vbrow", [1, 128], F32))
    vbbc = ctx.enter_context(nc.sbuf_tensor(f"{tag}_vbbc", [128, 128], F32))
    wpr = ctx.enter_context(nc.sbuf_tensor(f"{tag}_wpr", [128, NC * D], BF16))
    bb = ctx.enter_context(nc.sbuf_tensor(f"{tag}_bb", [128, D], F32))
    otl = ctx.enter_context(nc.sbuf_tensor(f"{tag}_otl", [128, NC * 128], BF16))
    out_sb = ctx.enter_context(nc.sbuf_tensor(f"{tag}_out", [128, D], F32))
    xres = ctx.enter_context(nc.sbuf_tensor(f"{tag}_xres", [128, D], F32))
    if softplus_qk:
        spa = ctx.enter_context(nc.sbuf_tensor(f"{tag}_spa", [128, T], F32))
        spb = ctx.enter_context(nc.sbuf_tensor(f"{tag}_spb", [128, T], F32))

    xlt_keys = emit_ln_xlt(B, x_dram, xlt, tag)

    em.dma("wg", "g", wpr[:, :].rearrange("k (c o) -> k c o", c=NC),
           wprojT.rearrange("(c k) o -> k c o", c=NC), writes=[("wpr",)])
    for c in range(NC):
        em.dma("wg", "g", vbrow[0:1, 0:128],
               bproj[c * 128:(c + 1) * 128].rearrange("(o p) -> o p", o=1),
               writes=[("bprow",)])
        em.op("t", lambda t: t.matmul(B.ps[3][:, 0:128], B.ones_f[0:1, 0:128],
                                      vbrow[0:1, 0:128], start=True, stop=True),
              reads=[("bprow",), ("ones_f",)], writes=[("ps", 3)])
        em.op("v", lambda v, c=c: v.tensor_copy(bb[:, c * 128:(c + 1) * 128], B.ps[3][:, 0:128]),
              reads=[("ps", 3)], writes=[("bb", c)])

    def softplus_inplace(dst, key):
        em.op("s", lambda s: s.activation(spa[:, :], dst[:, :], AF.Abs),
              reads=[key], writes=[(tag, "spa")])
        em.op("s", lambda s: s.activation(spa[:, :], spa[:, :], AF.Exp, scale=-1.0),
              reads=[(tag, "spa")], writes=[(tag, "spa")])
        em.op("s", lambda s: s.activation(spa[:, :], spa[:, :], AF.Ln,
                                          bias=B.consts[:, 2:3], scale=1.0),
              reads=[(tag, "spa"), ("consts", 2)], writes=[(tag, "spa")])
        em.op("v", lambda v: v.tensor_scalar(spb[:, :], dst[:, :], 0.0, None, OP.max),
              reads=[key], writes=[(tag, "spb")])
        em.op("v", lambda v: v.tensor_tensor(dst[:, :], spa[:, :], spb[:, :], OP.add),
              reads=[(tag, "spa"), (tag, "spb")], writes=[key])

    for pair in range(8):
        pair_loads(em, pair, wq, wk, wv, bcols, vbrow)
        em.op("t", lambda t: t.matmul(B.ps[3][:, 0:128], B.ones_f[0:1, 0:128],
                                      vbrow[0:1, 0:128], start=True, stop=True),
              reads=[("bprow",), ("ones_f",)], writes=[("ps", 3)])
        em.op("v", lambda v: v.tensor_copy(vbbc[:, :], B.ps[3][:, 0:128]),
              reads=[("ps", 3)], writes=[("vbbc",)])
        emit_proj_T(B, xlt, xlt_keys, wq, ("wq",), A.qT, (tag, "qT"),
                    bias_col=bcols[:, 0:1], bias_key=("bcol", 0))
        emit_proj_T(B, xlt, xlt_keys, wk, ("wk",), A.kT, (tag, "kT"),
                    bias_col=bcols[:, 1:2], bias_key=("bcol", 1))
        if softplus_qk:
            softplus_inplace(A.qT, (tag, "qT"))
            softplus_inplace(A.kT, (tag, "kT"))
        for t in range(NT):
            em.op("t", lambda tt, t=t: [
                tt.matmul(B.ps[1][:, 0:128],
                          xlt_chunk(xlt, c, t * 128, 128),
                          wv[:, c * 128:(c + 1) * 128],
                          start=(c == 0), stop=(c == NC - 1)) for c in range(NC)][-1],
                reads=[("wv",)] + xlt_keys, writes=[("ps", 1)])
            em.op("v", lambda v, t=t: v.tensor_tensor(
                A.vton[:, t * 130:t * 130 + 130].rearrange("p (h e) -> p h e", h=2)[:, :, 0:64],
                B.ps[1][:, 0:128].rearrange("p (h e) -> p h e", h=2),
                vbbc[:, :].rearrange("p (h e) -> p h e", h=2), OP.add),
                reads=[("ps", 1), ("vbbc",)], writes=[(tag, "vton")])
        if use_exp:
            emit_bounds(B, A, scale)
        emit_attn_core(B, A, pair, causal=causal, use_exp=use_exp, scale=scale,
                       dbg_cb=getattr(B, "dbg_cb", None))

    emit_outproj(B, A, tag, x_dram, xout_dram, wpr, bb, otl, out_sb, xres,
                 with_bias=True)


def emit_outproj(B, A, tag, x_dram, xout_dram, wpr, bb, otl, out_sb, xres,
                 with_bias=True):
    em = B.em
    othbm_keys = [(tag, "ot_hbm", p, h) for p in range(8) for h in range(2)]
    for t in range(NT):
        em.dma("oc", "y", otl[:, :].rearrange("k (c o) -> k c o", c=NC),
               A.ot_hbm[:, t * 128:(t + 1) * 128].rearrange("(c k) o -> k c o", c=NC),
               reads=othbm_keys, writes=[("otl",)])
        for half in range(2):
            em.op("t", lambda tt, t=t, half=half: [
                tt.matmul(B.ps[4 + half][:, :],
                          otl[:, c * 128:(c + 1) * 128],
                          wpr[:, c * D + half * 512: c * D + half * 512 + 512],
                          start=(c == 0), stop=(c == NC - 1)) for c in range(NC)][-1],
                reads=[("otl",), ("wpr",)], writes=[("ps", 4 + half)])
        em.dma("xy", "y", xres[:, :], x_dram[t * 128:(t + 1) * 128, :], writes=[("xres",)])
        for half in range(2):
            if with_bias:
                em.op("v", lambda v, half=half: v.tensor_tensor(
                    out_sb[:, half * 512:half * 512 + 512], B.ps[4 + half][:, :],
                    bb[:, half * 512:half * 512 + 512], OP.add),
                    reads=[("ps", 4 + half)] + [("bb", c) for c in range(NC)],
                    writes=[("osb", half)])
                em.op("v", lambda v, half=half: v.tensor_tensor(
                    out_sb[:, half * 512:half * 512 + 512],
                    out_sb[:, half * 512:half * 512 + 512],
                    xres[:, half * 512:half * 512 + 512], OP.add),
                    reads=[("osb", half), ("xres",)], writes=[("osb", half)])
            else:
                em.op("v", lambda v, half=half: v.tensor_tensor(
                    out_sb[:, half * 512:half * 512 + 512], B.ps[4 + half][:, :],
                    xres[:, half * 512:half * 512 + 512], OP.add),
                    reads=[("ps", 4 + half), ("xres",)], writes=[("osb", half)])
        em.dma("xo", "y", xout_dram[t * 128:(t + 1) * 128, :], out_sb[:, :],
               reads=[("osb", 0), ("osb", 1)])


def emit_l0a(B, x_dram, wqkvT, bqkv, wprojT, bproj, xout_dram, tag="l0a", dbg=False):
    def pair_loads(em, pair, wq, wk, wv, bcols, vbrow):
        for wsb, key, col0 in ((wq, ("wq",), pair * 128), (wk, ("wk",), D + pair * 128),
                               (wv, ("wv",), 2 * D + pair * 128)):
            em.dma("wg", "g", wsb[:, :].rearrange("k (c o) -> k c o", c=NC),
                   wqkvT[:, col0:col0 + 128].rearrange("(c k) o -> k c o", c=NC),
                   writes=[key])
        em.dma("wg", "g", bcols[:, 0:1],
               bqkv[pair * 128:pair * 128 + 128].rearrange("(p o) -> p o", o=1),
               writes=[("bcol", 0)])
        em.dma("wg", "g", bcols[:, 1:2],
               bqkv[D + pair * 128:D + pair * 128 + 128].rearrange("(p o) -> p o", o=1),
               writes=[("bcol", 1)])
        em.dma("wg", "g", vbrow[0:1, 0:128],
               bqkv[2 * D + pair * 128:2 * D + pair * 128 + 128].rearrange("(o p) -> o p", o=1),
               writes=[("bprow",)])

    emit_mha(B, x_dram, xout_dram, tag, pair_loads, wprojT, bproj,
             causal=True, use_exp=True, scale=1.0 / 8.0, dbg=dbg)


def emit_l2a(B, x_dram, wqT, bq, wkT, bk, wvT, bv, woT, bo, xout_dram, tag="l2a"):
    def pair_loads(em, pair, wq, wk, wv, bcols, vbrow):
        for wsb, key, src in ((wq, ("wq",), wqT), (wk, ("wk",), wkT), (wv, ("wv",), wvT)):
            em.dma("wg", "g", wsb[:, :].rearrange("k (c o) -> k c o", c=NC),
                   src[:, pair * 128:(pair + 1) * 128].rearrange("(c k) o -> k c o", c=NC),
                   writes=[key])
        em.dma("wg", "g", bcols[:, 0:1],
               bq[pair * 128:pair * 128 + 128].rearrange("(p o) -> p o", o=1),
               writes=[("bcol", 0)])
        em.dma("wg", "g", bcols[:, 1:2],
               bk[pair * 128:pair * 128 + 128].rearrange("(p o) -> p o", o=1),
               writes=[("bcol", 1)])
        em.dma("wg", "g", vbrow[0:1, 0:128],
               bv[pair * 128:pair * 128 + 128].rearrange("(o p) -> o p", o=1),
               writes=[("bprow",)])

    emit_mha(B, x_dram, xout_dram, tag, pair_loads, woT, bo,
             causal=True, use_exp=False, scale=1.0, softplus_qk=True)


def emit_l3a(B, x_dram, wqkvT, bqkv, woutT, bout, xout_dram, tag="l3a"):
    wr = wqkvT.rearrange("(c k) (g e) -> k c g e", c=NC, e=192)
    br = bqkv.rearrange("(g e) -> g e", e=192)

    def pair_loads(em, pair, wq, wk, wv, bcols, vbrow):
        for wsb, key, off in ((wq, ("wq",), 0), (wk, ("wk",), 64), (wv, ("wv",), 128)):
            for g in range(2):
                em.dma("wg", "g",
                       wsb[:, :].rearrange("k (c o) -> k c o", c=NC)[:, :, 64 * g:64 * g + 64],
                       wr[:, :, 2 * pair + g, off:off + 64],
                       writes=[key])
        for g in range(2):
            h = 2 * pair + g
            em.dma("wg", "g", bcols[64 * g:64 * g + 64, 0:1],
                   bqkv[192 * h:192 * h + 64].rearrange("(p o) -> p o", o=1),
                   writes=[("bcol", 0)])
            em.dma("wg", "g", bcols[64 * g:64 * g + 64, 1:2],
                   bqkv[192 * h + 64:192 * h + 128].rearrange("(p o) -> p o", o=1),
                   writes=[("bcol", 1)])
            em.dma("wg", "g", vbrow[0:1, 64 * g:64 * g + 64],
                   bqkv[192 * h + 128:192 * h + 192].rearrange("(o p) -> o p", o=1),
                   writes=[("bprow",)])

    emit_mha(B, x_dram, xout_dram, tag, pair_loads, woutT, bout,
             causal=False, use_exp=True, scale=1.0 / 8.0)



def emit_l1a(B, x_dram, W_kv, g_kv, W_kdec, W_vdec, W_q, W_qdec, W_krope, W_qrope,
             W_o, cos4, sin4, rot_permT, xout_dram, tag="l1a"):
    em, nc, ctx = B.em, B.nc, B.ctx
    RSC = 1.0 / math.sqrt(96.0)
    qcT = ctx.enter_context(nc.sbuf_tensor(f"{tag}_qcT", [128, 4 * T], BF16))
    kvT = ctx.enter_context(nc.sbuf_tensor(f"{tag}_kvT", [128, 2 * T], BF16))
    krT = ctx.enter_context(nc.sbuf_tensor(f"{tag}_krT", [128, 4 * T], BF16))
    qrT = ctx.enter_context(nc.sbuf_tensor(f"{tag}_qrT", [128, 4 * T], BF16))
    ropeb = ctx.enter_context(nc.sbuf_tensor(f"{tag}_ropeb", [128, 12], F32))

    with ExitStack() as ctx2:
        xlt = ctx2.enter_context(nc.sbuf_tensor(f"{tag}_xlt", [128, NC * T], BF16))
        cosb = ctx2.enter_context(nc.sbuf_tensor(f"{tag}_cos", [128, 512], BF16))
        sinb = ctx2.enter_context(nc.sbuf_tensor(f"{tag}_sin", [128, 512], BF16))
        rotp = ctx2.enter_context(nc.sbuf_tensor(f"{tag}_rotp", [128, 128], BF16))
        wkv = ctx2.enter_context(nc.sbuf_tensor(f"{tag}_wkv", [128, NC * 256], BF16))
        wbig = ctx2.enter_context(nc.sbuf_tensor(f"{tag}_wbig", [128, NC * 128], BF16))
        gkvr = ctx2.enter_context(nc.sbuf_tensor(f"{tag}_gkvr", [1, 256], F32))
        gkvb = ctx2.enter_context(nc.sbuf_tensor(f"{tag}_gkvb", [128, 256], F32))
        kvtok = ctx2.enter_context(nc.sbuf_tensor(f"{tag}_kvtok", [128, 256], BF16))
        rcol = ctx2.enter_context(nc.sbuf_tensor(f"{tag}_rcol", [128, 4], F32))
        kvsq = B.xl[:, 0:256]
        rsc1 = B.xtile[:, 0:512]
        rsc2 = B.xtile[:, 512:1024]

        xlt_keys = emit_ln_xlt(B, x_dram, xlt, tag)
        em.dma("wg", "g", rotp[:, :], rot_permT, writes=[("rotp",)])
        em.dma("wg", "g", wkv[:, :].rearrange("k (c o) -> k c o", c=NC),
               W_kv.rearrange("(c k) o -> k c o", c=NC), writes=[("wkv",)])
        em.dma("wg", "g", gkvr[0:1, :], g_kv.rearrange("(o p) -> o p", o=1),
               writes=[("gkvr",)])
        em.op("t", lambda t: t.matmul(B.ps[3][:, 0:256], B.ones_f[0:1, 0:128],
                                      gkvr[0:1, :], start=True, stop=True),
              reads=[("gkvr",), ("ones_f",)], writes=[("ps", 3)])
        em.op("v", lambda v: v.tensor_copy(gkvb[:, :], B.ps[3][:, 0:256]),
              reads=[("ps", 3)], writes=[("gkvb",)])

        for t in range(NT):
            em.op("t", lambda tt, t=t: [
                tt.matmul(B.ps[0][:, 0:256], xlt_chunk(xlt, c, t * 128, 128),
                          wkv[:, c * 256:(c + 1) * 256],
                          start=(c == 0), stop=(c == NC - 1)) for c in range(NC)][-1],
                reads=[("wkv",)] + xlt_keys, writes=[("ps", 0)])
            em.op("s", lambda s: s.activation(kvsq, B.ps[0][:, 0:256], AF.Square,
                                              accum_out=rcol[:, 0:1]),
                  reads=[("ps", 0)], writes=[("xl",), ("rcol", 0)])
            em.op("s", lambda s: s.activation(rcol[:, 1:2], rcol[:, 0:1], AF.Ln,
                                              bias=B.consts[:, 1:2], scale=1.0 / 256),
                  reads=[("rcol", 0), ("consts", 1)], writes=[("rcol", 1)])
            em.op("s", lambda s: s.activation(rcol[:, 2:3], rcol[:, 1:2], AF.Exp, scale=-0.5),
                  reads=[("rcol", 1)], writes=[("rcol", 2)])
            em.op("v", lambda v: v.scalar_tensor_tensor(kvtok[:, :], B.ps[0][:, 0:256],
                                                        rcol[:, 2:3], gkvb[:, :],
                                                        OP.mult, OP.mult),
                  reads=[("ps", 0), ("rcol", 2), ("gkvb",)], writes=[("kvtok",)])
            em.op("t", lambda tt, t=t: [
                tt.transpose(B.ps_bf[:, r * 128:(r + 1) * 128], kvtok[:, r * 128:(r + 1) * 128],
                             B.ident[:, :]) for r in range(2)][-1],
                reads=[("kvtok",), ("ident",)], writes=[("psbf",)])
            em.op("v", lambda v, t=t: v.tensor_copy(
                kvT[:, :].rearrange("k (r tt) -> k r tt", r=2)[:, :, t * 128:(t + 1) * 128],
                B.ps_bf[:, 0:256].rearrange("k (r m) -> k r m", r=2)),
                reads=[("psbf",)], writes=[(tag, "kvT")])

        for i in range(4):
            em.dma("wg", "g", wbig[:, :].rearrange("k (c o) -> k c o", c=NC),
                   W_q[:, i * 128:(i + 1) * 128].rearrange("(c k) o -> k c o", c=NC),
                   writes=[("wbig",)])
            emit_proj_T(B, xlt, xlt_keys, wbig, ("wbig",),
                        qcT[:, i * T:(i + 1) * T], (tag, "qcT"))
        for i in range(4):
            em.dma("wg", "g", wbig[:, :].rearrange("k (c o) -> k c o", c=NC),
                   W_krope[:, i * 128:(i + 1) * 128].rearrange("(c k) o -> k c o", c=NC),
                   writes=[("wbig",)])
            emit_proj_T(B, xlt, xlt_keys, wbig, ("wbig",),
                        krT[:, i * T:(i + 1) * T], (tag, "krT"))
        for i in range(4):
            em.dma("wg", "g", wbig[:, 0:4 * 128].rearrange("k (c o) -> k c o", c=4),
                   W_qrope[:, i * 128:(i + 1) * 128].rearrange("(c k) o -> k c o", c=4),
                   writes=[("wbig",)])
            emit_proj_T(B, xlt, [(tag, "qcT")], wbig, ("wbig",),
                        qrT[:, i * T:(i + 1) * T], (tag, "qrT"), nc_in=4,
                        src_chunk=lambda c, t0, n: qcT[:, c * T + t0:c * T + t0 + n])

        for tc in range(NTC):
            em.dma("wg", "g", cosb[:, :], cos4[:, tc * 512:(tc + 1) * 512], writes=[("cosb",)])
            em.dma("wg", "g", sinb[:, :], sin4[:, tc * 512:(tc + 1) * 512], writes=[("sinb",)])
            for name, ten, key in (("kr", krT, (tag, "krT")), ("qr", qrT, (tag, "qrT"))):
                for i in range(4):
                    sl = slice(i * T + tc * 512, i * T + tc * 512 + 512)
                    em.op("t", lambda t, sl=sl, ten=ten: t.matmul(
                        B.ps[0][:, :], rotp[:, :], ten[:, sl], start=True, stop=True),
                        reads=[key, ("rotp",)], writes=[("ps", 0)])
                    em.op("v", lambda v, sl=sl, ten=ten: v.tensor_tensor(
                        rsc1, ten[:, sl], cosb[:, :], OP.mult),
                        reads=[key, ("cosb",)], writes=[("xtile",)])
                    em.op("v", lambda v: v.tensor_tensor(
                        rsc2, B.ps[0][:, :], sinb[:, :], OP.mult),
                        reads=[("ps", 0), ("sinb",)], writes=[("xtile2",)])
                    em.op("v", lambda v, sl=sl, ten=ten: v.tensor_tensor(
                        ten[:, sl], rsc1, rsc2, OP.add),
                        reads=[("xtile",), ("xtile2",)], writes=[key])

        em.op("v", lambda v: v.memset(ropeb[:, :], 0.0), writes=[(tag, "ropeb")])
        sqs = B.xl[:, 0:512]
        for which, ten, key, col in (("q", qrT, (tag, "qrT"), 0), ("k", krT, (tag, "krT"), 4)):
            for i in range(4):
                for tc in range(NTC):
                    em.op("s", lambda s, ten=ten, i=i, tc=tc: s.activation(
                        sqs, ten[:, i * T + tc * 512:i * T + tc * 512 + 512], AF.Square),
                        reads=[key], writes=[("xl",)])
                    em.op("t", lambda t: t.matmul(B.ps[3][0:1, :], B.ones_t[:, 0:1],
                                                  sqs, start=True, stop=True),
                          reads=[("xl",), ("ones_t",)], writes=[("ps", 3)])
                    em.op("v", lambda v, col=col, i=i, tc=tc: v.reduce_max(
                        rsc2[0:1, 8 * i + tc + (256 if col else 0):8 * i + tc + 1 + (256 if col else 0)],
                        B.ps[3][0:1, :], axis=AX.X),
                        reads=[("ps", 3)], writes=[(tag, "rnm", col, i, tc)])
        for i in range(4):
            em.op("v", lambda v, i=i: v.reduce_max(ropeb[0:1, i:i + 1],
                                                   rsc2[0:1, 8 * i:8 * i + 8], axis=AX.X),
                  reads=[(tag, "rnm", 0, i, tc) for tc in range(NTC)],
                  writes=[(tag, "ropebq", i)])
            em.op("v", lambda v, i=i: v.reduce_max(ropeb[0:1, 4 + i:5 + i],
                                                   rsc2[0:1, 256 + 8 * i:256 + 8 * i + 8], axis=AX.X),
                  reads=[(tag, "rnm", 4, i, tc) for tc in range(NTC)],
                  writes=[(tag, "ropebk", i)])
            em.op("v", lambda v, i=i: v.tensor_tensor(ropeb[0:1, 8 + i:9 + i],
                                                      ropeb[0:1, i:i + 1],
                                                      ropeb[0:1, 4 + i:5 + i], OP.mult),
                  reads=[(tag, "ropebq", i), (tag, "ropebk", i)], writes=[(tag, "ropebp", i)])
            em.op("s", lambda s, i=i: s.activation(ropeb[0:1, 8 + i:9 + i],
                                                   ropeb[0:1, 8 + i:9 + i], AF.Sqrt),
                  reads=[(tag, "ropebp", i)], writes=[(tag, "ropebp", i)])
        em.op("t", lambda t: t.matmul(B.ps[3][:, 0:4], B.ones_f[0:1, 0:128],
                                      ropeb[0:1, 8:12], start=True, stop=True),
              reads=[(tag, "ropebp", i) for i in range(4)] + [("ones_f",)],
              writes=[("ps", 3)])
        em.op("v", lambda v: v.tensor_copy(ropeb[:, 8:12], B.ps[3][:, 0:4]),
              reads=[("ps", 3)], writes=[(tag, "ropebb")])

    em.barrier()

    A = alloc_attn(B, tag)
    A.ot_hbm = nc.dram_tensor(f"{tag}_othbm", [D, T], BF16).ap()
    wdec = ctx.enter_context(nc.sbuf_tensor(f"{tag}_wdec", [128, 4 * 128], BF16))
    wkdc = ctx.enter_context(nc.sbuf_tensor(f"{tag}_wkdc", [128, 2 * 128], BF16))
    wvdc = ctx.enter_context(nc.sbuf_tensor(f"{tag}_wvdc", [128, 2 * 128], BF16))
    wpr = ctx.enter_context(nc.sbuf_tensor(f"{tag}_wpr", [128, NC * D], BF16))
    otl = ctx.enter_context(nc.sbuf_tensor(f"{tag}_otl", [128, NC * 128], BF16))
    out_sb = ctx.enter_context(nc.sbuf_tensor(f"{tag}_out", [128, D], F32))
    xres = ctx.enter_context(nc.sbuf_tensor(f"{tag}_xres", [128, D], F32))
    em.dma("wg", "g", wpr[:, :].rearrange("k (c o) -> k c o", c=NC),
           W_o.rearrange("(c k) o -> k c o", c=NC), writes=[("wpr",)])

    def rope_rows(h):
        i, j = h // 4, h % 4
        return i, 32 * j

    for pair in range(8):
        em.dma("wg", "g", wdec[:, :].rearrange("k (c o) -> k c o", c=4),
               W_qdec[:, pair * 128:(pair + 1) * 128].rearrange("(c k) o -> k c o", c=4),
               writes=[("wdec",)])
        em.dma("wg", "g", wkdc[:, :].rearrange("k (c o) -> k c o", c=2),
               W_kdec[:, pair * 128:(pair + 1) * 128].rearrange("(c k) o -> k c o", c=2),
               writes=[("wkdc",)])
        em.dma("wg", "g", wvdc[:, :].rearrange("k (c o) -> k c o", c=2),
               W_vdec[:, pair * 128:(pair + 1) * 128].rearrange("(c k) o -> k c o", c=2),
               writes=[("wvdc",)])
        emit_proj_T(B, None, [(tag, "qcT")], wdec, ("wdec",), A.qT, (tag, "qT"),
                    nc_in=4, src_chunk=lambda c, t0, n: qcT[:, c * T + t0:c * T + t0 + n])
        emit_proj_T(B, None, [(tag, "kvT")], wkdc, ("wkdc",), A.kT, (tag, "kT"),
                    nc_in=2, src_chunk=lambda c, t0, n: kvT[:, c * T + t0:c * T + t0 + n])
        for t in range(NT):
            em.op("t", lambda tt, t=t: [
                tt.matmul(B.ps[1][:, 0:128], kvT[:, c * T + t * 128:c * T + t * 128 + 128],
                          wvdc[:, c * 128:(c + 1) * 128],
                          start=(c == 0), stop=(c == 1)) for c in range(2)][-1],
                reads=[("wvdc",), (tag, "kvT")], writes=[("ps", 1)])
            em.op("v", lambda v, t=t: v.tensor_copy(
                A.vton[:, t * 130:t * 130 + 130].rearrange("p (h e) -> p h e", h=2)[:, :, 0:64],
                B.ps[1][:, 0:128].rearrange("p (h e) -> p h e", h=2)),
                reads=[("ps", 1)], writes=[(tag, "vton")])
        emit_bounds(B, A, RSC, extra_col=(ropeb, (tag, "ropebb"), 8 + pair // 2))
        def s_extra(h, qc, ki, sbank, pair=pair):
            hh = pair * 2 + h
            i, r0 = rope_rows(hh)
            tp = {"tile_position": (r0, 0)} if r0 == 96 else {}
            em.op("t", lambda t, h=h, qc=qc, ki=ki, sbank=sbank, i=i, r0=r0, tp=tp: t.matmul(
                B.ps[sbank][:, :],
                krT[r0:r0 + 32, i * T + ki * 128:i * T + (ki + 1) * 128],
                qrT[r0:r0 + 32, i * T + qc * 512:i * T + (qc + 1) * 512],
                start=False, stop=True, **tp),
                reads=[(tag, "krT"), (tag, "qrT")], writes=[("ps", sbank)])
        emit_attn_core(B, A, pair, causal=True, use_exp=True, scale=RSC,
                       s_extra=s_extra)

    emit_outproj(B, A, tag, x_dram, xout_dram, wpr, None, otl, out_sb, xres,
                 with_bias=False)



L0F_VAR = set()


def emit_l0f(B, x_dram, w_gate, w1, w3, w2, xout_dram, tag="l0f"):
    VAR = L0F_VAR
    em, nc, ctx = B.em, B.nc, B.ctx
    E, H, NH = 8, 2048, 16
    xlt = ctx.enter_context(nc.sbuf_tensor(f"{tag}_xlt", [128, NC * T], BF16))
    y_hbm = nc.dram_tensor(f"{tag}_yhbm", [T, D], F32).ap()
    wg_sb = ctx.enter_context(nc.sbuf_tensor(f"{tag}_wg", [128, NC * E], BF16))
    w1sb = ctx.enter_context(nc.sbuf_tensor(f"{tag}_w1", [128, NC * H], BF16))
    w3sb = ctx.enter_context(nc.sbuf_tensor(f"{tag}_w3", [128, NC * H], BF16))
    w2sb = ctx.enter_context(nc.sbuf_tensor(f"{tag}_w2", [128, NH * D], BF16))
    hsb = ctx.enter_context(nc.sbuf_tensor(f"{tag}_h", [128, NH * 512], BF16))
    hsil = ctx.enter_context(nc.sbuf_tensor(f"{tag}_hsil", [128, 512], F32))
    comb = ctx.enter_context(nc.sbuf_tensor(f"{tag}_comb", [128, NT * E], F32))
    yscr = ctx.enter_context(nc.sbuf_tensor(f"{tag}_yscr", [128, D], F32))
    r_lall = ctx.enter_context(nc.sbuf_tensor(f"{tag}_rlall", [128, NT * E], F32))
    r_t1 = ctx.enter_context(nc.sbuf_tensor(f"{tag}_rt1", [128, NT * E], F32))
    r_m = ctx.enter_context(nc.sbuf_tensor(f"{tag}_rm", [128, 2 * NT], F32))
    xres = ctx.enter_context(nc.sbuf_tensor(f"{tag}_xres", [128, D], F32))
    out_sb = ctx.enter_context(nc.sbuf_tensor(f"{tag}_out", [128, D], F32))

    xlt_keys = emit_ln_xlt(B, x_dram, xlt, tag)

    em.dma("wg", "g", wg_sb[:, :].rearrange("k (c e) -> k c e", c=NC),
           w_gate.rearrange("(c k) e -> k c e", c=NC), writes=[("wgsb",)])
    for t in range(NT):
        em.op("t", lambda tt, t=t: [
            tt.matmul(B.ps[0][:, 0:E], xlt_chunk(xlt, c, t * 128, 128),
                      wg_sb[:, c * E:(c + 1) * E],
                      start=(c == 0), stop=(c == NC - 1)) for c in range(NC)][-1],
            reads=[("wgsb",)] + xlt_keys, writes=[("ps", 0)])
        em.op("v", lambda v, t=t: v.tensor_copy(r_lall[:, t * E:(t + 1) * E], B.ps[0][:, 0:E]),
              reads=[("ps", 0)], writes=[("r_lall",)])

    def v3(ten):
        return ten[:, :].rearrange("p (t e) -> p t e", e=E)

    def bc(ten):
        return bass.AP(ten, 0, [[2 * NT, 128], [1, NT], [0, E]])

    em.op("v", lambda v: v.tensor_reduce(r_m[:, 0:NT], v3(r_lall), axis=AX.X, op=OP.max),
          reads=[("r_lall",)], writes=[("r_m",)])
    em.op("v", lambda v: v.tensor_tensor(v3(r_t1), v3(r_lall), bc(r_m), OP.subtract),
          reads=[("r_lall",), ("r_m",)], writes=[("r_t1",)])
    em.op("s", lambda s: s.activation(r_lall[:, :], r_t1[:, :], AF.Exp),
          reads=[("r_t1",), ("r_lall",)], writes=[("r_lall",)])
    em.op("v", lambda v: v.tensor_scalar(comb[:, :], r_t1[:, :], 0.0, 1e30,
                                         OP.is_ge, OP.mult),
          reads=[("r_t1",)], writes=[("r_comb_t",)])
    em.op("v", lambda v: v.tensor_tensor(comb[:, :], r_t1[:, :], comb[:, :], OP.subtract),
          reads=[("r_t1",), ("r_comb_t",)], writes=[("r_comb_t",)])
    em.op("v", lambda v: v.tensor_reduce(r_m[:, NT:2 * NT], v3(comb), axis=AX.X, op=OP.max),
          reads=[("r_comb_t",)], writes=[("r_m2",)])
    em.op("v", lambda v: v.tensor_tensor(v3(comb), v3(r_t1),
                                         bass.AP(r_m, NT, [[2 * NT, 128], [1, NT], [0, E]]),
                                         OP.is_ge),
          reads=[("r_t1",), ("r_m2",)], writes=[("r_comb_t",)])
    em.op("v", lambda v: v.tensor_tensor(r_t1[:, :], r_lall[:, :], comb[:, :], OP.mult),
          reads=[("r_lall",), ("r_comb_t",)], writes=[("r_t1",)])
    em.op("v", lambda v: v.tensor_reduce(r_m[:, 0:NT], v3(r_t1), axis=AX.X, op=OP.add),
          reads=[("r_t1",)], writes=[("r_m",)])
    em.op("v", lambda v: v.reciprocal(r_m[:, NT:2 * NT], r_m[:, 0:NT]),
          reads=[("r_m",)], writes=[("r_m2",)])
    em.op("v", lambda v: v.tensor_tensor(v3(comb), v3(r_t1),
                                         bass.AP(r_m, NT, [[2 * NT, 128], [1, NT], [0, E]]),
                                         OP.mult),
          reads=[("r_t1",), ("r_m2",)],
          writes=[(tag, "comb", t) for t in range(NT)])

    comb_keys = [(tag, "comb", t) for t in range(NT)]

    ya_keys = []
    n_experts = 1 if "oneexp" in VAR else E
    for e in range(n_experts):
        em.dma("wg", "g", w1sb[:, :].rearrange("k (c h) -> k c h", c=NC),
               w1[e].rearrange("(c k) h -> k c h", c=NC), writes=[("w1sb",)])
        em.dma("wg", "g", w3sb[:, :].rearrange("k (c h) -> k c h", c=NC),
               w3[e].rearrange("(c k) h -> k c h", c=NC), writes=[("w3sb",)])
        em.dma("wg", "g", w2sb[:, :].rearrange("k (hc d) -> k hc d", hc=NH),
               w2[e].rearrange("(hc k) d -> k hc d", hc=NH), writes=[("w2sb",)])
        for tc in range(NTC):
            for hc in range(NH):
                em.op("t", lambda tt, tc=tc, hc=hc: [
                    tt.matmul(B.ps[4][:, :], w1sb[:, c * H + hc * 128: c * H + hc * 128 + 128],
                              xlt_chunk(xlt, c, tc * 512, 512),
                              start=(c == 0), stop=(c == NC - 1)) for c in range(NC)][-1],
                    reads=[("w1sb",)] + xlt_keys, writes=[("ps", 4)])
                em.op("t", lambda tt, tc=tc, hc=hc: [
                    tt.matmul(B.ps[5][:, :], w3sb[:, c * H + hc * 128: c * H + hc * 128 + 128],
                              xlt_chunk(xlt, c, tc * 512, 512),
                              start=(c == 0), stop=(c == NC - 1)) for c in range(NC)][-1],
                    reads=[("w3sb",)] + xlt_keys, writes=[("ps", 5)])
                em.op("s", lambda s: s.activation(hsil[:, :], B.ps[4][:, :],
                                                  AF.Copy if "nosilu" in VAR else AF.Silu),
                      reads=[("ps", 4)], writes=[("hsil",)])
                em.op("v", lambda v, hc=hc: v.tensor_tensor(
                    hsb[:, hc * 512:(hc + 1) * 512], hsil[:, :], B.ps[5][:, :], OP.mult),
                    reads=[("hsil",), ("ps", 5)], writes=[("hsb", hc)])
            for ti in range(4):
                t = tc * 4 + ti
                for half in range(2):
                    em.op("t", lambda tt, t=t, ti=ti, half=half: [
                        tt.matmul(B.ps[2 + half][:, :],
                                  hsb[:, hc * 512 + ti * 128: hc * 512 + ti * 128 + 128],
                                  w2sb[:, hc * D + half * 512: hc * D + half * 512 + 512],
                                  start=(hc == 0), stop=(hc == NH - 1)) for hc in range(NH)][-1],
                        reads=[("hsb", hc) for hc in range(NH)] + [("w2sb",)],
                        writes=[("ps", 2 + half)])
                    em.op("v", lambda v, t=t, e=e, half=half: v.tensor_scalar(
                        yscr[:, half * 512:half * 512 + 512], B.ps[2 + half][:, :],
                        comb[:, t * E + e:t * E + e + 1], None, OP.mult),
                        reads=[("ps", 2 + half), (tag, "comb", t)], writes=[("ysc", half)])
                kw = {} if (e == 0 or "noacc" in VAR) else {"accum_op": OP.add}
                s = em.dma("ya", "g", y_hbm[t * 128:(t + 1) * 128, :], yscr[:, :],
                           reads=[("ysc", 0), ("ysc", 1)],
                           writes=[(tag, "yhbm", t) if e == n_experts - 1 else (tag, "yhbm_p", t, e)],
                           **kw)
                if e == n_experts - 1:
                    ya_keys.append((tag, "yhbm", t))

    for t in range(NT):
        em.dma("xy", "y", xres[:, :], x_dram[t * 128:(t + 1) * 128, :], writes=[("xres",)])
        em.dma("yl", "y", yscr[:, :], y_hbm[t * 128:(t + 1) * 128, :],
               reads=[(tag, "yhbm", t), ("ysc", 0), ("ysc", 1)], writes=[("yscr_f",)])
        em.op("v", lambda v: v.tensor_tensor(out_sb[:, :], yscr[:, :], xres[:, :], OP.add),
              reads=[("yscr_f",), ("xres",)], writes=[("osb", 0)])
        em.dma("xo", "y", xout_dram[t * 128:(t + 1) * 128, :], out_sb[:, :],
               reads=[("osb", 0)])



def row_to_cols(B, em, dst, dst_key, row_sb, row_key, n):
    em.op("t", lambda t: [
        t.matmul(B.ps[1][:, j:j + 1], row_sb[0:1, j * 128:(j + 1) * 128],
                 B.ones_f[0:1, 0:1], start=(j == 0), stop=(j == n - 1)) for j in range(n)][-1],
        reads=[row_key, ("ones_f",)], writes=[("ps", 1)])
    em.op("v", lambda v: v.tensor_copy(dst[:, 0:n], B.ps[1][:, 0:n]),
          reads=[("ps", 1)], writes=[dst_key])


def emit_l3f(B, x_dram, gate_wT, fc1_wT, fc1_b, geglu, fc2_wT, b16,
             s_fc1_wT, s_fc1_b, s_geglu, s_fc2_wT, xout_dram, tag="l3f"):
    em, nc, ctx = B.em, B.nc, B.ctx
    E = 15
    xlt = ctx.enter_context(nc.sbuf_tensor(f"{tag}_xlt", [128, NC * T], BF16))
    y_hbm = nc.dram_tensor(f"{tag}_yhbm", [T, D], F32).ap()
    wg_sb = ctx.enter_context(nc.sbuf_tensor(f"{tag}_wg", [128, NC * E], BF16))
    comb = ctx.enter_context(nc.sbuf_tensor(f"{tag}_comb", [128, NT * 16], F32))
    r_lall = ctx.enter_context(nc.sbuf_tensor(f"{tag}_rlall", [128, NT * 15], F32))
    r_t1 = ctx.enter_context(nc.sbuf_tensor(f"{tag}_rt1", [128, NT * 15], F32))
    r_t2 = ctx.enter_context(nc.sbuf_tensor(f"{tag}_rt2", [128, NT * 15], F32))
    r_m = ctx.enter_context(nc.sbuf_tensor(f"{tag}_rm", [128, 2 * NT], F32))
    yscr = ctx.enter_context(nc.sbuf_tensor(f"{tag}_yscr", [128, D], F32))
    gel = ctx.enter_context(nc.sbuf_tensor(f"{tag}_gel", [128, 512], F32))
    fbrow = ctx.enter_context(nc.sbuf_tensor(f"{tag}_fbrow", [1, 4096], F32))
    fb = ctx.enter_context(nc.sbuf_tensor(f"{tag}_fb", [128, 32], F32))
    gg = ctx.enter_context(nc.sbuf_tensor(f"{tag}_gg", [128, 16], F32))
    b16sb = ctx.enter_context(nc.sbuf_tensor(f"{tag}_b16", [16, D], BF16))
    combt = ctx.enter_context(nc.sbuf_tensor(f"{tag}_combt", [16, 128], BF16))
    ytile = ctx.enter_context(nc.sbuf_tensor(f"{tag}_yt", [128, D], F32))
    xres = ctx.enter_context(nc.sbuf_tensor(f"{tag}_xres", [128, D], F32))
    out_sb = ctx.enter_context(nc.sbuf_tensor(f"{tag}_out", [128, D], F32))

    xlt_keys = emit_ln_xlt(B, x_dram, xlt, tag)

    em.dma("wg", "g", wg_sb[:, :].rearrange("k (c e) -> k c e", c=NC),
           gate_wT.rearrange("(c k) e -> k c e", c=NC), writes=[("wgsb",)])
    em.dma("wg", "g", b16sb[0:16, :], b16, writes=[("b16sb",)])
    em.op("v", lambda v: v.memset(
        comb[:, :].rearrange("k (t e) -> k t e", e=16)[:, :, 15:16], 1.0),
        writes=[(tag, "comb1",)])
    for t in range(NT):
        em.op("t", lambda tt, t=t: [
            tt.matmul(B.ps[0][:, 0:E], xlt_chunk(xlt, c, t * 128, 128),
                      wg_sb[:, c * E:(c + 1) * E],
                      start=(c == 0), stop=(c == NC - 1)) for c in range(NC)][-1],
            reads=[("wgsb",)] + xlt_keys, writes=[("ps", 0)])
        em.op("v", lambda v, t=t: v.tensor_copy(r_lall[:, t * E:(t + 1) * E], B.ps[0][:, 0:E]),
              reads=[("ps", 0)], writes=[("r_lall",)])

    def v3(ten):
        return ten[:, :].rearrange("p (t e) -> p t e", e=E)

    def bc(off):
        return bass.AP(r_m, off, [[2 * NT, 128], [1, NT], [0, E]])

    em.op("v", lambda v: v.tensor_reduce(r_m[:, 0:NT], v3(r_lall), axis=AX.X, op=OP.max),
          reads=[("r_lall",)], writes=[("r_m",)])
    em.op("v", lambda v: v.tensor_tensor(v3(r_t1), v3(r_lall), bc(0), OP.subtract),
          reads=[("r_lall",), ("r_m",)], writes=[("r_t1",)])
    em.op("s", lambda s: s.activation(r_lall[:, :], r_t1[:, :], AF.Exp),
          reads=[("r_t1",), ("r_lall",)], writes=[("r_lall",)])
    em.op("v", lambda v: v.tensor_reduce(r_m[:, 0:NT], v3(r_lall), axis=AX.X, op=OP.add),
          reads=[("r_lall",)], writes=[("r_m",)])
    em.op("v", lambda v: v.reciprocal(r_m[:, NT:2 * NT], r_m[:, 0:NT]),
          reads=[("r_m",)], writes=[("r_m2",)])
    em.op("v", lambda v: v.tensor_tensor(v3(r_t1), v3(r_lall), bc(NT), OP.mult),
          reads=[("r_lall",), ("r_m2",)], writes=[("r_t1",)])
    em.op("v", lambda v: v.tensor_reduce(r_m[:, 0:NT], v3(r_t1), axis=AX.X, op=OP.max),
          reads=[("r_t1",)], writes=[("r_m",)])
    em.op("v", lambda v: v.tensor_tensor(v3(r_lall), v3(r_t1), bc(0), OP.is_ge),
          reads=[("r_t1",), ("r_m",)], writes=[("r_lall",)])
    em.op("v", lambda v: v.tensor_tensor(r_lall[:, :], r_t1[:, :], r_lall[:, :], OP.subtract),
          reads=[("r_t1",), ("r_lall",)], writes=[("r_lall",)])
    em.op("v", lambda v: v.tensor_reduce(r_m[:, 0:NT], v3(r_lall), axis=AX.X, op=OP.max),
          reads=[("r_lall",)], writes=[("r_m",)])
    em.op("v", lambda v: v.tensor_tensor(v3(r_t2), v3(r_lall), bc(0), OP.is_ge),
          reads=[("r_lall",), ("r_m",)], writes=[("r_t2",)])
    em.op("v", lambda v: v.tensor_tensor(r_lall[:, :], r_lall[:, :], r_t2[:, :], OP.subtract),
          reads=[("r_lall",), ("r_t2",)], writes=[("r_lall",)])
    em.op("v", lambda v: v.tensor_reduce(r_m[:, 0:NT], v3(r_lall), axis=AX.X, op=OP.max),
          reads=[("r_lall",)], writes=[("r_m",)])
    em.op("v", lambda v: v.tensor_tensor(v3(r_lall), v3(r_t1), bc(0), OP.is_ge),
          reads=[("r_t1",), ("r_m",)], writes=[("r_lall",)])
    em.op("v", lambda v: v.tensor_tensor(r_t1[:, :], r_t1[:, :], r_lall[:, :], OP.mult),
          reads=[("r_t1",), ("r_lall",)], writes=[("r_t1",)])
    em.op("v", lambda v: v.tensor_reduce(r_m[:, 0:NT], v3(r_t1), axis=AX.X, op=OP.add),
          reads=[("r_t1",)], writes=[("r_m",)])
    em.op("v", lambda v: v.tensor_scalar(r_m[:, 0:NT], r_m[:, 0:NT], 1e-20, None, OP.add),
          reads=[("r_m",)], writes=[("r_m",)])
    em.op("v", lambda v: v.reciprocal(r_m[:, NT:2 * NT], r_m[:, 0:NT]),
          reads=[("r_m",)], writes=[("r_m2",)])
    em.op("v", lambda v: v.tensor_tensor(
        comb[:, :].rearrange("k (t g) -> k t g", g=16)[:, :, 0:E],
        v3(r_t1), bc(NT), OP.mult),
        reads=[("r_t1",), ("r_m2",)],
        writes=[(tag, "comb", t) for t in range(NT)])

    comb_keys = [(tag, "comb", t) for t in range(NT)] + [(tag, "comb1",)]

    with ExitStack() as ctx2:
        w1sb = ctx2.enter_context(nc.sbuf_tensor(f"{tag}_w1", [128, NC * 2048], BF16))
        w2sb = ctx2.enter_context(nc.sbuf_tensor(f"{tag}_w2", [128, NC * D], BF16))
        xh_sb = ctx2.enter_context(nc.sbuf_tensor(f"{tag}_xh", [128, 8 * 512], BF16))
        a_sb = ctx2.enter_context(nc.sbuf_tensor(f"{tag}_a", [128, 8 * 512], BF16))
        for e in range(E):
            em.dma("wg", "g", w1sb[:, :].rearrange("k (c h) -> k c h", c=NC),
                   fc1_wT[e].rearrange("(c k) h -> k c h", c=NC), writes=[("w1sb",)])
            em.dma("wg", "g", w2sb[:, :].rearrange("k (fc d) -> k fc d", fc=NC),
                   fc2_wT[e].rearrange("(fc k) d -> k fc d", fc=NC), writes=[("w2sb",)])
            em.dma("wg", "g", fbrow[0:1, 0:2048],
                   fc1_b[e].rearrange("(o p) -> o p", o=1), writes=[("fbrow",)])
            row_to_cols(B, em, fb, ("fb",), fbrow, ("fbrow",), 16)
            em.dma("wg", "g", fbrow[0:1, 0:1024],
                   geglu[e].rearrange("(o p) -> o p", o=1), writes=[("fbrow",)])
            row_to_cols(B, em, gg, ("gg",), fbrow, ("fbrow",), 8)
            for tc in range(NTC):
                for j in range(8):
                    em.op("t", lambda tt, tc=tc, j=j: [
                        tt.matmul(B.ps[4][:, :],
                                  w1sb[:, c * 2048 + j * 128: c * 2048 + j * 128 + 128],
                                  xlt_chunk(xlt, c, tc * 512, 512),
                                  start=(c == 0), stop=(c == NC - 1)) for c in range(NC)][-1],
                        reads=[("w1sb",)] + xlt_keys, writes=[("ps", 4)])
                    em.op("t", lambda tt, tc=tc, j=j: [
                        tt.matmul(B.ps[5][:, :],
                                  w1sb[:, c * 2048 + 1024 + j * 128: c * 2048 + 1024 + j * 128 + 128],
                                  xlt_chunk(xlt, c, tc * 512, 512),
                                  start=(c == 0), stop=(c == NC - 1)) for c in range(NC)][-1],
                        reads=[("w1sb",)] + xlt_keys, writes=[("ps", 5)])
                    em.op("v", lambda v, j=j: v.tensor_scalar(
                        xh_sb[:, j * 512:(j + 1) * 512], B.ps[4][:, :],
                        fb[:, j:j + 1], None, OP.add),
                        reads=[("ps", 4), ("fb",)], writes=[("xh", j)])
                    em.op("s", lambda s, j=j: s.activation(gel[:, :], B.ps[5][:, :], AF.Gelu,
                                                           bias=fb[:, 8 + j:9 + j], scale=1.0),
                          reads=[("ps", 5), ("fb",)], writes=[("gel",)])
                    em.op("v", lambda v, j=j: v.scalar_tensor_tensor(
                        a_sb[:, j * 512:(j + 1) * 512], gel[:, :], gg[:, j:j + 1],
                        xh_sb[:, j * 512:(j + 1) * 512], OP.mult, OP.mult),
                        reads=[("gel",), ("gg",), ("xh", j)], writes=[("a", j)])
                for ti in range(4):
                    t = tc * 4 + ti
                    for half in range(2):
                        em.op("t", lambda tt, ti=ti, half=half: [
                            tt.matmul(B.ps[2 + half][:, :],
                                      a_sb[:, j * 512 + ti * 128: j * 512 + ti * 128 + 128],
                                      w2sb[:, j * D + half * 512: j * D + half * 512 + 512],
                                      start=(j == 0), stop=(j == 7)) for j in range(8)][-1],
                            reads=[("a", j) for j in range(8)] + [("w2sb",)],
                            writes=[("ps", 2 + half)])
                        em.op("v", lambda v, t=t, e=e, half=half: v.tensor_scalar(
                            yscr[:, half * 512:half * 512 + 512], B.ps[2 + half][:, :],
                            comb[:, t * 16 + e:t * 16 + e + 1], None, OP.mult),
                            reads=[("ps", 2 + half), (tag, "comb", t)], writes=[("ysc", half)])
                    kw = {} if e == 0 else {"accum_op": OP.add}
                    em.dma("ya", "g", y_hbm[t * 128:(t + 1) * 128, :], yscr[:, :],
                           reads=[("ysc", 0), ("ysc", 1)],
                           writes=[(tag, "yhbm_p", t, e)], **kw)
    em.barrier()

    with ExitStack() as ctx2:
        s1sb = ctx2.enter_context(nc.sbuf_tensor(f"{tag}_s1", [128, NC * 2048], BF16))
        s2sb = ctx2.enter_context(nc.sbuf_tensor(f"{tag}_s2", [128, 8 * D], BF16))
        xh2 = ctx2.enter_context(nc.sbuf_tensor(f"{tag}_xh2", [128, 8 * 512], BF16))
        a2 = ctx2.enter_context(nc.sbuf_tensor(f"{tag}_a2", [128, 8 * 512], BF16))
        s1r = s_fc1_wT.rearrange("(c k) h -> k c h", c=NC)
        s2r = s_fc2_wT.rearrange("(fc k) d -> k fc d", fc=16)
        for fh in range(2):
            em.dma("wg", "g",
                   s1sb[:, :].rearrange("k (c h) -> k c h", c=NC)[:, :, 0:1024],
                   s1r[:, :, fh * 1024:fh * 1024 + 1024], writes=[("s1sb",)])
            em.dma("wg", "g",
                   s1sb[:, :].rearrange("k (c h) -> k c h", c=NC)[:, :, 1024:2048],
                   s1r[:, :, 2048 + fh * 1024:2048 + fh * 1024 + 1024], writes=[("s1sb",)])
            em.dma("wg", "g", s2sb[:, :].rearrange("k (fc d) -> k fc d", fc=8),
                   s2r[:, fh * 8:fh * 8 + 8, :], writes=[("s2sb",)])
            em.dma("wg", "g", fbrow[0:1, 0:1024],
                   s_fc1_b[fh * 1024:fh * 1024 + 1024].rearrange("(o p) -> o p", o=1),
                   writes=[("fbrow",)])
            em.dma("wg", "g", fbrow[0:1, 1024:2048],
                   s_fc1_b[2048 + fh * 1024:2048 + fh * 1024 + 1024].rearrange("(o p) -> o p", o=1),
                   writes=[("fbrow",)])
            row_to_cols(B, em, fb, ("fb",), fbrow, ("fbrow",), 16)
            em.dma("wg", "g", fbrow[0:1, 0:1024],
                   s_geglu[fh * 1024:fh * 1024 + 1024].rearrange("(o p) -> o p", o=1),
                   writes=[("fbrow",)])
            row_to_cols(B, em, gg, ("gg",), fbrow, ("fbrow",), 8)
            for tc in range(NTC):
                for j in range(8):
                    em.op("t", lambda tt, tc=tc, j=j: [
                        tt.matmul(B.ps[4][:, :],
                                  s1sb[:, c * 2048 + j * 128: c * 2048 + j * 128 + 128],
                                  xlt_chunk(xlt, c, tc * 512, 512),
                                  start=(c == 0), stop=(c == NC - 1)) for c in range(NC)][-1],
                        reads=[("s1sb",)] + xlt_keys, writes=[("ps", 4)])
                    em.op("t", lambda tt, tc=tc, j=j: [
                        tt.matmul(B.ps[5][:, :],
                                  s1sb[:, c * 2048 + 1024 + j * 128: c * 2048 + 1024 + j * 128 + 128],
                                  xlt_chunk(xlt, c, tc * 512, 512),
                                  start=(c == 0), stop=(c == NC - 1)) for c in range(NC)][-1],
                        reads=[("s1sb",)] + xlt_keys, writes=[("ps", 5)])
                    em.op("v", lambda v, j=j: v.tensor_scalar(
                        xh2[:, j * 512:(j + 1) * 512], B.ps[4][:, :],
                        fb[:, j:j + 1], None, OP.add),
                        reads=[("ps", 4), ("fb",)], writes=[("xh2", j)])
                    em.op("s", lambda s, j=j: s.activation(gel[:, :], B.ps[5][:, :], AF.Gelu,
                                                           bias=fb[:, 8 + j:9 + j], scale=1.0),
                          reads=[("ps", 5), ("fb",)], writes=[("gel",)])
                    em.op("v", lambda v, j=j: v.scalar_tensor_tensor(
                        a2[:, j * 512:(j + 1) * 512], gel[:, :], gg[:, j:j + 1],
                        xh2[:, j * 512:(j + 1) * 512], OP.mult, OP.mult),
                        reads=[("gel",), ("gg",), ("xh2", j)], writes=[("a2", j)])
                for ti in range(4):
                    t = tc * 4 + ti
                    for half in range(2):
                        em.op("t", lambda tt, ti=ti, half=half: [
                            tt.matmul(B.ps[2 + half][:, :],
                                      a2[:, j * 512 + ti * 128: j * 512 + ti * 128 + 128],
                                      s2sb[:, j * D + half * 512: j * D + half * 512 + 512],
                                      start=(j == 0), stop=(j == 7)) for j in range(8)][-1],
                            reads=[("a2", j) for j in range(8)] + [("s2sb",)],
                            writes=[("ps", 2 + half)])
                        em.op("v", lambda v, half=half: v.tensor_copy(
                            yscr[:, half * 512:half * 512 + 512], B.ps[2 + half][:, :]),
                            reads=[("ps", 2 + half)], writes=[("ysc", half)])
                    em.dma("ya", "g", y_hbm[t * 128:(t + 1) * 128, :], yscr[:, :],
                           reads=[("ysc", 0), ("ysc", 1)],
                           writes=[(tag, "yhbm", t) if fh == 1 else (tag, "yhbm_s", t)],
                           accum_op=OP.add)

    for t in range(NT):
        em.op("t", lambda tt, t=t: tt.matmul(B.ps[0][0:16, 0:128],
                                             comb[:, t * 16:(t + 1) * 16],
                                             B.iden_f32[:, :], start=True, stop=True),
              reads=[(tag, "comb", t), (tag, "comb1",), ("iden_f32",)], writes=[("ps", 0)])
        em.op("v", lambda v: v.tensor_copy(combt[0:16, :], B.ps[0][0:16, 0:128]),
              reads=[("ps", 0)], writes=[("combt",)])
        for half in range(2):
            em.op("t", lambda tt, half=half: tt.matmul(
                B.ps[2 + half][:, :], combt[0:16, :],
                b16sb[0:16, half * 512:half * 512 + 512], start=True, stop=True),
                reads=[("combt",), ("b16sb",)], writes=[("ps", 2 + half)])
        em.dma("xy", "y", xres[:, :], x_dram[t * 128:(t + 1) * 128, :], writes=[("xres",)])
        em.dma("yl", "y", ytile[:, :], y_hbm[t * 128:(t + 1) * 128, :],
               reads=[(tag, "yhbm", t)], writes=[("ytile",)])
        for half in range(2):
            em.op("v", lambda v, half=half: v.tensor_tensor(
                out_sb[:, half * 512:half * 512 + 512], B.ps[2 + half][:, :],
                ytile[:, half * 512:half * 512 + 512], OP.add),
                reads=[("ps", 2 + half), ("ytile",)], writes=[("osb", half)])
            em.op("v", lambda v, half=half: v.tensor_tensor(
                out_sb[:, half * 512:half * 512 + 512],
                out_sb[:, half * 512:half * 512 + 512],
                xres[:, half * 512:half * 512 + 512], OP.add),
                reads=[("osb", half), ("xres",)], writes=[("osb", half)])
        em.dma("xo", "y", xout_dram[t * 128:(t + 1) * 128, :], out_sb[:, :],
               reads=[("osb", 0), ("osb", 1)])



def emit_l2f(B, x_dram, W_expT, b_exp, up_w, up_b, down_w, W_con, b_con, down_b,
             xout_dram, tag="l2f"):
    em, nc, ctx = B.em, B.nc, B.ctx
    F = 4096
    MuT = ctx.enter_context(nc.sbuf_tensor(f"{tag}_MuT", [128, 32 * 128], BF16))
    MdT = ctx.enter_context(nc.sbuf_tensor(f"{tag}_MdT", [128, 32 * 128], BF16))
    GuT = ctx.enter_context(nc.sbuf_tensor(f"{tag}_GuT", [128, 32 * 128], BF16))
    GdT = ctx.enter_context(nc.sbuf_tensor(f"{tag}_GdT", [128, 32 * 128], BF16))
    b1sh = ctx.enter_context(nc.sbuf_tensor(f"{tag}_b1sh", [128, 32], F32))
    upbsh = ctx.enter_context(nc.sbuf_tensor(f"{tag}_upbsh", [128, 32], F32))

    gdram = nc.dram_tensor(f"{tag}_gdram", [F, 512], BF16).ap()
    zdram = nc.dram_tensor(f"{tag}_zdram", [F, 512], BF16).ap()
    h5dram = nc.dram_tensor(f"{tag}_h5dram", [F, T], BF16).ap()
    y_hbm = nc.dram_tensor(f"{tag}_yhbm", [T, D], F32).ap()
    b1dram = nc.dram_tensor(f"{tag}_b1d", [F], F32).ap()

    with ExitStack() as ctx2:
        masks = ctx2.enter_context(nc.sbuf_tensor(f"{tag}_masks", [128, 14 * 128], BF16))
        stg = ctx2.enter_context(nc.sbuf_tensor(f"{tag}_stg", [1, 3 * F], F32))
        wstage = ctx2.enter_context(nc.sbuf_tensor(f"{tag}_wst", [1, 8192], F32))
        vb = [ctx2.enter_context(nc.sbuf_tensor(f"{tag}_vb{i}", [128, 32 * 128], BF16))
              for i in range(2)]
        acc = ctx2.enter_context(nc.sbuf_tensor(f"{tag}_acc", [128, 128], BF16))
        tmp = ctx2.enter_context(nc.sbuf_tensor(f"{tag}_tmp", [128, 128], BF16))

        for mi in range(7):
            for sign in range(2):
                d = (2 ** mi) * (1 if sign == 0 else -1)
                em.op("v", lambda v, mi=mi, sign=sign, d=d: v.tensor_scalar(
                    masks[:, (2 * mi + sign) * 128:(2 * mi + sign + 1) * 128],
                    B.iota_fp[:, 0:128], float(d), None, OP.is_equal),
                    reads=[("iota_fp",)], writes=[(tag, "mask", mi, sign)])

        def build_chain(w_dram, big, MT, MT_key):
            nstages = 5 if big else 7
            for si in range(nstages):
                s_abs = (7 + si) if big else si
                em.dma("wg", "g", wstage[0:1, :],
                       w_dram[s_abs].rearrange("k a b -> (k a b)").rearrange("(o p) -> o p", o=1),
                       writes=[("wstage",)])
                em.op("v", lambda v: v.memset(stg[0:1, :], 0.0), writes=[("stg",)])
                if not big:
                    s2 = 2 ** si
                    q = 2048 // s2
                    for base, off in ((0, 0), (s2, 3), (F, 1), (2 * F + s2, 2)):
                        em.op("v", lambda v, s2=s2, q=q, off=off, base=base: v.tensor_copy(
                            bass.AP(stg, base, [[3 * F, 1], [2 * s2, q], [1, s2]]),
                            bass.AP(wstage, off, [[8192, 1], [4 * s2, q], [4, s2]])),
                            reads=[("wstage",)], writes=[("stg",)])
                    mi = si
                else:
                    sg = 2 ** si
                    nhi = 32 // (2 * sg)
                    for part, bit, off in ((0, 0, 0), (0, 1, 3), (F, 0, 1), (2 * F, 1, 2)):
                        for r in range(4):
                            em.op("v", lambda v, sg=sg, nhi=nhi, part=part, bit=bit,
                                  off=off, r=r: v.tensor_copy(
                                bass.AP(stg, part + 4 * sg * bit + r,
                                        [[3 * F, 1], [8 * sg, nhi], [4, sg], [128, 32]]),
                                bass.AP(wstage, off + 4 * r,
                                        [[8192, 1], [512 * sg, nhi], [512, sg], [16, 32]])),
                                reads=[("wstage",)], writes=[("stg",)])
                    mi = si + 2
                for b in range(32):
                    em.op("t", lambda t, b=b: t.matmul(
                        B.ps[0][:, 0:128], B.ones_f[0:1, 0:128], stg[0:1, b * 128:(b + 1) * 128],
                        start=True, stop=True),
                        reads=[("stg",), ("ones_f",)], writes=[("ps", 0)])
                    em.op("v", lambda v: v.tensor_tensor(acc[:, :], B.ps[0][:, 0:128],
                                                         B.ident[:, :], OP.mult),
                          reads=[("ps", 0), ("ident",)], writes=[("acc",)])
                    em.op("t", lambda t, b=b: t.matmul(
                        B.ps[0][:, 0:128], B.ones_f[0:1, 0:128],
                        stg[0:1, F + b * 128:F + (b + 1) * 128],
                        start=True, stop=True),
                        reads=[("stg",), ("ones_f",)], writes=[("ps", 0)])
                    em.op("v", lambda v, mi=mi: v.tensor_tensor(
                        tmp[:, :], B.ps[0][:, 0:128],
                        masks[:, (2 * mi + 1) * 128:(2 * mi + 2) * 128], OP.mult),
                        reads=[("ps", 0), (tag, "mask", mi, 1)], writes=[("tmp",)])
                    em.op("v", lambda v: v.tensor_tensor(acc[:, :], acc[:, :], tmp[:, :], OP.add),
                          reads=[("acc",), ("tmp",)], writes=[("acc",)])
                    em.op("t", lambda t, b=b: t.matmul(
                        B.ps[0][:, 0:128], B.ones_f[0:1, 0:128],
                        stg[0:1, 2 * F + b * 128:2 * F + (b + 1) * 128],
                        start=True, stop=True),
                        reads=[("stg",), ("ones_f",)], writes=[("ps", 0)])
                    em.op("v", lambda v, mi=mi: v.tensor_tensor(
                        tmp[:, :], B.ps[0][:, 0:128],
                        masks[:, (2 * mi) * 128:(2 * mi + 1) * 128], OP.mult),
                        reads=[("ps", 0), (tag, "mask", mi, 0)], writes=[("tmp",)])
                    em.op("v", lambda v: v.tensor_tensor(acc[:, :], acc[:, :], tmp[:, :], OP.add),
                          reads=[("acc",), ("tmp",)], writes=[("acc",)])
                    src = B.ident[:, :] if si == 0 else vb[(si + 1) % 2][:, b * 128:(b + 1) * 128]
                    src_key = ("ident",) if si == 0 else ("vb", (si + 1) % 2, b)
                    em.op("t", lambda t, src=src: t.matmul(
                        B.ps[1][:, 0:128], acc[:, :], src, start=True, stop=True),
                        reads=[("acc",), src_key], writes=[("ps", 1)])
                    em.op("v", lambda v, b=b, si=si: v.tensor_copy(
                        vb[si % 2][:, b * 128:(b + 1) * 128], B.ps[1][:, 0:128]),
                        reads=[("ps", 1)], writes=[("vb", si % 2, b)])
            last = (nstages - 1) % 2
            for b in range(32):
                em.op("t", lambda t, b=b, last=last: t.matmul(
                    B.ps[1][:, 0:128], vb[last][:, b * 128:(b + 1) * 128], B.ident[:, :],
                    start=True, stop=True),
                    reads=[("vb", last, b), ("ident",)], writes=[("ps", 1)])
                em.op("v", lambda v, b=b: v.tensor_copy(MT[:, b * 128:(b + 1) * 128],
                                                        B.ps[1][:, 0:128]),
                      reads=[("ps", 1)], writes=[MT_key])

        build_chain(up_w, False, MuT, (tag, "MuT"))
        build_chain(up_w, True, GuT, (tag, "GuT"))
        build_chain(down_w, False, MdT, (tag, "MdT"))
        build_chain(down_w, True, GdT, (tag, "GdT"))
    em.barrier()

    xlt = ctx.enter_context(nc.sbuf_tensor(f"{tag}_xlt", [128, NC * T], BF16))
    w1eff = ctx.enter_context(nc.sbuf_tensor(f"{tag}_w1eff", [128, NC * F], BF16))
    bb = ctx.enter_context(nc.sbuf_tensor(f"{tag}_bb", [128, D], F32))
    hstage = ctx.enter_context(nc.sbuf_tensor(f"{tag}_hst", [128, 512], BF16))
    gstage = ctx.enter_context(nc.sbuf_tensor(f"{tag}_gst", [128, 512], BF16))
    zstage = ctx.enter_context(nc.sbuf_tensor(f"{tag}_zst", [128, 512], BF16))
    gnorm = ctx.enter_context(nc.sbuf_tensor(f"{tag}_gn", [128, 512], BF16))
    zshuf = ctx.enter_context(nc.sbuf_tensor(f"{tag}_zs", [128, 512], BF16))
    h5l = ctx.enter_context(nc.sbuf_tensor(f"{tag}_h5l", [128, 4 * 128], BF16))
    wcs = ctx.enter_context(nc.sbuf_tensor(f"{tag}_wcs", [128, 4 * D], BF16))
    yscr = ctx.enter_context(nc.sbuf_tensor(f"{tag}_yscr", [128, D], F32))
    xres = ctx.enter_context(nc.sbuf_tensor(f"{tag}_xres", [128, D], F32))
    out_sb = ctx.enter_context(nc.sbuf_tensor(f"{tag}_out", [128, D], F32))
    brow = xres[0:1, :]
    brow2 = yscr[0:1, :]

    xlt_keys = emit_ln_xlt(B, x_dram, xlt, tag)

    with ExitStack() as ctx3:
        bcol = ctx3.enter_context(nc.sbuf_tensor(f"{tag}_bcol", [128, 32], BF16))
        bce = ctx3.enter_context(nc.sbuf_tensor(f"{tag}_bce", [128, 128], F32))
        wexpb = ctx3.enter_context(nc.sbuf_tensor(f"{tag}_wex", [128, D], BF16))
        for b in range(32):
            em.dma("wg", "g", wexpb[:, :],
                   W_expT[b * 128:(b + 1) * 128, :], writes=[("wexpb",)])
            for c in range(NC):
                em.op("t", lambda t, b=b, c=c: t.matmul(
                    B.ps[2][:, 0:128], wexpb[:, c * 128:(c + 1) * 128],
                    MuT[:, b * 128:(b + 1) * 128], start=True, stop=True),
                    reads=[("wexpb",), (tag, "MuT")], writes=[("ps", 2)])
                em.op("v", lambda v, b=b, c=c: v.tensor_copy(
                    bass.AP(w1eff, c * F + 4 * b, [[NC * F, 128], [128, 32], [1, 4]]),
                    B.ps[2][:, 0:128].rearrange("k (g r) -> k g r", r=4)),
                    reads=[("ps", 2)], writes=[("w1eff",)])
        for b in range(32):
            em.dma("wg", "g", bcol[:, b:b + 1],
                   b_exp[b * 128:(b + 1) * 128].rearrange("(p o) -> p o", o=1),
                   writes=[("bcol",)])
        em.op("t", lambda t: [
            t.matmul(B.ps[2][:, b:b + 1], MuT[:, b * 128:(b + 1) * 128], bcol[:, b:b + 1],
                     start=(b == 0), stop=(b == 31)) for b in range(32)][-1],
            reads=[(tag, "MuT"), ("bcol",)], writes=[("ps", 2)])
        em.op("v", lambda v: v.tensor_copy(bce[:, 0:32], B.ps[2][:, 0:32]),
              reads=[("ps", 2)], writes=[("bce",)])
        em.op("t", lambda t: t.matmul(B.ps[3][0:32, 0:128], bce[:, 0:32], B.iden_f32[:, :],
                                      start=True, stop=True),
              reads=[("bce",), ("iden_f32",)], writes=[("ps", 3)])
        em.op("v", lambda v: v.tensor_copy(bce[0:32, 0:128], B.ps[3][0:32, 0:128]),
              reads=[("ps", 3)], writes=[("bceT",)])
        em.dma("bf", "y", b1dram.rearrange("(b k) -> b k", k=128), bce[0:32, 0:128],
               reads=[("bceT",)], writes=[(tag, "b1d")])
        for g in range(32):
            em.dma("bf", "y", b1sh[:, g:g + 1],
                   bass.AP(b1dram.tensor, 4 * g, [[128, 32], [1, 4]]),
                   reads=[(tag, "b1d")], writes=[(tag, "b1sh")])
            em.dma("wg", "g", upbsh[:, g:g + 1],
                   bass.AP(up_b.tensor, 4 * g, [[128, 32], [1, 4]]),
                   writes=[(tag, "upbsh")])

    em.dma("wg", "g", brow[0:1, 0:1024], b_con.rearrange("(o p) -> o p", o=1),
           writes=[("brow",)])
    em.dma("wg", "g", brow2[0:1, 0:1024], down_b.rearrange("(o p) -> o p", o=1),
           writes=[("brow2",)])
    em.op("v", lambda v: v.tensor_tensor(brow[0:1, :], brow[0:1, :], brow2[0:1, :], OP.add),
          reads=[("brow",), ("brow2",)], writes=[("brow",)])
    for half in range(2):
        em.op("t", lambda t, half=half: t.matmul(
            B.ps[3][:, :], B.ones_f[0:1, 0:128], brow[0:1, half * 512:half * 512 + 512],
            start=True, stop=True),
            reads=[("brow",), ("ones_f",)], writes=[("ps", 3)])
        em.op("v", lambda v, half=half: v.tensor_copy(bb[:, half * 512:half * 512 + 512],
                                                      B.ps[3][:, :]),
              reads=[("ps", 3)], writes=[("bb", half)])

    for tc in range(NTC):
        for g in range(32):
            em.op("t", lambda t, tc=tc, g=g: [
                t.matmul(B.ps[4][:, :], w1eff[:, c * F + g * 128: c * F + (g + 1) * 128],
                         xlt_chunk(xlt, c, tc * 512, 512),
                         start=(c == 0), stop=(c == NC - 1)) for c in range(NC)][-1],
                reads=[("w1eff",)] + xlt_keys, writes=[("ps", 4)])
            em.op("v", lambda v, g=g: v.tensor_scalar(hstage[:, :], B.ps[4][:, :],
                                                      b1sh[:, g:g + 1], None, OP.add),
                  reads=[("ps", 4), (tag, "b1sh")], writes=[("hstage",)])
            em.op("t", lambda t, g=g: t.matmul(B.ps[5][:, :], GuT[:, g * 128:(g + 1) * 128],
                                               hstage[:, :], start=True, stop=True),
                  reads=[("hstage",), (tag, "GuT")], writes=[("ps", 5)])
            em.op("s", lambda s, g=g: s.activation(gstage[:, :], B.ps[5][:, :], AF.Gelu,
                                                   bias=upbsh[:, g:g + 1], scale=1.0),
                  reads=[("ps", 5), (tag, "upbsh")], writes=[("gstage",)])
            em.dma("bf", "y",
                   bass.AP(gdram.tensor, 4 * g * 512, [[128 * 512, 32], [512, 4], [1, 512]]),
                   gstage[:, :], reads=[("gstage",)], writes=[(tag, "gd", g)])
        for b in range(32):
            em.dma("bf2", "s", gnorm[:, :], gdram[b * 128:(b + 1) * 128, :],
                   reads=[(tag, "gd", gg2) for gg2 in range(32)], writes=[("gnorm",)])
            em.op("t", lambda t, b=b: t.matmul(B.ps[2][:, :], MdT[:, b * 128:(b + 1) * 128],
                                               gnorm[:, :], start=True, stop=True),
                  reads=[("gnorm",), (tag, "MdT")], writes=[("ps", 2)])
            em.op("v", lambda v: v.tensor_copy(zstage[:, :], B.ps[2][:, :]),
                  reads=[("ps", 2)], writes=[("zstage",)])
            em.dma("bf", "y", zdram[b * 128:(b + 1) * 128, :], zstage[:, :],
                   reads=[("zstage",)], writes=[(tag, "zd", b)])
        for g in range(32):
            em.dma("bf2", "s", zshuf[:, :],
                   bass.AP(zdram.tensor, 4 * g * 512, [[128 * 512, 32], [512, 4], [1, 512]]),
                   reads=[(tag, "zd", bb2) for bb2 in range(32)], writes=[("zshuf",)])
            em.op("t", lambda t, g=g: t.matmul(B.ps[3][:, :], GdT[:, g * 128:(g + 1) * 128],
                                               zshuf[:, :], start=True, stop=True),
                  reads=[("zshuf",), (tag, "GdT")], writes=[("ps", 3)])
            em.op("v", lambda v: v.tensor_copy(zstage[:, :], B.ps[3][:, :]),
                  reads=[("ps", 3)], writes=[("zstage",)])
            em.dma("bf", "y", h5dram[g * 128:(g + 1) * 128, tc * 512:(tc + 1) * 512],
                   zstage[:, :], reads=[("zstage",)], writes=[(tag, "h5", tc, g)])

    h5keys = [(tag, "h5", tc, g) for tc in range(NTC) for g in range(32)]
    for bat in range(8):
        for gi in range(4):
            g = bat * 4 + gi
            em.dma("wg", "g", wcs[:, gi * D:(gi + 1) * D],
                   bass.AP(W_con.tensor, 4 * g * D, [[128 * D, 32], [D, 4], [1, D]]),
                   writes=[("wcs", gi)])
        for t in range(NT):
            em.dma("oc", "y", h5l[:, :],
                   bass.AP(h5dram.tensor, bat * 4 * 128 * T + t * 128,
                           [[T, 128], [128 * T, 4], [1, 128]]),
                   reads=h5keys, writes=[("h5l",)])
            for half in range(2):
                em.op("t", lambda tt, half=half: [
                    tt.matmul(B.ps[2 + half][:, :], h5l[:, gi * 128:(gi + 1) * 128],
                              wcs[:, gi * D + half * 512: gi * D + half * 512 + 512],
                              start=(gi == 0), stop=(gi == 3)) for gi in range(4)][-1],
                    reads=[("h5l",)] + [("wcs", gi) for gi in range(4)],
                    writes=[("ps", 2 + half)])
                em.op("v", lambda v, half=half: v.tensor_copy(
                    yscr[:, half * 512:half * 512 + 512], B.ps[2 + half][:, :]),
                    reads=[("ps", 2 + half)], writes=[("ysc", half)])
            kw = {} if bat == 0 else {"accum_op": OP.add}
            em.dma("ya", "g", y_hbm[t * 128:(t + 1) * 128, :], yscr[:, :],
                   reads=[("ysc", 0), ("ysc", 1)],
                   writes=[(tag, "yhbm", t) if bat == 7 else (tag, "yhbm_p", t, bat)], **kw)

    for t in range(NT):
        em.dma("xy", "y", xres[:, :], x_dram[t * 128:(t + 1) * 128, :], writes=[("xres",)])
        em.dma("yl", "y", yscr[:, :], y_hbm[t * 128:(t + 1) * 128, :],
               reads=[(tag, "yhbm", t), ("ysc", 0), ("ysc", 1)], writes=[("yscr_f",)])
        for half in range(2):
            em.op("v", lambda v, half=half: v.tensor_tensor(
                out_sb[:, half * 512:half * 512 + 512], yscr[:, half * 512:half * 512 + 512],
                bb[:, half * 512:half * 512 + 512], OP.add),
                reads=[("yscr_f",), ("bb", half)], writes=[("osb", half)])
            em.op("v", lambda v, half=half: v.tensor_tensor(
                out_sb[:, half * 512:half * 512 + 512],
                out_sb[:, half * 512:half * 512 + 512],
                xres[:, half * 512:half * 512 + 512], OP.add),
                reads=[("osb", half), ("xres",)], writes=[("osb", half)])
        em.dma("xo", "y", xout_dram[t * 128:(t + 1) * 128, :], out_sb[:, :],
               reads=[("osb", 0), ("osb", 1)])



def build_l2a_program():
    nc = bass.Bass("TRN2", target_bir_lowering=False)
    x = nc.dram_tensor("x", [T, D], F32, kind="ExternalInput")
    names = ["wqT", "bq", "wkT", "bk", "wvT", "bv", "woT", "bo"]
    hs = []
    for n in names:
        if n.startswith("w"):
            hs.append(nc.dram_tensor(n, [D, D], F32, kind="ExternalInput"))
        else:
            hs.append(nc.dram_tensor(n, [D], F32, kind="ExternalInput"))
    xout = nc.dram_tensor("xout", [T, D], F32, kind="ExternalOutput")
    with ExitStack() as ctx:
        em = Em(nc, ctx)
        block = ctx.enter_context(nc.Block())
        B = alloc_common(nc, ctx, em)
        emit_l2a(B, x.ap(), *[h.ap() for h in hs], xout.ap())
        em.finalize(block)
    return nc


def build_l3a_program():
    nc = bass.Bass("TRN2", target_bir_lowering=False)
    x = nc.dram_tensor("x", [T, D], F32, kind="ExternalInput")
    wqkvT = nc.dram_tensor("wqkvT", [D, 3 * D], F32, kind="ExternalInput")
    bqkv = nc.dram_tensor("bqkv", [3 * D], F32, kind="ExternalInput")
    woutT = nc.dram_tensor("woutT", [D, D], F32, kind="ExternalInput")
    bout = nc.dram_tensor("bout", [D], F32, kind="ExternalInput")
    xout = nc.dram_tensor("xout", [T, D], F32, kind="ExternalOutput")
    with ExitStack() as ctx:
        em = Em(nc, ctx)
        block = ctx.enter_context(nc.Block())
        B = alloc_common(nc, ctx, em)
        emit_l3a(B, x.ap(), wqkvT.ap(), bqkv.ap(), woutT.ap(), bout.ap(), xout.ap())
        em.finalize(block)
    return nc


def build_l2f_program():
    nc = bass.Bass("TRN2", target_bir_lowering=False)
    x = nc.dram_tensor("x", [T, D], F32, kind="ExternalInput")
    WexpT = nc.dram_tensor("WexpT", [4096, D], F32, kind="ExternalInput")
    b_exp = nc.dram_tensor("b_exp", [4096], F32, kind="ExternalInput")
    up_w = nc.dram_tensor("up_w", [12, 2048, 2, 2], F32, kind="ExternalInput")
    up_b = nc.dram_tensor("up_b", [4096], F32, kind="ExternalInput")
    down_w = nc.dram_tensor("down_w", [12, 2048, 2, 2], F32, kind="ExternalInput")
    W_con = nc.dram_tensor("W_con", [4096, D], F32, kind="ExternalInput")
    b_con = nc.dram_tensor("b_con", [D], F32, kind="ExternalInput")
    down_b = nc.dram_tensor("down_b", [D], F32, kind="ExternalInput")
    xout = nc.dram_tensor("xout", [T, D], F32, kind="ExternalOutput")
    with ExitStack() as ctx:
        em = Em(nc, ctx)
        block = ctx.enter_context(nc.Block())
        B = alloc_common(nc, ctx, em)
        emit_l2f(B, x.ap(), WexpT.ap(), b_exp.ap(), up_w.ap(), up_b.ap(),
                 down_w.ap(), W_con.ap(), b_con.ap(), down_b.ap(), xout.ap())
        em.finalize(block)
    return nc


def rope_tables_np():
    dim = 32
    inv_freq = 1.0 / 10000.0 ** (np.arange(0, dim, 2, dtype=np.float32) / dim)
    tt = np.arange(T, dtype=np.float32)
    freqs = np.outer(tt, inv_freq)
    emb = np.concatenate([freqs, freqs], -1)
    cos = np.cos(emb).astype(np.float32)
    sin = np.sin(emb).astype(np.float32)
    cos4 = np.ascontiguousarray(np.tile(cos.T, (4, 1)))
    sin4 = np.ascontiguousarray(np.tile(sin.T, (4, 1)))
    RT = np.zeros((128, 128), dtype=np.float32)
    for b in range(0, 128, 32):
        for j in range(16):
            RT[b + j + 16, b + j] = -1.0
            RT[b + j, b + j + 16] = 1.0
    return cos4, sin4, RT


def build_l1a_program():
    nc = bass.Bass("TRN2", target_bir_lowering=False)
    x = nc.dram_tensor("x", [T, D], F32, kind="ExternalInput")
    names = [("W_kv", [D, 256]), ("g_kv", [256]), ("W_kdec", [256, D]),
             ("W_vdec", [256, D]), ("W_q", [D, 512]), ("W_qdec", [512, D]),
             ("W_krope", [D, 512]), ("W_qrope", [512, 512]), ("W_o", [D, D]),
             ("cos4", [128, T]), ("sin4", [128, T]), ("rot_permT", [128, 128])]
    hs = {n: nc.dram_tensor(n, sh, F32, kind="ExternalInput") for n, sh in names}
    xout = nc.dram_tensor("xout", [T, D], F32, kind="ExternalOutput")
    with ExitStack() as ctx:
        em = Em(nc, ctx)
        block = ctx.enter_context(nc.Block())
        B = alloc_common(nc, ctx, em)
        emit_l1a(B, x.ap(), *[hs[n].ap() for n, _ in names], xout.ap())
        em.finalize(block)
    return nc


def build_l1f_program():
    nc = bass.Bass("TRN2", target_bir_lowering=False)
    x = nc.dram_tensor("x", [T, D], F32, kind="ExternalInput")
    wT = nc.dram_tensor("wT", [D, D], F32, kind="ExternalInput")
    xout = nc.dram_tensor("xout", [T, D], F32, kind="ExternalOutput")
    with ExitStack() as ctx:
        em = Em(nc, ctx)
        block = ctx.enter_context(nc.Block())
        B = alloc_common(nc, ctx, em)
        emit_l1f(B, x.ap(), wT.ap(), xout.ap())
        em.finalize(block)
    return nc


def build_l0a_program(dbg=False):
    nc = bass.Bass("TRN2", target_bir_lowering=False)
    x = nc.dram_tensor("x", [T, D], F32, kind="ExternalInput")
    wqkvT = nc.dram_tensor("wqkvT", [D, 3 * D], F32, kind="ExternalInput")
    bqkv = nc.dram_tensor("bqkv", [3 * D], F32, kind="ExternalInput")
    wprojT = nc.dram_tensor("wprojT", [D, D], F32, kind="ExternalInput")
    bproj = nc.dram_tensor("bproj", [D], F32, kind="ExternalInput")
    xout = nc.dram_tensor("xout", [T, D], F32, kind="ExternalOutput")
    with ExitStack() as ctx:
        em = Em(nc, ctx)
        block = ctx.enter_context(nc.Block())
        B = alloc_common(nc, ctx, em)
        emit_l0a(B, x.ap(), wqkvT.ap(), bqkv.ap(), wprojT.ap(), bproj.ap(), xout.ap(), dbg=dbg)
        em.finalize(block)
    return nc


def build_l0f_program():
    nc = bass.Bass("TRN2", target_bir_lowering=False)
    x = nc.dram_tensor("x", [T, D], F32, kind="ExternalInput")
    wg = nc.dram_tensor("wg", [D, 8], F32, kind="ExternalInput")
    w1 = nc.dram_tensor("w1", [8, D, 2048], F32, kind="ExternalInput")
    w3 = nc.dram_tensor("w3", [8, D, 2048], F32, kind="ExternalInput")
    w2 = nc.dram_tensor("w2", [8, 2048, D], F32, kind="ExternalInput")
    xout = nc.dram_tensor("xout", [T, D], F32, kind="ExternalOutput")
    with ExitStack() as ctx:
        em = Em(nc, ctx)
        block = ctx.enter_context(nc.Block())
        B = alloc_common(nc, ctx, em)
        emit_l0f(B, x.ap(), wg.ap(), [w1.ap()[e] for e in range(8)],
                 [w3.ap()[e] for e in range(8)], [w2.ap()[e] for e in range(8)], xout.ap())
        em.finalize(block)
    return nc


def build_l3f_program():
    nc = bass.Bass("TRN2", target_bir_lowering=False)
    x = nc.dram_tensor("x", [T, D], F32, kind="ExternalInput")
    gwT = nc.dram_tensor("gwT", [D, 15], F32, kind="ExternalInput")
    fc1T = nc.dram_tensor("fc1T", [15, D, 2048], F32, kind="ExternalInput")
    fc1b = nc.dram_tensor("fc1b", [15, 2048], F32, kind="ExternalInput")
    gg = nc.dram_tensor("gg", [15, D], F32, kind="ExternalInput")
    fc2T = nc.dram_tensor("fc2T", [15, D, D], F32, kind="ExternalInput")
    b16 = nc.dram_tensor("b16", [16, D], F32, kind="ExternalInput")
    s1T = nc.dram_tensor("s1T", [D, 4096], F32, kind="ExternalInput")
    s1b = nc.dram_tensor("s1b", [4096], F32, kind="ExternalInput")
    sgg = nc.dram_tensor("sgg", [2048], F32, kind="ExternalInput")
    s2T = nc.dram_tensor("s2T", [2048, D], F32, kind="ExternalInput")
    xout = nc.dram_tensor("xout", [T, D], F32, kind="ExternalOutput")
    with ExitStack() as ctx:
        em = Em(nc, ctx)
        block = ctx.enter_context(nc.Block())
        B = alloc_common(nc, ctx, em)
        emit_l3f(B, x.ap(), gwT.ap(), [fc1T.ap()[e] for e in range(15)], fc1b.ap(),
                 gg.ap(), [fc2T.ap()[e] for e in range(15)], b16.ap(),
                 s1T.ap(), s1b.ap(), sgg.ap(), s2T.ap(), xout.ap())
        em.finalize(block)
    return nc



def _prep_stages(inputs):
    i = {k: np.ascontiguousarray(v) for k, v in inputs.items()}
    ct = np.ascontiguousarray
    cos4, sin4, rotT = rope_tables_np()
    stages = [
        ("l0a", build_l0a_program,
         {"wqkvT": ct(i["L0_attn_c_attn_kernel"].T), "bqkv": i["L0_attn_c_attn_bias"],
          "wprojT": ct(i["L0_attn_c_proj_kernel"].T), "bproj": i["L0_attn_c_proj_bias"]}),
        ("l0f", build_l0f_program,
         {"wg": i["L0_ffn_w_gate"], "w1": i["L0_ffn_w1"], "w3": i["L0_ffn_w3"],
          "w2": i["L0_ffn_w2"]}),
        ("l1a", build_l1a_program,
         {"W_kv": i["L1_attn_W_kv"], "g_kv": i["L1_attn_g_kv"],
          "W_kdec": i["L1_attn_W_kdec"], "W_vdec": i["L1_attn_W_vdec"],
          "W_q": i["L1_attn_W_q"], "W_qdec": i["L1_attn_W_qdec"],
          "W_krope": i["L1_attn_W_krope"], "W_qrope": i["L1_attn_W_qrope"],
          "W_o": i["L1_attn_W_o"], "cos4": cos4, "sin4": sin4, "rot_permT": rotT}),
        ("l1f", build_l1f_program, {"wT": ct(i["L1_ffn_weight"].T)}),
        ("l2a", build_l2a_program,
         {"wqT": ct(i["L2_attn_Wq"].T), "bq": i["L2_attn_bq"],
          "wkT": ct(i["L2_attn_Wk"].T), "bk": i["L2_attn_bk"],
          "wvT": ct(i["L2_attn_Wv"].T), "bv": i["L2_attn_bv"],
          "woT": ct(i["L2_attn_Wo"].T), "bo": i["L2_attn_bo"]}),
        ("l2f", build_l2f_program,
         {"WexpT": ct(i["L2_ffn_W_exp"].T), "b_exp": i["L2_ffn_b_exp"],
          "up_w": i["L2_ffn_up_weights"], "up_b": i["L2_ffn_up_bias"],
          "down_w": i["L2_ffn_down_weights"], "W_con": i["L2_ffn_W_con"],
          "b_con": i["L2_ffn_b_con"], "down_b": i["L2_ffn_down_bias"]}),
        ("l3a", build_l3a_program,
         {"wqkvT": ct(i["L3_attn_W_qkv"].T), "bqkv": i["L3_attn_b_qkv"],
          "woutT": ct(i["L3_attn_W_out"].T), "bout": i["L3_attn_b_out"]}),
        ("l3f", build_l3f_program,
         {"gwT": ct(i["L3_ffn_gate_w"].T), "fc1T": ct(i["L3_ffn_fc1_w"].transpose(0, 2, 1)),
          "fc1b": i["L3_ffn_fc1_b"], "gg": i["L3_ffn_geglu_mult"],
          "fc2T": ct(i["L3_ffn_fc2_w"].transpose(0, 2, 1)),
          "b16": ct(np.concatenate([i["L3_ffn_fc2_b"], i["L3_ffn_s_fc2_b"][None]], 0)),
          "s1T": ct(i["L3_ffn_s_fc1_w"].T), "s1b": i["L3_ffn_s_fc1_b"],
          "sgg": i["L3_ffn_s_geglu_mult"], "s2T": ct(i["L3_ffn_s_fc2_w"].T)}),
    ]
    return stages


def build_fused_program(dbg=False):
    nc = bass.Bass("TRN2", target_bir_lowering=False)
    x_in = nc.dram_tensor("x", [T, D], F32, kind="ExternalInput")
    xout = nc.dram_tensor("xout", [T, D], F32, kind="ExternalOutput")
    kindmid = {"kind": "ExternalOutput"} if dbg else {}
    xmid = [nc.dram_tensor(f"xmid{i}", [T, D], F32, **kindmid) for i in range(7)]
    xs = [x_in.ap()] + [t.ap() for t in xmid] + [xout.ap()]

    def din(stage, name, shape):
        return nc.dram_tensor(f"in_{stage}_{name}", shape, F32, kind="ExternalInput").ap()

    with ExitStack() as ctx:
        em = Em(nc, ctx)
        block = ctx.enter_context(nc.Block())
        B = alloc_common(nc, ctx, em)
        gctx = B.ctx

        def stage(i, fn):
            with ExitStack() as sctx:
                B.ctx = sctx
                fn(xs[i], xs[i + 1])
                B.ctx = gctx
            em.barrier()

        stage(0, lambda xi, xo: emit_l0a(
            B, xi, din("l0a", "wqkvT", [D, 3 * D]), din("l0a", "bqkv", [3 * D]),
            din("l0a", "wprojT", [D, D]), din("l0a", "bproj", [D]), xo))
        w1 = din("l0f", "w1", [8, D, 2048])
        w3 = din("l0f", "w3", [8, D, 2048])
        w2 = din("l0f", "w2", [8, 2048, D])
        stage(1, lambda xi, xo: emit_l0f(
            B, xi, din("l0f", "wg", [D, 8]), [w1[e] for e in range(8)],
            [w3[e] for e in range(8)], [w2[e] for e in range(8)], xo))
        stage(2, lambda xi, xo: emit_l1a(
            B, xi, din("l1a", "W_kv", [D, 256]), din("l1a", "g_kv", [256]),
            din("l1a", "W_kdec", [256, D]), din("l1a", "W_vdec", [256, D]),
            din("l1a", "W_q", [D, 512]), din("l1a", "W_qdec", [512, D]),
            din("l1a", "W_krope", [D, 512]), din("l1a", "W_qrope", [512, 512]),
            din("l1a", "W_o", [D, D]), din("l1a", "cos4", [128, T]),
            din("l1a", "sin4", [128, T]), din("l1a", "rot_permT", [128, 128]), xo))
        stage(3, lambda xi, xo: emit_l1f(B, xi, din("l1f", "wT", [D, D]), xo))
        stage(4, lambda xi, xo: emit_l2a(
            B, xi, din("l2a", "wqT", [D, D]), din("l2a", "bq", [D]),
            din("l2a", "wkT", [D, D]), din("l2a", "bk", [D]),
            din("l2a", "wvT", [D, D]), din("l2a", "bv", [D]),
            din("l2a", "woT", [D, D]), din("l2a", "bo", [D]), xo))
        stage(5, lambda xi, xo: emit_l2f(
            B, xi, din("l2f", "WexpT", [4096, D]), din("l2f", "b_exp", [4096]),
            din("l2f", "up_w", [12, 2048, 2, 2]), din("l2f", "up_b", [4096]),
            din("l2f", "down_w", [12, 2048, 2, 2]), din("l2f", "W_con", [4096, D]),
            din("l2f", "b_con", [D]), din("l2f", "down_b", [D]), xo))
        stage(6, lambda xi, xo: emit_l3a(
            B, xi, din("l3a", "wqkvT", [D, 3 * D]), din("l3a", "bqkv", [3 * D]),
            din("l3a", "woutT", [D, D]), din("l3a", "bout", [D]), xo))
        fc1T = din("l3f", "fc1T", [15, D, 2048])
        fc2T = din("l3f", "fc2T", [15, D, D])
        stage(7, lambda xi, xo: emit_l3f(
            B, xi, din("l3f", "gwT", [D, 15]), [fc1T[e] for e in range(15)],
            din("l3f", "fc1b", [15, 2048]), din("l3f", "gg", [15, D]),
            [fc2T[e] for e in range(15)], din("l3f", "b16", [16, D]),
            din("l3f", "s1T", [D, 4096]), din("l3f", "s1b", [4096]),
            din("l3f", "sgg", [2048]), din("l3f", "s2T", [2048, D]), xo))
        em.finalize(block)
    return nc


def kernel_fused(**inputs):
    x = np.ascontiguousarray(np.asarray(inputs["L0_attn_hidden_states"], dtype=np.float32))
    ncores = x.shape[0]
    wmap = {}
    for name, build, wm in _prep_stages(inputs):
        for k, v in wm.items():
            wmap[f"in_{name}_{k}"] = v
    in_maps = [dict(wmap, x=np.ascontiguousarray(x[c])) for c in range(ncores)]
    res = _launch_retry(build_fused_program, in_maps, ncores, attempts=2)
    return np.stack([np.asarray(res.results[c]["xout"]) for c in range(ncores)], axis=0)


def _launch_retry(build, in_maps, ncores, attempts=3):
    import time as _time
    last = None
    for att in range(attempts):
        nc = build()
        try:
            return run_bass_kernel_spmd(nc, in_maps, core_ids=list(range(ncores)))
        except Exception as ex:
            last = ex
            _time.sleep(10)
    raise last


def kernel(**inputs):
    return kernel_fused(**inputs)


def kernel_unfused(**inputs):
    x = np.ascontiguousarray(np.asarray(inputs["L0_attn_hidden_states"], dtype=np.float32))
    ncores = x.shape[0]
    shards = [np.ascontiguousarray(x[c]) for c in range(ncores)]
    for name, build, wmap in _prep_stages(inputs):
        in_maps = [dict(wmap, x=shards[c]) for c in range(ncores)]
        res = _launch_retry(build, in_maps, ncores)
        shards = [np.asarray(res.results[c]["xout"]) for c in range(ncores)]
    return np.stack(shards, axis=0)


# version 04
# speedup vs baseline: 1.0225x; 1.0159x over previous
import math
from contextlib import ExitStack

import numpy as np

import concourse.bass as bass
import concourse.mybir as mybir
from concourse.bass_utils import run_bass_kernel_spmd
from concourse.alu_op_type import AluOpType as OP

F32 = mybir.dt.float32
BF16 = mybir.dt.bfloat16
I32 = mybir.dt.int32
AF = mybir.ActivationFunctionType
AX = mybir.AxisListType

T = 4096
D = 1024
NT = T // 128
NC = D // 128
TC = 512
NTC = T // TC

ENGS = ("t", "v", "s", "g", "y")


class Step:
    __slots__ = ("eng", "fn", "deps", "inc", "val", "ch")

    def __init__(self, eng, fn, inc=1, ch=None):
        self.eng, self.fn, self.inc, self.ch = eng, fn, inc, ch
        self.deps = []
        self.val = None


class Em:

    def __init__(self, nc, ctx):
        self.nc = nc
        self.ctx = ctx
        self.steps = []
        self.sems = {e: ctx.enter_context(nc.semaphore(f"S_{e}")) for e in ENGS}
        self.chsems = {}
        self.chcount = {}
        self.writer = {}
        self.readers = {}

    def _dep_list(self, reads, writes):
        deps = []
        for b in reads:
            w = self.writer.get(b)
            if w is not None:
                deps.append(w)
        for b in writes:
            deps.extend(self.readers.get(b, ()))
            w = self.writer.get(b)
            if w is not None:
                deps.append(w)
        out = []
        seen = set()
        for d in deps:
            if id(d) not in seen:
                seen.add(id(d))
                if d.ch is not None:
                    out.append((d, self.chcount[d.ch]))
                else:
                    out.append((d, None))
        return out

    def _record(self, s, reads, writes):
        self.steps.append(s)
        for b in reads:
            self.readers.setdefault(b, []).append(s)
        for b in writes:
            self.writer[b] = s
            self.readers[b] = []

    def op(self, eng, fn, reads=(), writes=(), extra_deps=()):
        s = Step(eng, fn)
        s.deps = self._dep_list(reads, writes) + [(d, None) for d in extra_deps]
        self._record(s, reads, writes)
        return s

    def dma(self, ch, eng, out, in_, reads=(), writes=(), **kw):
        if ch not in self.chsems:
            self.chsems[ch] = self.ctx.enter_context(self.nc.semaphore(f"C_{ch}"))
            self.chcount[ch] = 0
            self.chlast = getattr(self, "chlast", {})
            self.chlast[ch] = None

        def fn(e, out=out, in_=in_, kw=kw):
            return e.dma_start(out, in_, **kw)

        s = Step(eng, fn, inc=16, ch=ch)
        s.deps = self._dep_list(reads, writes)
        prev = self.chlast.get(ch)
        if prev is not None:
            s.deps.append((prev, prev.val))
        self.chcount[ch] += 16
        s.val = self.chcount[ch]
        self.chlast[ch] = s
        self._record(s, reads, writes)
        return s

    def barrier(self):
        lasts = {}
        for s in self.steps:
            lasts[(s.eng, s.ch)] = s
        alll = [(d, self.chcount[d.ch] if d.ch is not None else None)
                for d in lasts.values()]
        newwriter = {}
        for e in ENGS:
            s = Step(e, None)
            s.deps = list(alll)
            self.steps.append(s)
            newwriter[("barrier", e)] = s
        self.writer = newwriter
        self.readers = {}

    def finalize(self, block):
        needed = set()
        for s in self.steps:
            s.deps = [(d, v) for (d, v) in s.deps
                      if not (d.ch is None and d.eng == s.eng == "t")]
            for d, v in s.deps:
                if d.ch is None:
                    needed.add(id(d))
        cnt = {e: 0 for e in ENGS}
        for s in self.steps:
            if s.ch is None and id(s) in needed and s.fn is not None:
                cnt[s.eng] += 1
                s.val = cnt[s.eng]
        final_ch = dict(self.chcount)
        final_eng = dict(cnt)

        def make_emitter(eng_name):
            mysteps = [s for s in self.steps if s.eng == eng_name]

            def emit(engobj):
                hw = {}
                for s in mysteps:
                    for d, vover in s.deps:
                        if d.ch is not None:
                            k, sem, v = ("ch", d.ch), self.chsems[d.ch], vover
                        else:
                            if d.val is None:
                                continue
                            k, sem, v = ("e", d.eng), self.sems[d.eng], d.val
                        if hw.get(k, 0) < v:
                            engobj.wait_ge(sem, v)
                            hw[k] = v
                    if s.fn is None:
                        continue
                    instr = s.fn(engobj)
                    if s.val is not None:
                        if s.ch is not None:
                            instr.then_inc(self.chsems[s.ch], 16)
                        else:
                            instr.then_inc(self.sems[s.eng], 1)
                if eng_name == "y":
                    for ch, v in final_ch.items():
                        if v > 0:
                            engobj.wait_ge(self.chsems[ch], v)
                    for e, v in final_eng.items():
                        if v > 0 and e != "y":
                            engobj.wait_ge(self.sems[e], v)
            return emit

        for e, section in (("t", block.tensor), ("v", block.vector), ("s", block.scalar),
                           ("g", block.gpsimd), ("y", block.sync)):
            section(make_emitter(e))



class Bundle:
    pass


def alloc_common(nc, ctx, em):
    B = Bundle()
    B.nc, B.em, B.ctx = nc, em, ctx
    B.ident = ctx.enter_context(nc.sbuf_tensor("ident_bf", [128, 128], BF16))
    B.iden_f32 = ctx.enter_context(nc.sbuf_tensor("iden_f32", [128, 128], F32))
    B.iota_fp = ctx.enter_context(nc.sbuf_tensor("iota_fp", [128, 512], F32))
    B.ones_t = ctx.enter_context(nc.sbuf_tensor("ones_t", [128, 128], BF16))
    B.consts = ctx.enter_context(nc.sbuf_tensor("consts", [128, 8], F32))
    B.iota_i = ctx.enter_context(nc.sbuf_tensor("iota_i", [128, 512], I32))
    B.xtile = ctx.enter_context(nc.sbuf_tensor("xtile", [128, D], F32))
    B.bn12 = ctx.enter_context(nc.sbuf_tensor("bn12", [128, 12], F32))
    B.mv = ctx.enter_context(nc.sbuf_tensor("mv", [128, 4], F32))
    B.xl = ctx.enter_context(nc.sbuf_tensor("xl", [128, D], BF16))
    B.ps_bf = ctx.enter_context(nc.psum_tensor("ps_bf", [128, 1024], BF16))
    B.ps = [ctx.enter_context(nc.psum_tensor(f"ps{i}", [128, 512], F32)) for i in range(7)]
    em.op("g", lambda g: g.iota(B.iota_i[:, :], [[1, 512]], base=0, channel_multiplier=-1),
          writes=[("iota_i",)])
    em.op("v", lambda v: v.tensor_copy(B.iota_fp[:, :], B.iota_i[:, :]),
          reads=[("iota_i",)], writes=[("iota_fp",)])
    em.op("v", lambda v: v.tensor_scalar(B.ident[:, :], B.iota_fp[:, 0:128], 0.0, None, OP.is_equal),
          reads=[("iota_fp",)], writes=[("ident",)])
    em.op("v", lambda v: v.tensor_scalar(B.iden_f32[:, :], B.iota_fp[:, 0:128], 0.0, None, OP.is_equal),
          reads=[("iota_fp",)], writes=[("iden_f32",)])
    em.op("v", lambda v: v.memset(B.ones_t[:, :], 1.0), writes=[("ones_t",)])
    B.ones_f = ctx.enter_context(nc.sbuf_tensor("ones_f", [128, 128], F32))
    em.op("v", lambda v: v.memset(B.ones_f[:, :], 1.0), writes=[("ones_f",)])
    em.op("v", lambda v: v.memset(B.consts[:, 0:1], 1e-5), writes=[("consts", 0)])
    em.op("v", lambda v: v.memset(B.consts[:, 1:2], 1e-6), writes=[("consts", 1)])
    em.op("v", lambda v: v.memset(B.consts[:, 2:3], 1.0), writes=[("consts", 2)])
    return B


def emit_ln_xlt(B, x_dram, xlt, tag):
    em = B.em
    kx, kbn, kmv, kxl, kps = ("xtile",), ("bn12",), ("mv",), ("xl",), ("psbf",)
    keys = []
    for t in range(NT):
        em.dma("xy", "y", B.xtile[:, :], x_dram[t * 128:(t + 1) * 128, :], writes=[kx])
        em.op("v", lambda v: v.bn_stats(B.bn12[:, 0:6], B.xtile[:, 0:512]),
              reads=[kx], writes=[(tag, "bn6a", t)])
        em.op("v", lambda v: v.bn_stats(B.bn12[:, 6:12], B.xtile[:, 512:1024]),
              reads=[kx], writes=[kbn], extra_deps=[em.writer[(tag, "bn6a", t)]])
        em.op("v", lambda v: v.bn_aggr(B.mv[:, 0:2], B.bn12[:, :]),
              reads=[kbn, (tag, "bn6a", t)], writes=[("mv", 0)])
        em.op("s", lambda s: s.activation(B.mv[:, 2:3], B.mv[:, 1:2], AF.Ln,
                                          bias=B.consts[:, 0:1], scale=1.0),
              reads=[("mv", 0), ("consts", 0)], writes=[("mv", 2)])
        em.op("s", lambda s: s.activation(B.mv[:, 3:4], B.mv[:, 2:3], AF.Exp, scale=-0.5),
              reads=[("mv", 2)], writes=[("mv", 3)])
        em.op("v", lambda v: v.tensor_scalar(B.xl[:, :], B.xtile[:, :],
                                             B.mv[:, 0:1], B.mv[:, 3:4],
                                             OP.subtract, OP.mult),
              reads=[kx, ("mv", 0), ("mv", 3)], writes=[kxl])
        for c in range(NC):
            em.op("t", lambda t_, c=c: t_.transpose(
                B.ps_bf[:, c * 128:(c + 1) * 128], B.xl[:, c * 128:(c + 1) * 128], B.ident[:, :]),
                reads=[kxl, ("ident",)], writes=[kps])
        k = (tag, "xlt", t)
        em.op("v", lambda v, t=t: v.tensor_copy(
            xlt[:, :].rearrange("k (c tt) -> k c tt", c=NC)[:, :, t * 128:(t + 1) * 128],
            B.ps_bf[:, :].rearrange("k (c m) -> k c m", c=NC)),
            reads=[kps], writes=[k])
        keys.append(k)
    return keys


def xlt_chunk(xlt, c, t0, n):
    return xlt[:, c * T + t0: c * T + t0 + n]



def emit_l1f(B, x_dram, wT_dram, xout_dram, tag="l1f"):
    em, nc, ctx = B.em, B.nc, B.ctx
    xlt = ctx.enter_context(nc.sbuf_tensor("l1f_xlt", [128, NC * T], BF16))
    w_f32 = ctx.enter_context(nc.sbuf_tensor("l1f_w", [128, NC * D], F32))
    w_bf = ctx.enter_context(nc.sbuf_tensor("l1f_wq", [128, NC * D], BF16))
    w_rt = ctx.enter_context(nc.sbuf_tensor("l1f_wrt", [128, 2 * D], F32))
    red = ctx.enter_context(nc.sbuf_tensor("l1f_red", [128, NC + 2], F32))
    sc = ctx.enter_context(nc.sbuf_tensor("l1f_sc", [128, 4], F32))
    scp = ctx.enter_context(nc.sbuf_tensor("l1f_scp", [1, 4], F32))
    out_sb = ctx.enter_context(nc.sbuf_tensor("l1f_out", [128, D], F32))
    xres = ctx.enter_context(nc.sbuf_tensor("l1f_xres", [128, D], F32))

    xlt_keys = emit_ln_xlt(B, x_dram, xlt, tag)

    wtr = wT_dram.rearrange("(c k) o -> k c o", c=NC)
    em.dma("wg", "g", w_f32[:, :].rearrange("k (c o) -> k c o", c=NC), wtr,
           writes=[("l1f_w",)])
    for c in range(NC):
        em.op("v", lambda v, c=c: v.reduce_sum(red[:, c:c + 1], w_f32[:, c * D:(c + 1) * D],
                                               axis=AX.X, apply_absolute_value=True),
              reads=[("l1f_w",)], writes=[("red", c)])
    em.op("v", lambda v: v.reduce_sum(red[:, NC:NC + 1], red[:, 0:NC], axis=AX.X),
          reads=[("red", c) for c in range(NC)], writes=[("redt",)])
    em.op("t", lambda t: t.matmul(B.ps[0][0:1, 0:128], red[:, NC:NC + 1],
                                  B.iden_f32[:, :], start=True, stop=True),
          reads=[("redt",), ("iden_f32",)], writes=[("ps", 0)])
    em.op("v", lambda v: v.reduce_sum(scp[0:1, 0:1], B.ps[0][0:1, 0:128], axis=AX.X),
          reads=[("ps", 0)], writes=[("scp", 0)])
    em.op("v", lambda v: v.tensor_scalar(scp[0:1, 1:2], scp[0:1, 0:1], 1.0 / (D * D), 1e-5,
                                         OP.mult, OP.max),
          reads=[("scp", 0)], writes=[("scp", 1)])
    em.op("v", lambda v: v.reciprocal(scp[0:1, 2:3], scp[0:1, 1:2]),
          reads=[("scp", 1)], writes=[("scp", 2)])
    em.op("t", lambda t: t.matmul(B.ps[1][:, 0:2], B.ones_f[0:1, 0:128], scp[0:1, 1:3],
                                  start=True, stop=True),
          reads=[("scp", 1), ("scp", 2), ("ones_f",)], writes=[("ps", 1)])
    em.op("v", lambda v: v.tensor_copy(sc[:, 0:2], B.ps[1][:, 0:2]),
          reads=[("ps", 1)], writes=[("sc",)])
    MAGIC = 12582912.0
    for c in range(NC):
        em.op("v", lambda v, c=c: v.tensor_scalar(w_rt[:, 0:D], w_f32[:, c * D:(c + 1) * D],
                                                  sc[:, 1:2], None, OP.mult),
              reads=[("l1f_w",), ("sc",)], writes=[("w_rt",)])
        em.op("v", lambda v, c=c: v.tensor_scalar(w_rt[:, D:2 * D], w_rt[:, 0:D],
                                                  MAGIC, MAGIC, OP.add, OP.subtract),
              reads=[("w_rt",)], writes=[("w_rt2",)])
        em.op("v", lambda v, c=c: v.tensor_scalar(w_bf[:, c * D:(c + 1) * D], w_rt[:, D:2 * D],
                                                  1.0, -1.0, OP.min, OP.max),
              reads=[("w_rt2",)], writes=[("wq_nm", c)])
        em.op("v", lambda v, c=c: v.tensor_scalar(w_bf[:, c * D:(c + 1) * D],
                                                  w_bf[:, c * D:(c + 1) * D],
                                                  sc[:, 0:1], None, OP.mult),
              reads=[("wq_nm", c), ("sc",)], writes=[("wq", c)])

    wq_keys = [("wq", c) for c in range(NC)]
    for t in range(NT):
        for half in range(2):
            for c in range(NC):
                em.op("t", lambda tt, t=t, c=c, half=half: tt.matmul(
                    B.ps[2 + half][:, :],
                    xlt_chunk(xlt, c, t * 128, 128),
                    w_bf[:, c * D + half * 512: c * D + half * 512 + 512],
                    start=(c == 0), stop=(c == NC - 1)),
                    reads=[xlt_keys[t]] + wq_keys, writes=[("ps", 2 + half)])
        em.dma("xy", "y", xres[:, :], x_dram[t * 128:(t + 1) * 128, :],
               writes=[("xres",)])
        for half in range(2):
            em.op("v", lambda v, t=t, half=half: v.tensor_tensor(
                out_sb[:, half * 512:half * 512 + 512], B.ps[2 + half][:, :],
                xres[:, half * 512:half * 512 + 512], OP.add),
                reads=[("ps", 2 + half), ("xres",)], writes=[("out", half)])
        em.dma("xo", "y", xout_dram[t * 128:(t + 1) * 128, :], out_sb[:, :],
               reads=[("out", 0), ("out", 1)])



def alloc_attn(B, tag):
    nc, ctx = B.nc, B.ctx
    A = Bundle()
    A.tag = tag
    A.qT = ctx.enter_context(nc.sbuf_tensor(f"{tag}_qT", [128, T], BF16))
    A.kT = ctx.enter_context(nc.sbuf_tensor(f"{tag}_kT", [128, T], BF16))
    A.vton = ctx.enter_context(nc.sbuf_tensor(f"{tag}_vton", [128, NT * 130], BF16))
    A.pt = [ctx.enter_context(nc.sbuf_tensor(f"{tag}_pt{i}", [128, 512], BF16)) for i in range(2)]
    A.rbc = ctx.enter_context(nc.sbuf_tensor(f"{tag}_rbc", [128, 512], BF16))
    A.rden_f = ctx.enter_context(nc.sbuf_tensor(f"{tag}_rdenf", [128, 512], F32))
    A.rden = ctx.enter_context(nc.sbuf_tensor(f"{tag}_rden", [128, 512], BF16))
    A.ot = ctx.enter_context(nc.sbuf_tensor(f"{tag}_ot", [128, 2 * T], BF16))
    A.nrm = ctx.enter_context(nc.sbuf_tensor(f"{tag}_nrm", [128, 24], F32))
    A.negb = ctx.enter_context(nc.sbuf_tensor(f"{tag}_negb", [128, 2], F32))
    A.indic = ctx.enter_context(nc.sbuf_tensor(f"{tag}_indic", [128, 128], BF16))
    A.masks = ctx.enter_context(nc.sbuf_tensor(f"{tag}_masks", [128, 4 * 512], BF16))
    em = B.em
    em.op("v", lambda v: v.memset(A.indic[:, :], 0.0), writes=[(tag, "indic")])
    em.op("v", lambda v: v.memset(A.indic[0:64, 0:1], 1.0),
          reads=[(tag, "indic")], writes=[(tag, "indic")])
    em.op("v", lambda v: v.memset(A.indic[64:128, 64:65], 1.0),
          reads=[(tag, "indic")], writes=[(tag, "indic")])
    for j in range(4):
        em.op("v", lambda v, j=j: v.tensor_scalar(A.masks[:, j * 512:(j + 1) * 512],
                                                  B.iota_fp[:, :], float(128 * j), None, OP.is_ge),
              reads=[("iota_fp",)], writes=[(tag, "mask", j)])
    em.op("v", lambda v: v.memset(
        A.vton[:, :].rearrange("p (t c) -> p t c", c=130)[:, :, 64:65], 1.0),
        writes=[(tag, "vton_ones")])
    em.op("v", lambda v: v.memset(
        A.vton[:, :].rearrange("p (t c) -> p t c", c=130)[:, :, 129:130], 1.0),
        writes=[(tag, "vton_ones2")])
    return A


def emit_bounds(B, A, scale, extra_col=None):
    em, tag = B.em, A.tag
    for which, src, key in (("q", A.qT, (tag, "qT")), ("k", A.kT, (tag, "kT"))):
        base = 0 if which == "q" else 8
        for tc in range(8):
            em.op("s", lambda s, src=src, tc=tc: s.activation(
                B.xl[:, 0:512], src[:, tc * 512:(tc + 1) * 512], AF.Square),
                reads=[key], writes=[("xl",)])
            em.op("t", lambda t, tc=tc: t.matmul(B.ps[3][:, :], A.indic[:, :],
                                                 B.xl[:, 0:512],
                                                 start=True, stop=True),
                  reads=[("xl",), (tag, "indic")], writes=[("ps", 3)])
            for r in (0, 64):
                em.op("v", lambda v, tc=tc, r=r, base=base: v.reduce_max(
                    A.nrm[r:r + 1, base + tc:base + tc + 1], B.ps[3][r:r + 1, :], axis=AX.X),
                    reads=[("ps", 3)], writes=[(tag, "nrm", which, tc, r)])
    for r in (0, 64):
        em.op("v", lambda v, r=r: v.reduce_max(A.nrm[r:r + 1, 16:17], A.nrm[r:r + 1, 0:8], axis=AX.X),
              reads=[(tag, "nrm", "q", tc, r) for tc in range(8)], writes=[(tag, "nq", r)])
        em.op("v", lambda v, r=r: v.reduce_max(A.nrm[r:r + 1, 17:18], A.nrm[r:r + 1, 8:16], axis=AX.X),
              reads=[(tag, "nrm", "k", tc, r) for tc in range(8)], writes=[(tag, "nk", r)])
        em.op("v", lambda v, r=r: v.tensor_tensor(A.nrm[r:r + 1, 18:19], A.nrm[r:r + 1, 16:17],
                                                  A.nrm[r:r + 1, 17:18], OP.mult),
              reads=[(tag, "nq", r), (tag, "nk", r)], writes=[(tag, "prod", r)])
        em.op("s", lambda s, r=r: s.activation(A.nrm[r:r + 1, 19:20], A.nrm[r:r + 1, 18:19], AF.Sqrt),
              reads=[(tag, "prod", r)], writes=[(tag, "sqb", r)])
        if extra_col is not None:
            et, ekey, ecol = extra_col
            em.op("v", lambda v, r=r: v.tensor_tensor(
                A.nrm[r:r + 1, 19:20], A.nrm[r:r + 1, 19:20],
                et[r:r + 1, ecol:ecol + 1], OP.add),
                reads=[(tag, "sqb", r), ekey], writes=[(tag, "sqb", r)])
        em.op("v", lambda v, r=r: v.tensor_scalar(A.nrm[r:r + 1, 20:21], A.nrm[r:r + 1, 19:20],
                                                  -scale, None, OP.mult),
              reads=[(tag, "sqb", r)], writes=[(tag, "negb1", r)])
        h = 0 if r == 0 else 1
        em.op("t", lambda t, r=r: t.matmul(B.ps[3][:, 0:1], B.ones_f[r:r + 1, 0:128],
                                           A.nrm[r:r + 1, 20:21], start=True, stop=True),
              reads=[(tag, "negb1", r), ("ones_f",)], writes=[("ps", 3)])
        em.op("v", lambda v, h=h: v.tensor_copy(A.negb[:, h:h + 1], B.ps[3][:, 0:1]),
              reads=[("ps", 3)], writes=[(tag, "negb", h)])


def emit_attn_core(B, A, pair, causal=True, use_exp=True, scale=1.0, dbg_cb=None,
                   s_extra=None):
    em, tag = B.em, A.tag
    nktiles = NT
    for h in range(2):
        hb = 64 * h
        for qc in range(NTC):
            rng = list(range(0, 4 * qc + 4)) if causal else list(range(nktiles))
            for i, ki in enumerate(rng):
                if dbg_cb is not None:
                    dbg_cb("pre_s", pair, h, qc, ki)
                sb = 4 + (ki % 2)
                em.op("t", lambda t, ki=ki, sb=sb, hb=hb, qc=qc: t.matmul(
                    B.ps[sb][:, :],
                    A.kT[hb:hb + 64, ki * 128:(ki + 1) * 128],
                    A.qT[hb:hb + 64, qc * 512:(qc + 1) * 512],
                    start=True, stop=(s_extra is None)),
                    reads=[(tag, "kT"), (tag, "qT")], writes=[("ps", sb)])
                if s_extra is not None:
                    s_extra(h, qc, ki, sb)
                pk = ("pt", ki % 2)
                if use_exp:
                    em.op("s", lambda s, ki=ki, sb=sb, h=h: s.activation(
                        A.pt[ki % 2][:, :], B.ps[sb][:, :], AF.Exp,
                        bias=A.negb[:, h:h + 1], scale=scale),
                        reads=[("ps", sb), (tag, "negb", h)], writes=[pk])
                else:
                    em.op("s", lambda s, ki=ki, sb=sb: s.activation(
                        A.pt[ki % 2][:, :], B.ps[sb][:, :], AF.Copy, scale=scale),
                        reads=[("ps", sb)], writes=[pk])
                if causal and ki >= 4 * qc:
                    j = ki - 4 * qc
                    em.op("v", lambda v, ki=ki, j=j: v.tensor_tensor(
                        A.pt[ki % 2][:, :], A.pt[ki % 2][:, :],
                        A.masks[:, j * 512:(j + 1) * 512], OP.mult),
                        reads=[pk, (tag, "mask", j)], writes=[pk])
                if dbg_cb is not None:
                    dbg_cb("post_mask", pair, h, qc, ki)
                em.op("t", lambda t, ki=ki, h=h, i=i, last=len(rng) - 1: t.matmul(
                    B.ps[2][0:65, :],
                    A.vton[:, ki * 130 + 65 * h: ki * 130 + 65 * h + 65],
                    A.pt[ki % 2][:, :],
                    start=(i == 0), stop=(i == last)),
                    reads=[pk, (tag, "vton")], writes=[("ps", 2)])
            em.op("v", lambda v: v.reciprocal(A.rden_f[64:65, :], B.ps[2][64:65, :]),
                  reads=[("ps", 2)], writes=[(tag, "rdenf")])
            em.op("v", lambda v: v.tensor_copy(A.rden[64:65, :], A.rden_f[64:65, :]),
                  reads=[(tag, "rdenf")], writes=[(tag, "rden")])
            em.op("t", lambda t: t.matmul(B.ps[3][:, :], B.ones_t[64:65, 0:128],
                                          A.rden[64:65, :], start=True, stop=True),
                  reads=[(tag, "rden"), ("ones_t",)], writes=[("ps", 3)])
            em.op("v", lambda v: v.tensor_copy(A.rbc[0:64, :], B.ps[3][0:64, :]),
                  reads=[("ps", 3)], writes=[(tag, "rbc")])
            if dbg_cb is not None:
                dbg_cb("post_pv", pair, h, qc, -1)
            em.op("v", lambda v, h=h, qc=qc: v.tensor_tensor(
                A.ot[0:64, h * T + qc * 512: h * T + (qc + 1) * 512],
                B.ps[2][0:64, :], A.rbc[0:64, :], OP.mult),
                reads=[("ps", 2), (tag, "rbc")], writes=[(tag, "ot", h, qc)])
    for h in range(2):
        em.dma("ot", "y", A.ot_hbm[128 * pair + 64 * h:128 * pair + 64 * h + 64, :],
               A.ot[0:64, h * T:(h + 1) * T],
               reads=[(tag, "ot", h, qc) for qc in range(NTC)],
               writes=[(tag, "ot_hbm", pair, h)])


def emit_proj_T(B, xlt, xlt_keys, w_sb, w_key, dst, dst_key, bias_col=None,
                bias_key=None, n_dout=128, nc_in=NC, src_chunk=None):
    em = B.em
    assert n_dout <= 128
    if src_chunk is None:
        src_chunk = lambda c, t0, n: xlt_chunk(xlt, c, t0, n)
    for tc in range(NTC):
        pb = (0, 6)[tc % 2]
        em.op("t", lambda t, tc=tc, pb=pb: [
            t.matmul(B.ps[pb][0:n_dout, :],
                     w_sb[:, c * n_dout:(c + 1) * n_dout],
                     src_chunk(c, tc * 512, 512),
                     start=(c == 0), stop=(c == nc_in - 1)) for c in range(nc_in)][-1],
            reads=[w_key] + xlt_keys, writes=[("ps", pb)])
        if bias_col is not None:
            em.op("v", lambda v, tc=tc, pb=pb: v.tensor_scalar(
                dst[0:n_dout, tc * 512:(tc + 1) * 512], B.ps[pb][0:n_dout, :],
                bias_col, None, OP.add),
                reads=[("ps", pb), bias_key], writes=[dst_key])
        else:
            em.op("v", lambda v, tc=tc, pb=pb: v.tensor_copy(
                dst[0:n_dout, tc * 512:(tc + 1) * 512], B.ps[pb][0:n_dout, :]),
                reads=[("ps", pb)], writes=[dst_key])



def emit_mha(B, x_dram, xout_dram, tag, pair_loads, wprojT, bproj,
             causal=True, use_exp=True, scale=1.0, softplus_qk=False, dbg=False):
    em, nc, ctx = B.em, B.nc, B.ctx
    xlt = ctx.enter_context(nc.sbuf_tensor(f"{tag}_xlt", [128, NC * T], BF16))
    A = alloc_attn(B, tag)
    if dbg:
        A.ot_hbm = nc.dram_tensor(f"{tag}_othbm", [D, T], BF16, kind="ExternalOutput").ap()
    else:
        A.ot_hbm = nc.dram_tensor(f"{tag}_othbm", [D, T], BF16).ap()
    wq = ctx.enter_context(nc.sbuf_tensor(f"{tag}_wq", [128, NC * 128], BF16))
    wk = ctx.enter_context(nc.sbuf_tensor(f"{tag}_wk", [128, NC * 128], BF16))
    wv = ctx.enter_context(nc.sbuf_tensor(f"{tag}_wv", [128, NC * 128], BF16))
    bcols = ctx.enter_context(nc.sbuf_tensor(f"{tag}_bcols", [128, 2], F32))
    vbrow = ctx.enter_context(nc.sbuf_tensor(f"{tag}_vbrow", [1, 128], F32))
    vbbc = ctx.enter_context(nc.sbuf_tensor(f"{tag}_vbbc", [128, 128], F32))
    wpr = ctx.enter_context(nc.sbuf_tensor(f"{tag}_wpr", [128, NC * D], BF16))
    bb = ctx.enter_context(nc.sbuf_tensor(f"{tag}_bb", [128, D], F32))
    otl = ctx.enter_context(nc.sbuf_tensor(f"{tag}_otl", [128, NC * 128], BF16))
    out_sb = ctx.enter_context(nc.sbuf_tensor(f"{tag}_out", [128, D], F32))
    xres = ctx.enter_context(nc.sbuf_tensor(f"{tag}_xres", [128, D], F32))
    if softplus_qk:
        spa = ctx.enter_context(nc.sbuf_tensor(f"{tag}_spa", [128, T], F32))
        spb = ctx.enter_context(nc.sbuf_tensor(f"{tag}_spb", [128, T], F32))

    xlt_keys = emit_ln_xlt(B, x_dram, xlt, tag)

    em.dma("wg", "g", wpr[:, :].rearrange("k (c o) -> k c o", c=NC),
           wprojT.rearrange("(c k) o -> k c o", c=NC), writes=[("wpr",)])
    for c in range(NC):
        em.dma("wg", "g", vbrow[0:1, 0:128],
               bproj[c * 128:(c + 1) * 128].rearrange("(o p) -> o p", o=1),
               writes=[("bprow",)])
        em.op("t", lambda t: t.matmul(B.ps[3][:, 0:128], B.ones_f[0:1, 0:128],
                                      vbrow[0:1, 0:128], start=True, stop=True),
              reads=[("bprow",), ("ones_f",)], writes=[("ps", 3)])
        em.op("v", lambda v, c=c: v.tensor_copy(bb[:, c * 128:(c + 1) * 128], B.ps[3][:, 0:128]),
              reads=[("ps", 3)], writes=[("bb", c)])

    def softplus_inplace(dst, key):
        em.op("s", lambda s: s.activation(spa[:, :], dst[:, :], AF.Abs),
              reads=[key], writes=[(tag, "spa")])
        em.op("s", lambda s: s.activation(spa[:, :], spa[:, :], AF.Exp, scale=-1.0),
              reads=[(tag, "spa")], writes=[(tag, "spa")])
        em.op("s", lambda s: s.activation(spa[:, :], spa[:, :], AF.Ln,
                                          bias=B.consts[:, 2:3], scale=1.0),
              reads=[(tag, "spa"), ("consts", 2)], writes=[(tag, "spa")])
        em.op("v", lambda v: v.tensor_scalar(spb[:, :], dst[:, :], 0.0, None, OP.max),
              reads=[key], writes=[(tag, "spb")])
        em.op("v", lambda v: v.tensor_tensor(dst[:, :], spa[:, :], spb[:, :], OP.add),
              reads=[(tag, "spa"), (tag, "spb")], writes=[key])

    for pair in range(8):
        pair_loads(em, pair, wq, wk, wv, bcols, vbrow)
        em.op("t", lambda t: t.matmul(B.ps[3][:, 0:128], B.ones_f[0:1, 0:128],
                                      vbrow[0:1, 0:128], start=True, stop=True),
              reads=[("bprow",), ("ones_f",)], writes=[("ps", 3)])
        em.op("v", lambda v: v.tensor_copy(vbbc[:, :], B.ps[3][:, 0:128]),
              reads=[("ps", 3)], writes=[("vbbc",)])
        emit_proj_T(B, xlt, xlt_keys, wq, ("wq",), A.qT, (tag, "qT"),
                    bias_col=bcols[:, 0:1], bias_key=("bcol", 0))
        emit_proj_T(B, xlt, xlt_keys, wk, ("wk",), A.kT, (tag, "kT"),
                    bias_col=bcols[:, 1:2], bias_key=("bcol", 1))
        if softplus_qk:
            softplus_inplace(A.qT, (tag, "qT"))
            softplus_inplace(A.kT, (tag, "kT"))
        for t in range(NT):
            em.op("t", lambda tt, t=t: [
                tt.matmul(B.ps[1][:, 0:128],
                          xlt_chunk(xlt, c, t * 128, 128),
                          wv[:, c * 128:(c + 1) * 128],
                          start=(c == 0), stop=(c == NC - 1)) for c in range(NC)][-1],
                reads=[("wv",)] + xlt_keys, writes=[("ps", 1)])
            em.op("v", lambda v, t=t: v.tensor_tensor(
                A.vton[:, t * 130:t * 130 + 130].rearrange("p (h e) -> p h e", h=2)[:, :, 0:64],
                B.ps[1][:, 0:128].rearrange("p (h e) -> p h e", h=2),
                vbbc[:, :].rearrange("p (h e) -> p h e", h=2), OP.add),
                reads=[("ps", 1), ("vbbc",)], writes=[(tag, "vton")])
        if use_exp:
            emit_bounds(B, A, scale)
        emit_attn_core(B, A, pair, causal=causal, use_exp=use_exp, scale=scale,
                       dbg_cb=getattr(B, "dbg_cb", None))

    emit_outproj(B, A, tag, x_dram, xout_dram, wpr, bb, otl, out_sb, xres,
                 with_bias=True)


def emit_outproj(B, A, tag, x_dram, xout_dram, wpr, bb, otl, out_sb, xres,
                 with_bias=True):
    em = B.em
    othbm_keys = [(tag, "ot_hbm", p, h) for p in range(8) for h in range(2)]
    for t in range(NT):
        em.dma("oc", "y", otl[:, :].rearrange("k (c o) -> k c o", c=NC),
               A.ot_hbm[:, t * 128:(t + 1) * 128].rearrange("(c k) o -> k c o", c=NC),
               reads=othbm_keys, writes=[("otl",)])
        for half in range(2):
            em.op("t", lambda tt, t=t, half=half: [
                tt.matmul(B.ps[4 + half][:, :],
                          otl[:, c * 128:(c + 1) * 128],
                          wpr[:, c * D + half * 512: c * D + half * 512 + 512],
                          start=(c == 0), stop=(c == NC - 1)) for c in range(NC)][-1],
                reads=[("otl",), ("wpr",)], writes=[("ps", 4 + half)])
        em.dma("xy", "y", xres[:, :], x_dram[t * 128:(t + 1) * 128, :], writes=[("xres",)])
        for half in range(2):
            if with_bias:
                em.op("v", lambda v, half=half: v.tensor_tensor(
                    out_sb[:, half * 512:half * 512 + 512], B.ps[4 + half][:, :],
                    bb[:, half * 512:half * 512 + 512], OP.add),
                    reads=[("ps", 4 + half)] + [("bb", c) for c in range(NC)],
                    writes=[("osb", half)])
                em.op("v", lambda v, half=half: v.tensor_tensor(
                    out_sb[:, half * 512:half * 512 + 512],
                    out_sb[:, half * 512:half * 512 + 512],
                    xres[:, half * 512:half * 512 + 512], OP.add),
                    reads=[("osb", half), ("xres",)], writes=[("osb", half)])
            else:
                em.op("v", lambda v, half=half: v.tensor_tensor(
                    out_sb[:, half * 512:half * 512 + 512], B.ps[4 + half][:, :],
                    xres[:, half * 512:half * 512 + 512], OP.add),
                    reads=[("ps", 4 + half), ("xres",)], writes=[("osb", half)])
        em.dma("xo", "y", xout_dram[t * 128:(t + 1) * 128, :], out_sb[:, :],
               reads=[("osb", 0), ("osb", 1)])


def emit_l0a(B, x_dram, wqkvT, bqkv, wprojT, bproj, xout_dram, tag="l0a", dbg=False):
    def pair_loads(em, pair, wq, wk, wv, bcols, vbrow):
        for wsb, key, col0 in ((wq, ("wq",), pair * 128), (wk, ("wk",), D + pair * 128),
                               (wv, ("wv",), 2 * D + pair * 128)):
            em.dma("wg", "g", wsb[:, :].rearrange("k (c o) -> k c o", c=NC),
                   wqkvT[:, col0:col0 + 128].rearrange("(c k) o -> k c o", c=NC),
                   writes=[key])
        em.dma("wg", "g", bcols[:, 0:1],
               bqkv[pair * 128:pair * 128 + 128].rearrange("(p o) -> p o", o=1),
               writes=[("bcol", 0)])
        em.dma("wg", "g", bcols[:, 1:2],
               bqkv[D + pair * 128:D + pair * 128 + 128].rearrange("(p o) -> p o", o=1),
               writes=[("bcol", 1)])
        em.dma("wg", "g", vbrow[0:1, 0:128],
               bqkv[2 * D + pair * 128:2 * D + pair * 128 + 128].rearrange("(o p) -> o p", o=1),
               writes=[("bprow",)])

    emit_mha(B, x_dram, xout_dram, tag, pair_loads, wprojT, bproj,
             causal=True, use_exp=True, scale=1.0 / 8.0, dbg=dbg)


def emit_l2a(B, x_dram, wqT, bq, wkT, bk, wvT, bv, woT, bo, xout_dram, tag="l2a"):
    def pair_loads(em, pair, wq, wk, wv, bcols, vbrow):
        for wsb, key, src in ((wq, ("wq",), wqT), (wk, ("wk",), wkT), (wv, ("wv",), wvT)):
            em.dma("wg", "g", wsb[:, :].rearrange("k (c o) -> k c o", c=NC),
                   src[:, pair * 128:(pair + 1) * 128].rearrange("(c k) o -> k c o", c=NC),
                   writes=[key])
        em.dma("wg", "g", bcols[:, 0:1],
               bq[pair * 128:pair * 128 + 128].rearrange("(p o) -> p o", o=1),
               writes=[("bcol", 0)])
        em.dma("wg", "g", bcols[:, 1:2],
               bk[pair * 128:pair * 128 + 128].rearrange("(p o) -> p o", o=1),
               writes=[("bcol", 1)])
        em.dma("wg", "g", vbrow[0:1, 0:128],
               bv[pair * 128:pair * 128 + 128].rearrange("(o p) -> o p", o=1),
               writes=[("bprow",)])

    emit_mha(B, x_dram, xout_dram, tag, pair_loads, woT, bo,
             causal=True, use_exp=False, scale=1.0, softplus_qk=True)


def emit_l3a(B, x_dram, wqkvT, bqkv, woutT, bout, xout_dram, tag="l3a"):
    wr = wqkvT.rearrange("(c k) (g e) -> k c g e", c=NC, e=192)
    br = bqkv.rearrange("(g e) -> g e", e=192)

    def pair_loads(em, pair, wq, wk, wv, bcols, vbrow):
        for wsb, key, off in ((wq, ("wq",), 0), (wk, ("wk",), 64), (wv, ("wv",), 128)):
            for g in range(2):
                em.dma("wg", "g",
                       wsb[:, :].rearrange("k (c o) -> k c o", c=NC)[:, :, 64 * g:64 * g + 64],
                       wr[:, :, 2 * pair + g, off:off + 64],
                       writes=[key])
        for g in range(2):
            h = 2 * pair + g
            em.dma("wg", "g", bcols[64 * g:64 * g + 64, 0:1],
                   bqkv[192 * h:192 * h + 64].rearrange("(p o) -> p o", o=1),
                   writes=[("bcol", 0)])
            em.dma("wg", "g", bcols[64 * g:64 * g + 64, 1:2],
                   bqkv[192 * h + 64:192 * h + 128].rearrange("(p o) -> p o", o=1),
                   writes=[("bcol", 1)])
            em.dma("wg", "g", vbrow[0:1, 64 * g:64 * g + 64],
                   bqkv[192 * h + 128:192 * h + 192].rearrange("(o p) -> o p", o=1),
                   writes=[("bprow",)])

    emit_mha(B, x_dram, xout_dram, tag, pair_loads, woutT, bout,
             causal=False, use_exp=True, scale=1.0 / 8.0)



def emit_l1a(B, x_dram, W_kv, g_kv, W_kdec, W_vdec, W_q, W_qdec, W_krope, W_qrope,
             W_o, cos4, sin4, rot_permT, xout_dram, tag="l1a"):
    em, nc, ctx = B.em, B.nc, B.ctx
    RSC = 1.0 / math.sqrt(96.0)
    qcT = ctx.enter_context(nc.sbuf_tensor(f"{tag}_qcT", [128, 4 * T], BF16))
    kvT = ctx.enter_context(nc.sbuf_tensor(f"{tag}_kvT", [128, 2 * T], BF16))
    krT = ctx.enter_context(nc.sbuf_tensor(f"{tag}_krT", [128, 4 * T], BF16))
    qrT = ctx.enter_context(nc.sbuf_tensor(f"{tag}_qrT", [128, 4 * T], BF16))
    ropeb = ctx.enter_context(nc.sbuf_tensor(f"{tag}_ropeb", [128, 12], F32))

    with ExitStack() as ctx2:
        xlt = ctx2.enter_context(nc.sbuf_tensor(f"{tag}_xlt", [128, NC * T], BF16))
        cosb = ctx2.enter_context(nc.sbuf_tensor(f"{tag}_cos", [128, 512], BF16))
        sinb = ctx2.enter_context(nc.sbuf_tensor(f"{tag}_sin", [128, 512], BF16))
        rotp = ctx2.enter_context(nc.sbuf_tensor(f"{tag}_rotp", [128, 128], BF16))
        wkv = ctx2.enter_context(nc.sbuf_tensor(f"{tag}_wkv", [128, NC * 256], BF16))
        wbig = ctx2.enter_context(nc.sbuf_tensor(f"{tag}_wbig", [128, NC * 128], BF16))
        gkvr = ctx2.enter_context(nc.sbuf_tensor(f"{tag}_gkvr", [1, 256], F32))
        gkvb = ctx2.enter_context(nc.sbuf_tensor(f"{tag}_gkvb", [128, 256], F32))
        kvtok = ctx2.enter_context(nc.sbuf_tensor(f"{tag}_kvtok", [128, 256], BF16))
        rcol = ctx2.enter_context(nc.sbuf_tensor(f"{tag}_rcol", [128, 4], F32))
        kvsq = B.xl[:, 0:256]
        rsc1 = B.xtile[:, 0:512]
        rsc2 = B.xtile[:, 512:1024]

        xlt_keys = emit_ln_xlt(B, x_dram, xlt, tag)
        em.dma("wg", "g", rotp[:, :], rot_permT, writes=[("rotp",)])
        em.dma("wg", "g", wkv[:, :].rearrange("k (c o) -> k c o", c=NC),
               W_kv.rearrange("(c k) o -> k c o", c=NC), writes=[("wkv",)])
        em.dma("wg", "g", gkvr[0:1, :], g_kv.rearrange("(o p) -> o p", o=1),
               writes=[("gkvr",)])
        em.op("t", lambda t: t.matmul(B.ps[3][:, 0:256], B.ones_f[0:1, 0:128],
                                      gkvr[0:1, :], start=True, stop=True),
              reads=[("gkvr",), ("ones_f",)], writes=[("ps", 3)])
        em.op("v", lambda v: v.tensor_copy(gkvb[:, :], B.ps[3][:, 0:256]),
              reads=[("ps", 3)], writes=[("gkvb",)])

        for t in range(NT):
            em.op("t", lambda tt, t=t: [
                tt.matmul(B.ps[0][:, 0:256], xlt_chunk(xlt, c, t * 128, 128),
                          wkv[:, c * 256:(c + 1) * 256],
                          start=(c == 0), stop=(c == NC - 1)) for c in range(NC)][-1],
                reads=[("wkv",)] + xlt_keys, writes=[("ps", 0)])
            em.op("s", lambda s: s.activation(kvsq, B.ps[0][:, 0:256], AF.Square,
                                              accum_out=rcol[:, 0:1]),
                  reads=[("ps", 0)], writes=[("xl",), ("rcol", 0)])
            em.op("s", lambda s: s.activation(rcol[:, 1:2], rcol[:, 0:1], AF.Ln,
                                              bias=B.consts[:, 1:2], scale=1.0 / 256),
                  reads=[("rcol", 0), ("consts", 1)], writes=[("rcol", 1)])
            em.op("s", lambda s: s.activation(rcol[:, 2:3], rcol[:, 1:2], AF.Exp, scale=-0.5),
                  reads=[("rcol", 1)], writes=[("rcol", 2)])
            em.op("v", lambda v: v.scalar_tensor_tensor(kvtok[:, :], B.ps[0][:, 0:256],
                                                        rcol[:, 2:3], gkvb[:, :],
                                                        OP.mult, OP.mult),
                  reads=[("ps", 0), ("rcol", 2), ("gkvb",)], writes=[("kvtok",)])
            em.op("t", lambda tt, t=t: [
                tt.transpose(B.ps_bf[:, r * 128:(r + 1) * 128], kvtok[:, r * 128:(r + 1) * 128],
                             B.ident[:, :]) for r in range(2)][-1],
                reads=[("kvtok",), ("ident",)], writes=[("psbf",)])
            em.op("v", lambda v, t=t: v.tensor_copy(
                kvT[:, :].rearrange("k (r tt) -> k r tt", r=2)[:, :, t * 128:(t + 1) * 128],
                B.ps_bf[:, 0:256].rearrange("k (r m) -> k r m", r=2)),
                reads=[("psbf",)], writes=[(tag, "kvT")])

        for i in range(4):
            em.dma("wg", "g", wbig[:, :].rearrange("k (c o) -> k c o", c=NC),
                   W_q[:, i * 128:(i + 1) * 128].rearrange("(c k) o -> k c o", c=NC),
                   writes=[("wbig",)])
            emit_proj_T(B, xlt, xlt_keys, wbig, ("wbig",),
                        qcT[:, i * T:(i + 1) * T], (tag, "qcT"))
        for i in range(4):
            em.dma("wg", "g", wbig[:, :].rearrange("k (c o) -> k c o", c=NC),
                   W_krope[:, i * 128:(i + 1) * 128].rearrange("(c k) o -> k c o", c=NC),
                   writes=[("wbig",)])
            emit_proj_T(B, xlt, xlt_keys, wbig, ("wbig",),
                        krT[:, i * T:(i + 1) * T], (tag, "krT"))
        for i in range(4):
            em.dma("wg", "g", wbig[:, 0:4 * 128].rearrange("k (c o) -> k c o", c=4),
                   W_qrope[:, i * 128:(i + 1) * 128].rearrange("(c k) o -> k c o", c=4),
                   writes=[("wbig",)])
            emit_proj_T(B, xlt, [(tag, "qcT")], wbig, ("wbig",),
                        qrT[:, i * T:(i + 1) * T], (tag, "qrT"), nc_in=4,
                        src_chunk=lambda c, t0, n: qcT[:, c * T + t0:c * T + t0 + n])

        for tc in range(NTC):
            em.dma("wg", "g", cosb[:, :], cos4[:, tc * 512:(tc + 1) * 512], writes=[("cosb",)])
            em.dma("wg", "g", sinb[:, :], sin4[:, tc * 512:(tc + 1) * 512], writes=[("sinb",)])
            for name, ten, key in (("kr", krT, (tag, "krT")), ("qr", qrT, (tag, "qrT"))):
                for i in range(4):
                    sl = slice(i * T + tc * 512, i * T + tc * 512 + 512)
                    em.op("t", lambda t, sl=sl, ten=ten: t.matmul(
                        B.ps[0][:, :], rotp[:, :], ten[:, sl], start=True, stop=True),
                        reads=[key, ("rotp",)], writes=[("ps", 0)])
                    em.op("v", lambda v, sl=sl, ten=ten: v.tensor_tensor(
                        rsc1, ten[:, sl], cosb[:, :], OP.mult),
                        reads=[key, ("cosb",)], writes=[("xtile",)])
                    em.op("v", lambda v: v.tensor_tensor(
                        rsc2, B.ps[0][:, :], sinb[:, :], OP.mult),
                        reads=[("ps", 0), ("sinb",)], writes=[("xtile2",)])
                    em.op("v", lambda v, sl=sl, ten=ten: v.tensor_tensor(
                        ten[:, sl], rsc1, rsc2, OP.add),
                        reads=[("xtile",), ("xtile2",)], writes=[key])

        em.op("v", lambda v: v.memset(ropeb[:, :], 0.0), writes=[(tag, "ropeb")])
        sqs = B.xl[:, 0:512]
        for which, ten, key, col in (("q", qrT, (tag, "qrT"), 0), ("k", krT, (tag, "krT"), 4)):
            for i in range(4):
                for tc in range(NTC):
                    em.op("s", lambda s, ten=ten, i=i, tc=tc: s.activation(
                        sqs, ten[:, i * T + tc * 512:i * T + tc * 512 + 512], AF.Square),
                        reads=[key], writes=[("xl",)])
                    em.op("t", lambda t: t.matmul(B.ps[3][0:1, :], B.ones_t[:, 0:1],
                                                  sqs, start=True, stop=True),
                          reads=[("xl",), ("ones_t",)], writes=[("ps", 3)])
                    em.op("v", lambda v, col=col, i=i, tc=tc: v.reduce_max(
                        rsc2[0:1, 8 * i + tc + (256 if col else 0):8 * i + tc + 1 + (256 if col else 0)],
                        B.ps[3][0:1, :], axis=AX.X),
                        reads=[("ps", 3)], writes=[(tag, "rnm", col, i, tc)])
        for i in range(4):
            em.op("v", lambda v, i=i: v.reduce_max(ropeb[0:1, i:i + 1],
                                                   rsc2[0:1, 8 * i:8 * i + 8], axis=AX.X),
                  reads=[(tag, "rnm", 0, i, tc) for tc in range(NTC)],
                  writes=[(tag, "ropebq", i)])
            em.op("v", lambda v, i=i: v.reduce_max(ropeb[0:1, 4 + i:5 + i],
                                                   rsc2[0:1, 256 + 8 * i:256 + 8 * i + 8], axis=AX.X),
                  reads=[(tag, "rnm", 4, i, tc) for tc in range(NTC)],
                  writes=[(tag, "ropebk", i)])
            em.op("v", lambda v, i=i: v.tensor_tensor(ropeb[0:1, 8 + i:9 + i],
                                                      ropeb[0:1, i:i + 1],
                                                      ropeb[0:1, 4 + i:5 + i], OP.mult),
                  reads=[(tag, "ropebq", i), (tag, "ropebk", i)], writes=[(tag, "ropebp", i)])
            em.op("s", lambda s, i=i: s.activation(ropeb[0:1, 8 + i:9 + i],
                                                   ropeb[0:1, 8 + i:9 + i], AF.Sqrt),
                  reads=[(tag, "ropebp", i)], writes=[(tag, "ropebp", i)])
        em.op("t", lambda t: t.matmul(B.ps[3][:, 0:4], B.ones_f[0:1, 0:128],
                                      ropeb[0:1, 8:12], start=True, stop=True),
              reads=[(tag, "ropebp", i) for i in range(4)] + [("ones_f",)],
              writes=[("ps", 3)])
        em.op("v", lambda v: v.tensor_copy(ropeb[:, 8:12], B.ps[3][:, 0:4]),
              reads=[("ps", 3)], writes=[(tag, "ropebb")])

    em.barrier()

    A = alloc_attn(B, tag)
    A.ot_hbm = nc.dram_tensor(f"{tag}_othbm", [D, T], BF16).ap()
    wdec = ctx.enter_context(nc.sbuf_tensor(f"{tag}_wdec", [128, 4 * 128], BF16))
    wkdc = ctx.enter_context(nc.sbuf_tensor(f"{tag}_wkdc", [128, 2 * 128], BF16))
    wvdc = ctx.enter_context(nc.sbuf_tensor(f"{tag}_wvdc", [128, 2 * 128], BF16))
    wpr = ctx.enter_context(nc.sbuf_tensor(f"{tag}_wpr", [128, NC * D], BF16))
    otl = ctx.enter_context(nc.sbuf_tensor(f"{tag}_otl", [128, NC * 128], BF16))
    out_sb = ctx.enter_context(nc.sbuf_tensor(f"{tag}_out", [128, D], F32))
    xres = ctx.enter_context(nc.sbuf_tensor(f"{tag}_xres", [128, D], F32))
    em.dma("wg", "g", wpr[:, :].rearrange("k (c o) -> k c o", c=NC),
           W_o.rearrange("(c k) o -> k c o", c=NC), writes=[("wpr",)])

    def rope_rows(h):
        i, j = h // 4, h % 4
        return i, 32 * j

    for pair in range(8):
        em.dma("wg", "g", wdec[:, :].rearrange("k (c o) -> k c o", c=4),
               W_qdec[:, pair * 128:(pair + 1) * 128].rearrange("(c k) o -> k c o", c=4),
               writes=[("wdec",)])
        em.dma("wg", "g", wkdc[:, :].rearrange("k (c o) -> k c o", c=2),
               W_kdec[:, pair * 128:(pair + 1) * 128].rearrange("(c k) o -> k c o", c=2),
               writes=[("wkdc",)])
        em.dma("wg", "g", wvdc[:, :].rearrange("k (c o) -> k c o", c=2),
               W_vdec[:, pair * 128:(pair + 1) * 128].rearrange("(c k) o -> k c o", c=2),
               writes=[("wvdc",)])
        emit_proj_T(B, None, [(tag, "qcT")], wdec, ("wdec",), A.qT, (tag, "qT"),
                    nc_in=4, src_chunk=lambda c, t0, n: qcT[:, c * T + t0:c * T + t0 + n])
        emit_proj_T(B, None, [(tag, "kvT")], wkdc, ("wkdc",), A.kT, (tag, "kT"),
                    nc_in=2, src_chunk=lambda c, t0, n: kvT[:, c * T + t0:c * T + t0 + n])
        for t in range(NT):
            em.op("t", lambda tt, t=t: [
                tt.matmul(B.ps[1][:, 0:128], kvT[:, c * T + t * 128:c * T + t * 128 + 128],
                          wvdc[:, c * 128:(c + 1) * 128],
                          start=(c == 0), stop=(c == 1)) for c in range(2)][-1],
                reads=[("wvdc",), (tag, "kvT")], writes=[("ps", 1)])
            em.op("v", lambda v, t=t: v.tensor_copy(
                A.vton[:, t * 130:t * 130 + 130].rearrange("p (h e) -> p h e", h=2)[:, :, 0:64],
                B.ps[1][:, 0:128].rearrange("p (h e) -> p h e", h=2)),
                reads=[("ps", 1)], writes=[(tag, "vton")])
        emit_bounds(B, A, RSC, extra_col=(ropeb, (tag, "ropebb"), 8 + pair // 2))
        def s_extra(h, qc, ki, sbank, pair=pair):
            hh = pair * 2 + h
            i, r0 = rope_rows(hh)
            tp = {"tile_position": (r0, 0)} if r0 == 96 else {}
            em.op("t", lambda t, h=h, qc=qc, ki=ki, sbank=sbank, i=i, r0=r0, tp=tp: t.matmul(
                B.ps[sbank][:, :],
                krT[r0:r0 + 32, i * T + ki * 128:i * T + (ki + 1) * 128],
                qrT[r0:r0 + 32, i * T + qc * 512:i * T + (qc + 1) * 512],
                start=False, stop=True, **tp),
                reads=[(tag, "krT"), (tag, "qrT")], writes=[("ps", sbank)])
        emit_attn_core(B, A, pair, causal=True, use_exp=True, scale=RSC,
                       s_extra=s_extra)

    emit_outproj(B, A, tag, x_dram, xout_dram, wpr, None, otl, out_sb, xres,
                 with_bias=False)



L0F_VAR = set()


def emit_l0f(B, x_dram, w_gate, w1, w3, w2, xout_dram, tag="l0f"):
    VAR = L0F_VAR
    em, nc, ctx = B.em, B.nc, B.ctx
    E, H, NH = 8, 2048, 16
    xlt = ctx.enter_context(nc.sbuf_tensor(f"{tag}_xlt", [128, NC * T], BF16))
    y_hbm = nc.dram_tensor(f"{tag}_yhbm", [T, D], F32).ap()
    wg_sb = ctx.enter_context(nc.sbuf_tensor(f"{tag}_wg", [128, NC * E], BF16))
    w1sb = ctx.enter_context(nc.sbuf_tensor(f"{tag}_w1", [128, NC * H], BF16))
    w3sb = ctx.enter_context(nc.sbuf_tensor(f"{tag}_w3", [128, NC * H], BF16))
    w2sb = ctx.enter_context(nc.sbuf_tensor(f"{tag}_w2", [128, NH * D], BF16))
    hsb = ctx.enter_context(nc.sbuf_tensor(f"{tag}_h", [128, NH * 512], BF16))
    hsil = ctx.enter_context(nc.sbuf_tensor(f"{tag}_hsil", [128, 1024], F32))
    comb = ctx.enter_context(nc.sbuf_tensor(f"{tag}_comb", [128, NT * E], F32))
    yscr = ctx.enter_context(nc.sbuf_tensor(f"{tag}_yscr", [128, D], F32))
    r_lall = ctx.enter_context(nc.sbuf_tensor(f"{tag}_rlall", [128, NT * E], F32))
    r_t1 = ctx.enter_context(nc.sbuf_tensor(f"{tag}_rt1", [128, NT * E], F32))
    r_m = ctx.enter_context(nc.sbuf_tensor(f"{tag}_rm", [128, 2 * NT], F32))
    xres = ctx.enter_context(nc.sbuf_tensor(f"{tag}_xres", [128, D], F32))
    out_sb = ctx.enter_context(nc.sbuf_tensor(f"{tag}_out", [128, D], F32))

    xlt_keys = emit_ln_xlt(B, x_dram, xlt, tag)

    em.dma("wg", "g", wg_sb[:, :].rearrange("k (c e) -> k c e", c=NC),
           w_gate.rearrange("(c k) e -> k c e", c=NC), writes=[("wgsb",)])
    for t in range(NT):
        em.op("t", lambda tt, t=t: [
            tt.matmul(B.ps[0][:, 0:E], xlt_chunk(xlt, c, t * 128, 128),
                      wg_sb[:, c * E:(c + 1) * E],
                      start=(c == 0), stop=(c == NC - 1)) for c in range(NC)][-1],
            reads=[("wgsb",)] + xlt_keys, writes=[("ps", 0)])
        em.op("v", lambda v, t=t: v.tensor_copy(r_lall[:, t * E:(t + 1) * E], B.ps[0][:, 0:E]),
              reads=[("ps", 0)], writes=[("r_lall",)])

    def v3(ten):
        return ten[:, :].rearrange("p (t e) -> p t e", e=E)

    def bc(ten):
        return bass.AP(ten, 0, [[2 * NT, 128], [1, NT], [0, E]])

    em.op("v", lambda v: v.tensor_reduce(r_m[:, 0:NT], v3(r_lall), axis=AX.X, op=OP.max),
          reads=[("r_lall",)], writes=[("r_m",)])
    em.op("v", lambda v: v.tensor_tensor(v3(r_t1), v3(r_lall), bc(r_m), OP.subtract),
          reads=[("r_lall",), ("r_m",)], writes=[("r_t1",)])
    em.op("s", lambda s: s.activation(r_lall[:, :], r_t1[:, :], AF.Exp),
          reads=[("r_t1",), ("r_lall",)], writes=[("r_lall",)])
    em.op("v", lambda v: v.tensor_scalar(comb[:, :], r_t1[:, :], 0.0, 1e30,
                                         OP.is_ge, OP.mult),
          reads=[("r_t1",)], writes=[("r_comb_t",)])
    em.op("v", lambda v: v.tensor_tensor(comb[:, :], r_t1[:, :], comb[:, :], OP.subtract),
          reads=[("r_t1",), ("r_comb_t",)], writes=[("r_comb_t",)])
    em.op("v", lambda v: v.tensor_reduce(r_m[:, NT:2 * NT], v3(comb), axis=AX.X, op=OP.max),
          reads=[("r_comb_t",)], writes=[("r_m2",)])
    em.op("v", lambda v: v.tensor_tensor(v3(comb), v3(r_t1),
                                         bass.AP(r_m, NT, [[2 * NT, 128], [1, NT], [0, E]]),
                                         OP.is_ge),
          reads=[("r_t1",), ("r_m2",)], writes=[("r_comb_t",)])
    em.op("v", lambda v: v.tensor_tensor(r_t1[:, :], r_lall[:, :], comb[:, :], OP.mult),
          reads=[("r_lall",), ("r_comb_t",)], writes=[("r_t1",)])
    em.op("v", lambda v: v.tensor_reduce(r_m[:, 0:NT], v3(r_t1), axis=AX.X, op=OP.add),
          reads=[("r_t1",)], writes=[("r_m",)])
    em.op("v", lambda v: v.reciprocal(r_m[:, NT:2 * NT], r_m[:, 0:NT]),
          reads=[("r_m",)], writes=[("r_m2",)])
    em.op("v", lambda v: v.tensor_tensor(v3(comb), v3(r_t1),
                                         bass.AP(r_m, NT, [[2 * NT, 128], [1, NT], [0, E]]),
                                         OP.mult),
          reads=[("r_t1",), ("r_m2",)],
          writes=[(tag, "comb", t) for t in range(NT)])

    comb_keys = [(tag, "comb", t) for t in range(NT)]

    ya_keys = []
    n_experts = 1 if "oneexp" in VAR else E
    for e in range(n_experts):
        em.dma("wg", "g", w1sb[:, :].rearrange("k (c h) -> k c h", c=NC),
               w1[e].rearrange("(c k) h -> k c h", c=NC), writes=[("w1sb",)])
        em.dma("wg", "g", w3sb[:, :].rearrange("k (c h) -> k c h", c=NC),
               w3[e].rearrange("(c k) h -> k c h", c=NC), writes=[("w3sb",)])
        em.dma("wg", "g", w2sb[:, :].rearrange("k (hc d) -> k hc d", hc=NH),
               w2[e].rearrange("(hc k) d -> k hc d", hc=NH), writes=[("w2sb",)])
        for tc in range(NTC):
            for hc in range(NH):
                pa, pb_ = ((4, 5), (6, 1))[hc % 2]
                em.op("t", lambda tt, tc=tc, hc=hc, pa=pa: [
                    tt.matmul(B.ps[pa][:, :], w1sb[:, c * H + hc * 128: c * H + hc * 128 + 128],
                              xlt_chunk(xlt, c, tc * 512, 512),
                              start=(c == 0), stop=(c == NC - 1)) for c in range(NC)][-1],
                    reads=[("w1sb",)] + xlt_keys, writes=[("ps", pa)])
                em.op("t", lambda tt, tc=tc, hc=hc, pb_=pb_: [
                    tt.matmul(B.ps[pb_][:, :], w3sb[:, c * H + hc * 128: c * H + hc * 128 + 128],
                              xlt_chunk(xlt, c, tc * 512, 512),
                              start=(c == 0), stop=(c == NC - 1)) for c in range(NC)][-1],
                    reads=[("w3sb",)] + xlt_keys, writes=[("ps", pb_)])
                em.op("s", lambda s, pa=pa, hc=hc: s.activation(
                    hsil[:, 512 * (hc % 2):512 * (hc % 2) + 512], B.ps[pa][:, :],
                    AF.Copy if "nosilu" in VAR else AF.Silu),
                    reads=[("ps", pa)], writes=[("hsil", hc % 2)])
                em.op("v", lambda v, hc=hc, pb_=pb_: v.tensor_tensor(
                    hsb[:, hc * 512:(hc + 1) * 512], hsil[:, 512 * (hc % 2):512 * (hc % 2) + 512],
                    B.ps[pb_][:, :], OP.mult),
                    reads=[("hsil", hc % 2), ("ps", pb_)], writes=[("hsb", hc)])
            for ti in range(4):
                t = tc * 4 + ti
                for half in range(2):
                    em.op("t", lambda tt, t=t, ti=ti, half=half: [
                        tt.matmul(B.ps[2 + half][:, :],
                                  hsb[:, hc * 512 + ti * 128: hc * 512 + ti * 128 + 128],
                                  w2sb[:, hc * D + half * 512: hc * D + half * 512 + 512],
                                  start=(hc == 0), stop=(hc == NH - 1)) for hc in range(NH)][-1],
                        reads=[("hsb", hc) for hc in range(NH)] + [("w2sb",)],
                        writes=[("ps", 2 + half)])
                    em.op("v", lambda v, t=t, e=e, half=half: v.tensor_scalar(
                        yscr[:, half * 512:half * 512 + 512], B.ps[2 + half][:, :],
                        comb[:, t * E + e:t * E + e + 1], None, OP.mult),
                        reads=[("ps", 2 + half), (tag, "comb", t)], writes=[("ysc", half)])
                kw = {} if (e == 0 or "noacc" in VAR) else {"accum_op": OP.add}
                s = em.dma("ya", "g", y_hbm[t * 128:(t + 1) * 128, :], yscr[:, :],
                           reads=[("ysc", 0), ("ysc", 1)],
                           writes=[(tag, "yhbm", t) if e == n_experts - 1 else (tag, "yhbm_p", t, e)],
                           **kw)
                if e == n_experts - 1:
                    ya_keys.append((tag, "yhbm", t))

    for t in range(NT):
        em.dma("xy", "y", xres[:, :], x_dram[t * 128:(t + 1) * 128, :], writes=[("xres",)])
        em.dma("yl", "y", yscr[:, :], y_hbm[t * 128:(t + 1) * 128, :],
               reads=[(tag, "yhbm", t), ("ysc", 0), ("ysc", 1)], writes=[("yscr_f",)])
        em.op("v", lambda v: v.tensor_tensor(out_sb[:, :], yscr[:, :], xres[:, :], OP.add),
              reads=[("yscr_f",), ("xres",)], writes=[("osb", 0)])
        em.dma("xo", "y", xout_dram[t * 128:(t + 1) * 128, :], out_sb[:, :],
               reads=[("osb", 0)])



def row_to_cols(B, em, dst, dst_key, row_sb, row_key, n):
    em.op("t", lambda t: [
        t.matmul(B.ps[1][:, j:j + 1], row_sb[0:1, j * 128:(j + 1) * 128],
                 B.ones_f[0:1, 0:1], start=(j == 0), stop=(j == n - 1)) for j in range(n)][-1],
        reads=[row_key, ("ones_f",)], writes=[("ps", 1)])
    em.op("v", lambda v: v.tensor_copy(dst[:, 0:n], B.ps[1][:, 0:n]),
          reads=[("ps", 1)], writes=[dst_key])


def emit_l3f(B, x_dram, gate_wT, fc1_wT, fc1_b, geglu, fc2_wT, b16,
             s_fc1_wT, s_fc1_b, s_geglu, s_fc2_wT, xout_dram, tag="l3f"):
    em, nc, ctx = B.em, B.nc, B.ctx
    E = 15
    xlt = ctx.enter_context(nc.sbuf_tensor(f"{tag}_xlt", [128, NC * T], BF16))
    y_hbm = nc.dram_tensor(f"{tag}_yhbm", [T, D], F32).ap()
    wg_sb = ctx.enter_context(nc.sbuf_tensor(f"{tag}_wg", [128, NC * E], BF16))
    comb = ctx.enter_context(nc.sbuf_tensor(f"{tag}_comb", [128, NT * 16], F32))
    r_lall = ctx.enter_context(nc.sbuf_tensor(f"{tag}_rlall", [128, NT * 15], F32))
    r_t1 = ctx.enter_context(nc.sbuf_tensor(f"{tag}_rt1", [128, NT * 15], F32))
    r_t2 = ctx.enter_context(nc.sbuf_tensor(f"{tag}_rt2", [128, NT * 15], F32))
    r_m = ctx.enter_context(nc.sbuf_tensor(f"{tag}_rm", [128, 2 * NT], F32))
    yscr = ctx.enter_context(nc.sbuf_tensor(f"{tag}_yscr", [128, D], F32))
    gel = ctx.enter_context(nc.sbuf_tensor(f"{tag}_gel", [128, 512], F32))
    fbrow = ctx.enter_context(nc.sbuf_tensor(f"{tag}_fbrow", [1, 4096], F32))
    fb = ctx.enter_context(nc.sbuf_tensor(f"{tag}_fb", [128, 32], F32))
    gg = ctx.enter_context(nc.sbuf_tensor(f"{tag}_gg", [128, 16], F32))
    b16sb = ctx.enter_context(nc.sbuf_tensor(f"{tag}_b16", [16, D], BF16))
    combt = ctx.enter_context(nc.sbuf_tensor(f"{tag}_combt", [16, 128], BF16))
    ytile = ctx.enter_context(nc.sbuf_tensor(f"{tag}_yt", [128, D], F32))
    xres = ctx.enter_context(nc.sbuf_tensor(f"{tag}_xres", [128, D], F32))
    out_sb = ctx.enter_context(nc.sbuf_tensor(f"{tag}_out", [128, D], F32))

    xlt_keys = emit_ln_xlt(B, x_dram, xlt, tag)

    em.dma("wg", "g", wg_sb[:, :].rearrange("k (c e) -> k c e", c=NC),
           gate_wT.rearrange("(c k) e -> k c e", c=NC), writes=[("wgsb",)])
    em.dma("wg", "g", b16sb[0:16, :], b16, writes=[("b16sb",)])
    em.op("v", lambda v: v.memset(
        comb[:, :].rearrange("k (t e) -> k t e", e=16)[:, :, 15:16], 1.0),
        writes=[(tag, "comb1",)])
    for t in range(NT):
        em.op("t", lambda tt, t=t: [
            tt.matmul(B.ps[0][:, 0:E], xlt_chunk(xlt, c, t * 128, 128),
                      wg_sb[:, c * E:(c + 1) * E],
                      start=(c == 0), stop=(c == NC - 1)) for c in range(NC)][-1],
            reads=[("wgsb",)] + xlt_keys, writes=[("ps", 0)])
        em.op("v", lambda v, t=t: v.tensor_copy(r_lall[:, t * E:(t + 1) * E], B.ps[0][:, 0:E]),
              reads=[("ps", 0)], writes=[("r_lall",)])

    def v3(ten):
        return ten[:, :].rearrange("p (t e) -> p t e", e=E)

    def bc(off):
        return bass.AP(r_m, off, [[2 * NT, 128], [1, NT], [0, E]])

    em.op("v", lambda v: v.tensor_reduce(r_m[:, 0:NT], v3(r_lall), axis=AX.X, op=OP.max),
          reads=[("r_lall",)], writes=[("r_m",)])
    em.op("v", lambda v: v.tensor_tensor(v3(r_t1), v3(r_lall), bc(0), OP.subtract),
          reads=[("r_lall",), ("r_m",)], writes=[("r_t1",)])
    em.op("s", lambda s: s.activation(r_lall[:, :], r_t1[:, :], AF.Exp),
          reads=[("r_t1",), ("r_lall",)], writes=[("r_lall",)])
    em.op("v", lambda v: v.tensor_reduce(r_m[:, 0:NT], v3(r_lall), axis=AX.X, op=OP.add),
          reads=[("r_lall",)], writes=[("r_m",)])
    em.op("v", lambda v: v.reciprocal(r_m[:, NT:2 * NT], r_m[:, 0:NT]),
          reads=[("r_m",)], writes=[("r_m2",)])
    em.op("v", lambda v: v.tensor_tensor(v3(r_t1), v3(r_lall), bc(NT), OP.mult),
          reads=[("r_lall",), ("r_m2",)], writes=[("r_t1",)])
    em.op("v", lambda v: v.tensor_reduce(r_m[:, 0:NT], v3(r_t1), axis=AX.X, op=OP.max),
          reads=[("r_t1",)], writes=[("r_m",)])
    em.op("v", lambda v: v.tensor_tensor(v3(r_lall), v3(r_t1), bc(0), OP.is_ge),
          reads=[("r_t1",), ("r_m",)], writes=[("r_lall",)])
    em.op("v", lambda v: v.tensor_tensor(r_lall[:, :], r_t1[:, :], r_lall[:, :], OP.subtract),
          reads=[("r_t1",), ("r_lall",)], writes=[("r_lall",)])
    em.op("v", lambda v: v.tensor_reduce(r_m[:, 0:NT], v3(r_lall), axis=AX.X, op=OP.max),
          reads=[("r_lall",)], writes=[("r_m",)])
    em.op("v", lambda v: v.tensor_tensor(v3(r_t2), v3(r_lall), bc(0), OP.is_ge),
          reads=[("r_lall",), ("r_m",)], writes=[("r_t2",)])
    em.op("v", lambda v: v.tensor_tensor(r_lall[:, :], r_lall[:, :], r_t2[:, :], OP.subtract),
          reads=[("r_lall",), ("r_t2",)], writes=[("r_lall",)])
    em.op("v", lambda v: v.tensor_reduce(r_m[:, 0:NT], v3(r_lall), axis=AX.X, op=OP.max),
          reads=[("r_lall",)], writes=[("r_m",)])
    em.op("v", lambda v: v.tensor_tensor(v3(r_lall), v3(r_t1), bc(0), OP.is_ge),
          reads=[("r_t1",), ("r_m",)], writes=[("r_lall",)])
    em.op("v", lambda v: v.tensor_tensor(r_t1[:, :], r_t1[:, :], r_lall[:, :], OP.mult),
          reads=[("r_t1",), ("r_lall",)], writes=[("r_t1",)])
    em.op("v", lambda v: v.tensor_reduce(r_m[:, 0:NT], v3(r_t1), axis=AX.X, op=OP.add),
          reads=[("r_t1",)], writes=[("r_m",)])
    em.op("v", lambda v: v.tensor_scalar(r_m[:, 0:NT], r_m[:, 0:NT], 1e-20, None, OP.add),
          reads=[("r_m",)], writes=[("r_m",)])
    em.op("v", lambda v: v.reciprocal(r_m[:, NT:2 * NT], r_m[:, 0:NT]),
          reads=[("r_m",)], writes=[("r_m2",)])
    em.op("v", lambda v: v.tensor_tensor(
        comb[:, :].rearrange("k (t g) -> k t g", g=16)[:, :, 0:E],
        v3(r_t1), bc(NT), OP.mult),
        reads=[("r_t1",), ("r_m2",)],
        writes=[(tag, "comb", t) for t in range(NT)])

    comb_keys = [(tag, "comb", t) for t in range(NT)] + [(tag, "comb1",)]

    with ExitStack() as ctx2:
        w1sb = ctx2.enter_context(nc.sbuf_tensor(f"{tag}_w1", [128, NC * 2048], BF16))
        w2sb = ctx2.enter_context(nc.sbuf_tensor(f"{tag}_w2", [128, NC * D], BF16))
        xh_sb = ctx2.enter_context(nc.sbuf_tensor(f"{tag}_xh", [128, 8 * 512], BF16))
        a_sb = ctx2.enter_context(nc.sbuf_tensor(f"{tag}_a", [128, 8 * 512], BF16))
        for e in range(E):
            em.dma("wg", "g", w1sb[:, :].rearrange("k (c h) -> k c h", c=NC),
                   fc1_wT[e].rearrange("(c k) h -> k c h", c=NC), writes=[("w1sb",)])
            em.dma("wg", "g", w2sb[:, :].rearrange("k (fc d) -> k fc d", fc=NC),
                   fc2_wT[e].rearrange("(fc k) d -> k fc d", fc=NC), writes=[("w2sb",)])
            em.dma("wg", "g", fbrow[0:1, 0:2048],
                   fc1_b[e].rearrange("(o p) -> o p", o=1), writes=[("fbrow",)])
            row_to_cols(B, em, fb, ("fb",), fbrow, ("fbrow",), 16)
            em.dma("wg", "g", fbrow[0:1, 0:1024],
                   geglu[e].rearrange("(o p) -> o p", o=1), writes=[("fbrow",)])
            row_to_cols(B, em, gg, ("gg",), fbrow, ("fbrow",), 8)
            for tc in range(NTC):
                for j in range(8):
                    em.op("t", lambda tt, tc=tc, j=j: [
                        tt.matmul(B.ps[4][:, :],
                                  w1sb[:, c * 2048 + j * 128: c * 2048 + j * 128 + 128],
                                  xlt_chunk(xlt, c, tc * 512, 512),
                                  start=(c == 0), stop=(c == NC - 1)) for c in range(NC)][-1],
                        reads=[("w1sb",)] + xlt_keys, writes=[("ps", 4)])
                    em.op("t", lambda tt, tc=tc, j=j: [
                        tt.matmul(B.ps[5][:, :],
                                  w1sb[:, c * 2048 + 1024 + j * 128: c * 2048 + 1024 + j * 128 + 128],
                                  xlt_chunk(xlt, c, tc * 512, 512),
                                  start=(c == 0), stop=(c == NC - 1)) for c in range(NC)][-1],
                        reads=[("w1sb",)] + xlt_keys, writes=[("ps", 5)])
                    em.op("v", lambda v, j=j: v.tensor_scalar(
                        xh_sb[:, j * 512:(j + 1) * 512], B.ps[4][:, :],
                        fb[:, j:j + 1], None, OP.add),
                        reads=[("ps", 4), ("fb",)], writes=[("xh", j)])
                    em.op("s", lambda s, j=j: s.activation(gel[:, :], B.ps[5][:, :], AF.Gelu,
                                                           bias=fb[:, 8 + j:9 + j], scale=1.0),
                          reads=[("ps", 5), ("fb",)], writes=[("gel",)])
                    em.op("v", lambda v, j=j: v.scalar_tensor_tensor(
                        a_sb[:, j * 512:(j + 1) * 512], gel[:, :], gg[:, j:j + 1],
                        xh_sb[:, j * 512:(j + 1) * 512], OP.mult, OP.mult),
                        reads=[("gel",), ("gg",), ("xh", j)], writes=[("a", j)])
                for ti in range(4):
                    t = tc * 4 + ti
                    for half in range(2):
                        em.op("t", lambda tt, ti=ti, half=half: [
                            tt.matmul(B.ps[2 + half][:, :],
                                      a_sb[:, j * 512 + ti * 128: j * 512 + ti * 128 + 128],
                                      w2sb[:, j * D + half * 512: j * D + half * 512 + 512],
                                      start=(j == 0), stop=(j == 7)) for j in range(8)][-1],
                            reads=[("a", j) for j in range(8)] + [("w2sb",)],
                            writes=[("ps", 2 + half)])
                        em.op("v", lambda v, t=t, e=e, half=half: v.tensor_scalar(
                            yscr[:, half * 512:half * 512 + 512], B.ps[2 + half][:, :],
                            comb[:, t * 16 + e:t * 16 + e + 1], None, OP.mult),
                            reads=[("ps", 2 + half), (tag, "comb", t)], writes=[("ysc", half)])
                    kw = {} if e == 0 else {"accum_op": OP.add}
                    em.dma("ya", "g", y_hbm[t * 128:(t + 1) * 128, :], yscr[:, :],
                           reads=[("ysc", 0), ("ysc", 1)],
                           writes=[(tag, "yhbm_p", t, e)], **kw)
    em.barrier()

    with ExitStack() as ctx2:
        s1sb = ctx2.enter_context(nc.sbuf_tensor(f"{tag}_s1", [128, NC * 2048], BF16))
        s2sb = ctx2.enter_context(nc.sbuf_tensor(f"{tag}_s2", [128, 8 * D], BF16))
        xh2 = ctx2.enter_context(nc.sbuf_tensor(f"{tag}_xh2", [128, 8 * 512], BF16))
        a2 = ctx2.enter_context(nc.sbuf_tensor(f"{tag}_a2", [128, 8 * 512], BF16))
        s1r = s_fc1_wT.rearrange("(c k) h -> k c h", c=NC)
        s2r = s_fc2_wT.rearrange("(fc k) d -> k fc d", fc=16)
        for fh in range(2):
            em.dma("wg", "g",
                   s1sb[:, :].rearrange("k (c h) -> k c h", c=NC)[:, :, 0:1024],
                   s1r[:, :, fh * 1024:fh * 1024 + 1024], writes=[("s1sb",)])
            em.dma("wg", "g",
                   s1sb[:, :].rearrange("k (c h) -> k c h", c=NC)[:, :, 1024:2048],
                   s1r[:, :, 2048 + fh * 1024:2048 + fh * 1024 + 1024], writes=[("s1sb",)])
            em.dma("wg", "g", s2sb[:, :].rearrange("k (fc d) -> k fc d", fc=8),
                   s2r[:, fh * 8:fh * 8 + 8, :], writes=[("s2sb",)])
            em.dma("wg", "g", fbrow[0:1, 0:1024],
                   s_fc1_b[fh * 1024:fh * 1024 + 1024].rearrange("(o p) -> o p", o=1),
                   writes=[("fbrow",)])
            em.dma("wg", "g", fbrow[0:1, 1024:2048],
                   s_fc1_b[2048 + fh * 1024:2048 + fh * 1024 + 1024].rearrange("(o p) -> o p", o=1),
                   writes=[("fbrow",)])
            row_to_cols(B, em, fb, ("fb",), fbrow, ("fbrow",), 16)
            em.dma("wg", "g", fbrow[0:1, 0:1024],
                   s_geglu[fh * 1024:fh * 1024 + 1024].rearrange("(o p) -> o p", o=1),
                   writes=[("fbrow",)])
            row_to_cols(B, em, gg, ("gg",), fbrow, ("fbrow",), 8)
            for tc in range(NTC):
                for j in range(8):
                    em.op("t", lambda tt, tc=tc, j=j: [
                        tt.matmul(B.ps[4][:, :],
                                  s1sb[:, c * 2048 + j * 128: c * 2048 + j * 128 + 128],
                                  xlt_chunk(xlt, c, tc * 512, 512),
                                  start=(c == 0), stop=(c == NC - 1)) for c in range(NC)][-1],
                        reads=[("s1sb",)] + xlt_keys, writes=[("ps", 4)])
                    em.op("t", lambda tt, tc=tc, j=j: [
                        tt.matmul(B.ps[5][:, :],
                                  s1sb[:, c * 2048 + 1024 + j * 128: c * 2048 + 1024 + j * 128 + 128],
                                  xlt_chunk(xlt, c, tc * 512, 512),
                                  start=(c == 0), stop=(c == NC - 1)) for c in range(NC)][-1],
                        reads=[("s1sb",)] + xlt_keys, writes=[("ps", 5)])
                    em.op("v", lambda v, j=j: v.tensor_scalar(
                        xh2[:, j * 512:(j + 1) * 512], B.ps[4][:, :],
                        fb[:, j:j + 1], None, OP.add),
                        reads=[("ps", 4), ("fb",)], writes=[("xh2", j)])
                    em.op("s", lambda s, j=j: s.activation(gel[:, :], B.ps[5][:, :], AF.Gelu,
                                                           bias=fb[:, 8 + j:9 + j], scale=1.0),
                          reads=[("ps", 5), ("fb",)], writes=[("gel",)])
                    em.op("v", lambda v, j=j: v.scalar_tensor_tensor(
                        a2[:, j * 512:(j + 1) * 512], gel[:, :], gg[:, j:j + 1],
                        xh2[:, j * 512:(j + 1) * 512], OP.mult, OP.mult),
                        reads=[("gel",), ("gg",), ("xh2", j)], writes=[("a2", j)])
                for ti in range(4):
                    t = tc * 4 + ti
                    for half in range(2):
                        em.op("t", lambda tt, ti=ti, half=half: [
                            tt.matmul(B.ps[2 + half][:, :],
                                      a2[:, j * 512 + ti * 128: j * 512 + ti * 128 + 128],
                                      s2sb[:, j * D + half * 512: j * D + half * 512 + 512],
                                      start=(j == 0), stop=(j == 7)) for j in range(8)][-1],
                            reads=[("a2", j) for j in range(8)] + [("s2sb",)],
                            writes=[("ps", 2 + half)])
                        em.op("v", lambda v, half=half: v.tensor_copy(
                            yscr[:, half * 512:half * 512 + 512], B.ps[2 + half][:, :]),
                            reads=[("ps", 2 + half)], writes=[("ysc", half)])
                    em.dma("ya", "g", y_hbm[t * 128:(t + 1) * 128, :], yscr[:, :],
                           reads=[("ysc", 0), ("ysc", 1)],
                           writes=[(tag, "yhbm", t) if fh == 1 else (tag, "yhbm_s", t)],
                           accum_op=OP.add)

    for t in range(NT):
        em.op("t", lambda tt, t=t: tt.matmul(B.ps[0][0:16, 0:128],
                                             comb[:, t * 16:(t + 1) * 16],
                                             B.iden_f32[:, :], start=True, stop=True),
              reads=[(tag, "comb", t), (tag, "comb1",), ("iden_f32",)], writes=[("ps", 0)])
        em.op("v", lambda v: v.tensor_copy(combt[0:16, :], B.ps[0][0:16, 0:128]),
              reads=[("ps", 0)], writes=[("combt",)])
        for half in range(2):
            em.op("t", lambda tt, half=half: tt.matmul(
                B.ps[2 + half][:, :], combt[0:16, :],
                b16sb[0:16, half * 512:half * 512 + 512], start=True, stop=True),
                reads=[("combt",), ("b16sb",)], writes=[("ps", 2 + half)])
        em.dma("xy", "y", xres[:, :], x_dram[t * 128:(t + 1) * 128, :], writes=[("xres",)])
        em.dma("yl", "y", ytile[:, :], y_hbm[t * 128:(t + 1) * 128, :],
               reads=[(tag, "yhbm", t)], writes=[("ytile",)])
        for half in range(2):
            em.op("v", lambda v, half=half: v.tensor_tensor(
                out_sb[:, half * 512:half * 512 + 512], B.ps[2 + half][:, :],
                ytile[:, half * 512:half * 512 + 512], OP.add),
                reads=[("ps", 2 + half), ("ytile",)], writes=[("osb", half)])
            em.op("v", lambda v, half=half: v.tensor_tensor(
                out_sb[:, half * 512:half * 512 + 512],
                out_sb[:, half * 512:half * 512 + 512],
                xres[:, half * 512:half * 512 + 512], OP.add),
                reads=[("osb", half), ("xres",)], writes=[("osb", half)])
        em.dma("xo", "y", xout_dram[t * 128:(t + 1) * 128, :], out_sb[:, :],
               reads=[("osb", 0), ("osb", 1)])



def emit_l2f(B, x_dram, W_expT, b_exp, up_w, up_b, down_w, W_con, b_con, down_b,
             xout_dram, tag="l2f"):
    em, nc, ctx = B.em, B.nc, B.ctx
    F = 4096
    MuT = ctx.enter_context(nc.sbuf_tensor(f"{tag}_MuT", [128, 32 * 128], BF16))
    MdT = ctx.enter_context(nc.sbuf_tensor(f"{tag}_MdT", [128, 32 * 128], BF16))
    GuT = ctx.enter_context(nc.sbuf_tensor(f"{tag}_GuT", [128, 32 * 128], BF16))
    GdT = ctx.enter_context(nc.sbuf_tensor(f"{tag}_GdT", [128, 32 * 128], BF16))
    b1sh = ctx.enter_context(nc.sbuf_tensor(f"{tag}_b1sh", [128, 32], F32))
    upbsh = ctx.enter_context(nc.sbuf_tensor(f"{tag}_upbsh", [128, 32], F32))

    gdram = nc.dram_tensor(f"{tag}_gdram", [F, 512], BF16).ap()
    zdram = nc.dram_tensor(f"{tag}_zdram", [F, 512], BF16).ap()
    h5dram = nc.dram_tensor(f"{tag}_h5dram", [F, T], BF16).ap()
    y_hbm = nc.dram_tensor(f"{tag}_yhbm", [T, D], F32).ap()
    b1dram = nc.dram_tensor(f"{tag}_b1d", [F], F32).ap()

    with ExitStack() as ctx2:
        masks = ctx2.enter_context(nc.sbuf_tensor(f"{tag}_masks", [128, 14 * 128], BF16))
        stg = ctx2.enter_context(nc.sbuf_tensor(f"{tag}_stg", [1, 3 * F], F32))
        wstage = ctx2.enter_context(nc.sbuf_tensor(f"{tag}_wst", [1, 8192], F32))
        vb = [ctx2.enter_context(nc.sbuf_tensor(f"{tag}_vb{i}", [128, 32 * 128], BF16))
              for i in range(2)]
        acc = ctx2.enter_context(nc.sbuf_tensor(f"{tag}_acc", [128, 128], BF16))
        tmp = ctx2.enter_context(nc.sbuf_tensor(f"{tag}_tmp", [128, 128], BF16))

        for mi in range(7):
            for sign in range(2):
                d = (2 ** mi) * (1 if sign == 0 else -1)
                em.op("v", lambda v, mi=mi, sign=sign, d=d: v.tensor_scalar(
                    masks[:, (2 * mi + sign) * 128:(2 * mi + sign + 1) * 128],
                    B.iota_fp[:, 0:128], float(d), None, OP.is_equal),
                    reads=[("iota_fp",)], writes=[(tag, "mask", mi, sign)])

        def build_chain(w_dram, big, MT, MT_key):
            nstages = 5 if big else 7
            for si in range(nstages):
                s_abs = (7 + si) if big else si
                em.dma("wg", "g", wstage[0:1, :],
                       w_dram[s_abs].rearrange("k a b -> (k a b)").rearrange("(o p) -> o p", o=1),
                       writes=[("wstage",)])
                em.op("v", lambda v: v.memset(stg[0:1, :], 0.0), writes=[("stg",)])
                if not big:
                    s2 = 2 ** si
                    q = 2048 // s2
                    for base, off in ((0, 0), (s2, 3), (F, 1), (2 * F + s2, 2)):
                        em.op("v", lambda v, s2=s2, q=q, off=off, base=base: v.tensor_copy(
                            bass.AP(stg, base, [[3 * F, 1], [2 * s2, q], [1, s2]]),
                            bass.AP(wstage, off, [[8192, 1], [4 * s2, q], [4, s2]])),
                            reads=[("wstage",)], writes=[("stg",)])
                    mi = si
                else:
                    sg = 2 ** si
                    nhi = 32 // (2 * sg)
                    for part, bit, off in ((0, 0, 0), (0, 1, 3), (F, 0, 1), (2 * F, 1, 2)):
                        for r in range(4):
                            em.op("v", lambda v, sg=sg, nhi=nhi, part=part, bit=bit,
                                  off=off, r=r: v.tensor_copy(
                                bass.AP(stg, part + 4 * sg * bit + r,
                                        [[3 * F, 1], [8 * sg, nhi], [4, sg], [128, 32]]),
                                bass.AP(wstage, off + 4 * r,
                                        [[8192, 1], [512 * sg, nhi], [512, sg], [16, 32]])),
                                reads=[("wstage",)], writes=[("stg",)])
                    mi = si + 2
                for b in range(32):
                    em.op("t", lambda t, b=b: t.matmul(
                        B.ps[0][:, 0:128], B.ones_f[0:1, 0:128], stg[0:1, b * 128:(b + 1) * 128],
                        start=True, stop=True),
                        reads=[("stg",), ("ones_f",)], writes=[("ps", 0)])
                    em.op("v", lambda v: v.tensor_tensor(acc[:, :], B.ps[0][:, 0:128],
                                                         B.ident[:, :], OP.mult),
                          reads=[("ps", 0), ("ident",)], writes=[("acc",)])
                    em.op("t", lambda t, b=b: t.matmul(
                        B.ps[0][:, 0:128], B.ones_f[0:1, 0:128],
                        stg[0:1, F + b * 128:F + (b + 1) * 128],
                        start=True, stop=True),
                        reads=[("stg",), ("ones_f",)], writes=[("ps", 0)])
                    em.op("v", lambda v, mi=mi: v.tensor_tensor(
                        tmp[:, :], B.ps[0][:, 0:128],
                        masks[:, (2 * mi + 1) * 128:(2 * mi + 2) * 128], OP.mult),
                        reads=[("ps", 0), (tag, "mask", mi, 1)], writes=[("tmp",)])
                    em.op("v", lambda v: v.tensor_tensor(acc[:, :], acc[:, :], tmp[:, :], OP.add),
                          reads=[("acc",), ("tmp",)], writes=[("acc",)])
                    em.op("t", lambda t, b=b: t.matmul(
                        B.ps[0][:, 0:128], B.ones_f[0:1, 0:128],
                        stg[0:1, 2 * F + b * 128:2 * F + (b + 1) * 128],
                        start=True, stop=True),
                        reads=[("stg",), ("ones_f",)], writes=[("ps", 0)])
                    em.op("v", lambda v, mi=mi: v.tensor_tensor(
                        tmp[:, :], B.ps[0][:, 0:128],
                        masks[:, (2 * mi) * 128:(2 * mi + 1) * 128], OP.mult),
                        reads=[("ps", 0), (tag, "mask", mi, 0)], writes=[("tmp",)])
                    em.op("v", lambda v: v.tensor_tensor(acc[:, :], acc[:, :], tmp[:, :], OP.add),
                          reads=[("acc",), ("tmp",)], writes=[("acc",)])
                    src = B.ident[:, :] if si == 0 else vb[(si + 1) % 2][:, b * 128:(b + 1) * 128]
                    src_key = ("ident",) if si == 0 else ("vb", (si + 1) % 2, b)
                    em.op("t", lambda t, src=src: t.matmul(
                        B.ps[1][:, 0:128], acc[:, :], src, start=True, stop=True),
                        reads=[("acc",), src_key], writes=[("ps", 1)])
                    em.op("v", lambda v, b=b, si=si: v.tensor_copy(
                        vb[si % 2][:, b * 128:(b + 1) * 128], B.ps[1][:, 0:128]),
                        reads=[("ps", 1)], writes=[("vb", si % 2, b)])
            last = (nstages - 1) % 2
            for b in range(32):
                em.op("t", lambda t, b=b, last=last: t.matmul(
                    B.ps[1][:, 0:128], vb[last][:, b * 128:(b + 1) * 128], B.ident[:, :],
                    start=True, stop=True),
                    reads=[("vb", last, b), ("ident",)], writes=[("ps", 1)])
                em.op("v", lambda v, b=b: v.tensor_copy(MT[:, b * 128:(b + 1) * 128],
                                                        B.ps[1][:, 0:128]),
                      reads=[("ps", 1)], writes=[MT_key])

        build_chain(up_w, False, MuT, (tag, "MuT"))
        build_chain(up_w, True, GuT, (tag, "GuT"))
        build_chain(down_w, False, MdT, (tag, "MdT"))
        build_chain(down_w, True, GdT, (tag, "GdT"))
    em.barrier()

    xlt = ctx.enter_context(nc.sbuf_tensor(f"{tag}_xlt", [128, NC * T], BF16))
    w1eff = ctx.enter_context(nc.sbuf_tensor(f"{tag}_w1eff", [128, NC * F], BF16))
    bb = ctx.enter_context(nc.sbuf_tensor(f"{tag}_bb", [128, D], F32))
    hstage = ctx.enter_context(nc.sbuf_tensor(f"{tag}_hst", [128, 512], BF16))
    gstage = ctx.enter_context(nc.sbuf_tensor(f"{tag}_gst", [128, 512], BF16))
    zstage = ctx.enter_context(nc.sbuf_tensor(f"{tag}_zst", [128, 512], BF16))
    gnorm = ctx.enter_context(nc.sbuf_tensor(f"{tag}_gn", [128, 512], BF16))
    zshuf = ctx.enter_context(nc.sbuf_tensor(f"{tag}_zs", [128, 512], BF16))
    h5l = ctx.enter_context(nc.sbuf_tensor(f"{tag}_h5l", [128, 4 * 128], BF16))
    wcs = ctx.enter_context(nc.sbuf_tensor(f"{tag}_wcs", [128, 4 * D], BF16))
    yscr = ctx.enter_context(nc.sbuf_tensor(f"{tag}_yscr", [128, D], F32))
    xres = ctx.enter_context(nc.sbuf_tensor(f"{tag}_xres", [128, D], F32))
    out_sb = ctx.enter_context(nc.sbuf_tensor(f"{tag}_out", [128, D], F32))
    brow = xres[0:1, :]
    brow2 = yscr[0:1, :]

    xlt_keys = emit_ln_xlt(B, x_dram, xlt, tag)

    with ExitStack() as ctx3:
        bcol = ctx3.enter_context(nc.sbuf_tensor(f"{tag}_bcol", [128, 32], BF16))
        bce = ctx3.enter_context(nc.sbuf_tensor(f"{tag}_bce", [128, 128], F32))
        wexpb = ctx3.enter_context(nc.sbuf_tensor(f"{tag}_wex", [128, D], BF16))
        for b in range(32):
            em.dma("wg", "g", wexpb[:, :],
                   W_expT[b * 128:(b + 1) * 128, :], writes=[("wexpb",)])
            for c in range(NC):
                em.op("t", lambda t, b=b, c=c: t.matmul(
                    B.ps[2][:, 0:128], wexpb[:, c * 128:(c + 1) * 128],
                    MuT[:, b * 128:(b + 1) * 128], start=True, stop=True),
                    reads=[("wexpb",), (tag, "MuT")], writes=[("ps", 2)])
                em.op("v", lambda v, b=b, c=c: v.tensor_copy(
                    bass.AP(w1eff, c * F + 4 * b, [[NC * F, 128], [128, 32], [1, 4]]),
                    B.ps[2][:, 0:128].rearrange("k (g r) -> k g r", r=4)),
                    reads=[("ps", 2)], writes=[("w1eff",)])
        for b in range(32):
            em.dma("wg", "g", bcol[:, b:b + 1],
                   b_exp[b * 128:(b + 1) * 128].rearrange("(p o) -> p o", o=1),
                   writes=[("bcol",)])
        em.op("t", lambda t: [
            t.matmul(B.ps[2][:, b:b + 1], MuT[:, b * 128:(b + 1) * 128], bcol[:, b:b + 1],
                     start=(b == 0), stop=(b == 31)) for b in range(32)][-1],
            reads=[(tag, "MuT"), ("bcol",)], writes=[("ps", 2)])
        em.op("v", lambda v: v.tensor_copy(bce[:, 0:32], B.ps[2][:, 0:32]),
              reads=[("ps", 2)], writes=[("bce",)])
        em.op("t", lambda t: t.matmul(B.ps[3][0:32, 0:128], bce[:, 0:32], B.iden_f32[:, :],
                                      start=True, stop=True),
              reads=[("bce",), ("iden_f32",)], writes=[("ps", 3)])
        em.op("v", lambda v: v.tensor_copy(bce[0:32, 0:128], B.ps[3][0:32, 0:128]),
              reads=[("ps", 3)], writes=[("bceT",)])
        em.dma("bf", "y", b1dram.rearrange("(b k) -> b k", k=128), bce[0:32, 0:128],
               reads=[("bceT",)], writes=[(tag, "b1d")])
        for g in range(32):
            em.dma("bf", "y", b1sh[:, g:g + 1],
                   bass.AP(b1dram.tensor, 4 * g, [[128, 32], [1, 4]]),
                   reads=[(tag, "b1d")], writes=[(tag, "b1sh")])
            em.dma("wg", "g", upbsh[:, g:g + 1],
                   bass.AP(up_b.tensor, 4 * g, [[128, 32], [1, 4]]),
                   writes=[(tag, "upbsh")])

    em.dma("wg", "g", brow[0:1, 0:1024], b_con.rearrange("(o p) -> o p", o=1),
           writes=[("brow",)])
    em.dma("wg", "g", brow2[0:1, 0:1024], down_b.rearrange("(o p) -> o p", o=1),
           writes=[("brow2",)])
    em.op("v", lambda v: v.tensor_tensor(brow[0:1, :], brow[0:1, :], brow2[0:1, :], OP.add),
          reads=[("brow",), ("brow2",)], writes=[("brow",)])
    for half in range(2):
        em.op("t", lambda t, half=half: t.matmul(
            B.ps[3][:, :], B.ones_f[0:1, 0:128], brow[0:1, half * 512:half * 512 + 512],
            start=True, stop=True),
            reads=[("brow",), ("ones_f",)], writes=[("ps", 3)])
        em.op("v", lambda v, half=half: v.tensor_copy(bb[:, half * 512:half * 512 + 512],
                                                      B.ps[3][:, :]),
              reads=[("ps", 3)], writes=[("bb", half)])

    for tc in range(NTC):
        for g in range(32):
            em.op("t", lambda t, tc=tc, g=g: [
                t.matmul(B.ps[4][:, :], w1eff[:, c * F + g * 128: c * F + (g + 1) * 128],
                         xlt_chunk(xlt, c, tc * 512, 512),
                         start=(c == 0), stop=(c == NC - 1)) for c in range(NC)][-1],
                reads=[("w1eff",)] + xlt_keys, writes=[("ps", 4)])
            em.op("v", lambda v, g=g: v.tensor_scalar(hstage[:, :], B.ps[4][:, :],
                                                      b1sh[:, g:g + 1], None, OP.add),
                  reads=[("ps", 4), (tag, "b1sh")], writes=[("hstage",)])
            em.op("t", lambda t, g=g: t.matmul(B.ps[5][:, :], GuT[:, g * 128:(g + 1) * 128],
                                               hstage[:, :], start=True, stop=True),
                  reads=[("hstage",), (tag, "GuT")], writes=[("ps", 5)])
            em.op("s", lambda s, g=g: s.activation(gstage[:, :], B.ps[5][:, :], AF.Gelu,
                                                   bias=upbsh[:, g:g + 1], scale=1.0),
                  reads=[("ps", 5), (tag, "upbsh")], writes=[("gstage",)])
            em.dma("bf", "y",
                   bass.AP(gdram.tensor, 4 * g * 512, [[128 * 512, 32], [512, 4], [1, 512]]),
                   gstage[:, :], reads=[("gstage",)], writes=[(tag, "gd", g)])
        for b in range(32):
            em.dma("bf2", "s", gnorm[:, :], gdram[b * 128:(b + 1) * 128, :],
                   reads=[(tag, "gd", gg2) for gg2 in range(32)], writes=[("gnorm",)])
            em.op("t", lambda t, b=b: t.matmul(B.ps[2][:, :], MdT[:, b * 128:(b + 1) * 128],
                                               gnorm[:, :], start=True, stop=True),
                  reads=[("gnorm",), (tag, "MdT")], writes=[("ps", 2)])
            em.op("v", lambda v: v.tensor_copy(zstage[:, :], B.ps[2][:, :]),
                  reads=[("ps", 2)], writes=[("zstage",)])
            em.dma("bf", "y", zdram[b * 128:(b + 1) * 128, :], zstage[:, :],
                   reads=[("zstage",)], writes=[(tag, "zd", b)])
        for g in range(32):
            em.dma("bf2", "s", zshuf[:, :],
                   bass.AP(zdram.tensor, 4 * g * 512, [[128 * 512, 32], [512, 4], [1, 512]]),
                   reads=[(tag, "zd", bb2) for bb2 in range(32)], writes=[("zshuf",)])
            em.op("t", lambda t, g=g: t.matmul(B.ps[3][:, :], GdT[:, g * 128:(g + 1) * 128],
                                               zshuf[:, :], start=True, stop=True),
                  reads=[("zshuf",), (tag, "GdT")], writes=[("ps", 3)])
            em.op("v", lambda v: v.tensor_copy(zstage[:, :], B.ps[3][:, :]),
                  reads=[("ps", 3)], writes=[("zstage",)])
            em.dma("bf", "y", h5dram[g * 128:(g + 1) * 128, tc * 512:(tc + 1) * 512],
                   zstage[:, :], reads=[("zstage",)], writes=[(tag, "h5", tc, g)])

    h5keys = [(tag, "h5", tc, g) for tc in range(NTC) for g in range(32)]
    for bat in range(8):
        for gi in range(4):
            g = bat * 4 + gi
            em.dma("wg", "g", wcs[:, gi * D:(gi + 1) * D],
                   bass.AP(W_con.tensor, 4 * g * D, [[128 * D, 32], [D, 4], [1, D]]),
                   writes=[("wcs", gi)])
        for t in range(NT):
            em.dma("oc", "y", h5l[:, :],
                   bass.AP(h5dram.tensor, bat * 4 * 128 * T + t * 128,
                           [[T, 128], [128 * T, 4], [1, 128]]),
                   reads=h5keys, writes=[("h5l",)])
            for half in range(2):
                em.op("t", lambda tt, half=half: [
                    tt.matmul(B.ps[2 + half][:, :], h5l[:, gi * 128:(gi + 1) * 128],
                              wcs[:, gi * D + half * 512: gi * D + half * 512 + 512],
                              start=(gi == 0), stop=(gi == 3)) for gi in range(4)][-1],
                    reads=[("h5l",)] + [("wcs", gi) for gi in range(4)],
                    writes=[("ps", 2 + half)])
                em.op("v", lambda v, half=half: v.tensor_copy(
                    yscr[:, half * 512:half * 512 + 512], B.ps[2 + half][:, :]),
                    reads=[("ps", 2 + half)], writes=[("ysc", half)])
            kw = {} if bat == 0 else {"accum_op": OP.add}
            em.dma("ya", "g", y_hbm[t * 128:(t + 1) * 128, :], yscr[:, :],
                   reads=[("ysc", 0), ("ysc", 1)],
                   writes=[(tag, "yhbm", t) if bat == 7 else (tag, "yhbm_p", t, bat)], **kw)

    for t in range(NT):
        em.dma("xy", "y", xres[:, :], x_dram[t * 128:(t + 1) * 128, :], writes=[("xres",)])
        em.dma("yl", "y", yscr[:, :], y_hbm[t * 128:(t + 1) * 128, :],
               reads=[(tag, "yhbm", t), ("ysc", 0), ("ysc", 1)], writes=[("yscr_f",)])
        for half in range(2):
            em.op("v", lambda v, half=half: v.tensor_tensor(
                out_sb[:, half * 512:half * 512 + 512], yscr[:, half * 512:half * 512 + 512],
                bb[:, half * 512:half * 512 + 512], OP.add),
                reads=[("yscr_f",), ("bb", half)], writes=[("osb", half)])
            em.op("v", lambda v, half=half: v.tensor_tensor(
                out_sb[:, half * 512:half * 512 + 512],
                out_sb[:, half * 512:half * 512 + 512],
                xres[:, half * 512:half * 512 + 512], OP.add),
                reads=[("osb", half), ("xres",)], writes=[("osb", half)])
        em.dma("xo", "y", xout_dram[t * 128:(t + 1) * 128, :], out_sb[:, :],
               reads=[("osb", 0), ("osb", 1)])



def build_l2a_program():
    nc = bass.Bass("TRN2", target_bir_lowering=False)
    x = nc.dram_tensor("x", [T, D], F32, kind="ExternalInput")
    names = ["wqT", "bq", "wkT", "bk", "wvT", "bv", "woT", "bo"]
    hs = []
    for n in names:
        if n.startswith("w"):
            hs.append(nc.dram_tensor(n, [D, D], F32, kind="ExternalInput"))
        else:
            hs.append(nc.dram_tensor(n, [D], F32, kind="ExternalInput"))
    xout = nc.dram_tensor("xout", [T, D], F32, kind="ExternalOutput")
    with ExitStack() as ctx:
        em = Em(nc, ctx)
        block = ctx.enter_context(nc.Block())
        B = alloc_common(nc, ctx, em)
        emit_l2a(B, x.ap(), *[h.ap() for h in hs], xout.ap())
        em.finalize(block)
    return nc


def build_l3a_program():
    nc = bass.Bass("TRN2", target_bir_lowering=False)
    x = nc.dram_tensor("x", [T, D], F32, kind="ExternalInput")
    wqkvT = nc.dram_tensor("wqkvT", [D, 3 * D], F32, kind="ExternalInput")
    bqkv = nc.dram_tensor("bqkv", [3 * D], F32, kind="ExternalInput")
    woutT = nc.dram_tensor("woutT", [D, D], F32, kind="ExternalInput")
    bout = nc.dram_tensor("bout", [D], F32, kind="ExternalInput")
    xout = nc.dram_tensor("xout", [T, D], F32, kind="ExternalOutput")
    with ExitStack() as ctx:
        em = Em(nc, ctx)
        block = ctx.enter_context(nc.Block())
        B = alloc_common(nc, ctx, em)
        emit_l3a(B, x.ap(), wqkvT.ap(), bqkv.ap(), woutT.ap(), bout.ap(), xout.ap())
        em.finalize(block)
    return nc


def build_l2f_program():
    nc = bass.Bass("TRN2", target_bir_lowering=False)
    x = nc.dram_tensor("x", [T, D], F32, kind="ExternalInput")
    WexpT = nc.dram_tensor("WexpT", [4096, D], F32, kind="ExternalInput")
    b_exp = nc.dram_tensor("b_exp", [4096], F32, kind="ExternalInput")
    up_w = nc.dram_tensor("up_w", [12, 2048, 2, 2], F32, kind="ExternalInput")
    up_b = nc.dram_tensor("up_b", [4096], F32, kind="ExternalInput")
    down_w = nc.dram_tensor("down_w", [12, 2048, 2, 2], F32, kind="ExternalInput")
    W_con = nc.dram_tensor("W_con", [4096, D], F32, kind="ExternalInput")
    b_con = nc.dram_tensor("b_con", [D], F32, kind="ExternalInput")
    down_b = nc.dram_tensor("down_b", [D], F32, kind="ExternalInput")
    xout = nc.dram_tensor("xout", [T, D], F32, kind="ExternalOutput")
    with ExitStack() as ctx:
        em = Em(nc, ctx)
        block = ctx.enter_context(nc.Block())
        B = alloc_common(nc, ctx, em)
        emit_l2f(B, x.ap(), WexpT.ap(), b_exp.ap(), up_w.ap(), up_b.ap(),
                 down_w.ap(), W_con.ap(), b_con.ap(), down_b.ap(), xout.ap())
        em.finalize(block)
    return nc


def rope_tables_np():
    dim = 32
    inv_freq = 1.0 / 10000.0 ** (np.arange(0, dim, 2, dtype=np.float32) / dim)
    tt = np.arange(T, dtype=np.float32)
    freqs = np.outer(tt, inv_freq)
    emb = np.concatenate([freqs, freqs], -1)
    cos = np.cos(emb).astype(np.float32)
    sin = np.sin(emb).astype(np.float32)
    cos4 = np.ascontiguousarray(np.tile(cos.T, (4, 1)))
    sin4 = np.ascontiguousarray(np.tile(sin.T, (4, 1)))
    RT = np.zeros((128, 128), dtype=np.float32)
    for b in range(0, 128, 32):
        for j in range(16):
            RT[b + j + 16, b + j] = -1.0
            RT[b + j, b + j + 16] = 1.0
    return cos4, sin4, RT


def build_l1a_program():
    nc = bass.Bass("TRN2", target_bir_lowering=False)
    x = nc.dram_tensor("x", [T, D], F32, kind="ExternalInput")
    names = [("W_kv", [D, 256]), ("g_kv", [256]), ("W_kdec", [256, D]),
             ("W_vdec", [256, D]), ("W_q", [D, 512]), ("W_qdec", [512, D]),
             ("W_krope", [D, 512]), ("W_qrope", [512, 512]), ("W_o", [D, D]),
             ("cos4", [128, T]), ("sin4", [128, T]), ("rot_permT", [128, 128])]
    hs = {n: nc.dram_tensor(n, sh, F32, kind="ExternalInput") for n, sh in names}
    xout = nc.dram_tensor("xout", [T, D], F32, kind="ExternalOutput")
    with ExitStack() as ctx:
        em = Em(nc, ctx)
        block = ctx.enter_context(nc.Block())
        B = alloc_common(nc, ctx, em)
        emit_l1a(B, x.ap(), *[hs[n].ap() for n, _ in names], xout.ap())
        em.finalize(block)
    return nc


def build_l1f_program():
    nc = bass.Bass("TRN2", target_bir_lowering=False)
    x = nc.dram_tensor("x", [T, D], F32, kind="ExternalInput")
    wT = nc.dram_tensor("wT", [D, D], F32, kind="ExternalInput")
    xout = nc.dram_tensor("xout", [T, D], F32, kind="ExternalOutput")
    with ExitStack() as ctx:
        em = Em(nc, ctx)
        block = ctx.enter_context(nc.Block())
        B = alloc_common(nc, ctx, em)
        emit_l1f(B, x.ap(), wT.ap(), xout.ap())
        em.finalize(block)
    return nc


def build_l0a_program(dbg=False):
    nc = bass.Bass("TRN2", target_bir_lowering=False)
    x = nc.dram_tensor("x", [T, D], F32, kind="ExternalInput")
    wqkvT = nc.dram_tensor("wqkvT", [D, 3 * D], F32, kind="ExternalInput")
    bqkv = nc.dram_tensor("bqkv", [3 * D], F32, kind="ExternalInput")
    wprojT = nc.dram_tensor("wprojT", [D, D], F32, kind="ExternalInput")
    bproj = nc.dram_tensor("bproj", [D], F32, kind="ExternalInput")
    xout = nc.dram_tensor("xout", [T, D], F32, kind="ExternalOutput")
    with ExitStack() as ctx:
        em = Em(nc, ctx)
        block = ctx.enter_context(nc.Block())
        B = alloc_common(nc, ctx, em)
        emit_l0a(B, x.ap(), wqkvT.ap(), bqkv.ap(), wprojT.ap(), bproj.ap(), xout.ap(), dbg=dbg)
        em.finalize(block)
    return nc


def build_l0f_program():
    nc = bass.Bass("TRN2", target_bir_lowering=False)
    x = nc.dram_tensor("x", [T, D], F32, kind="ExternalInput")
    wg = nc.dram_tensor("wg", [D, 8], F32, kind="ExternalInput")
    w1 = nc.dram_tensor("w1", [8, D, 2048], F32, kind="ExternalInput")
    w3 = nc.dram_tensor("w3", [8, D, 2048], F32, kind="ExternalInput")
    w2 = nc.dram_tensor("w2", [8, 2048, D], F32, kind="ExternalInput")
    xout = nc.dram_tensor("xout", [T, D], F32, kind="ExternalOutput")
    with ExitStack() as ctx:
        em = Em(nc, ctx)
        block = ctx.enter_context(nc.Block())
        B = alloc_common(nc, ctx, em)
        emit_l0f(B, x.ap(), wg.ap(), [w1.ap()[e] for e in range(8)],
                 [w3.ap()[e] for e in range(8)], [w2.ap()[e] for e in range(8)], xout.ap())
        em.finalize(block)
    return nc


def build_l3f_program():
    nc = bass.Bass("TRN2", target_bir_lowering=False)
    x = nc.dram_tensor("x", [T, D], F32, kind="ExternalInput")
    gwT = nc.dram_tensor("gwT", [D, 15], F32, kind="ExternalInput")
    fc1T = nc.dram_tensor("fc1T", [15, D, 2048], F32, kind="ExternalInput")
    fc1b = nc.dram_tensor("fc1b", [15, 2048], F32, kind="ExternalInput")
    gg = nc.dram_tensor("gg", [15, D], F32, kind="ExternalInput")
    fc2T = nc.dram_tensor("fc2T", [15, D, D], F32, kind="ExternalInput")
    b16 = nc.dram_tensor("b16", [16, D], F32, kind="ExternalInput")
    s1T = nc.dram_tensor("s1T", [D, 4096], F32, kind="ExternalInput")
    s1b = nc.dram_tensor("s1b", [4096], F32, kind="ExternalInput")
    sgg = nc.dram_tensor("sgg", [2048], F32, kind="ExternalInput")
    s2T = nc.dram_tensor("s2T", [2048, D], F32, kind="ExternalInput")
    xout = nc.dram_tensor("xout", [T, D], F32, kind="ExternalOutput")
    with ExitStack() as ctx:
        em = Em(nc, ctx)
        block = ctx.enter_context(nc.Block())
        B = alloc_common(nc, ctx, em)
        emit_l3f(B, x.ap(), gwT.ap(), [fc1T.ap()[e] for e in range(15)], fc1b.ap(),
                 gg.ap(), [fc2T.ap()[e] for e in range(15)], b16.ap(),
                 s1T.ap(), s1b.ap(), sgg.ap(), s2T.ap(), xout.ap())
        em.finalize(block)
    return nc



def _prep_stages(inputs):
    i = {k: np.ascontiguousarray(v) for k, v in inputs.items()}
    ct = np.ascontiguousarray
    cos4, sin4, rotT = rope_tables_np()
    stages = [
        ("l0a", build_l0a_program,
         {"wqkvT": ct(i["L0_attn_c_attn_kernel"].T), "bqkv": i["L0_attn_c_attn_bias"],
          "wprojT": ct(i["L0_attn_c_proj_kernel"].T), "bproj": i["L0_attn_c_proj_bias"]}),
        ("l0f", build_l0f_program,
         {"wg": i["L0_ffn_w_gate"], "w1": i["L0_ffn_w1"], "w3": i["L0_ffn_w3"],
          "w2": i["L0_ffn_w2"]}),
        ("l1a", build_l1a_program,
         {"W_kv": i["L1_attn_W_kv"], "g_kv": i["L1_attn_g_kv"],
          "W_kdec": i["L1_attn_W_kdec"], "W_vdec": i["L1_attn_W_vdec"],
          "W_q": i["L1_attn_W_q"], "W_qdec": i["L1_attn_W_qdec"],
          "W_krope": i["L1_attn_W_krope"], "W_qrope": i["L1_attn_W_qrope"],
          "W_o": i["L1_attn_W_o"], "cos4": cos4, "sin4": sin4, "rot_permT": rotT}),
        ("l1f", build_l1f_program, {"wT": ct(i["L1_ffn_weight"].T)}),
        ("l2a", build_l2a_program,
         {"wqT": ct(i["L2_attn_Wq"].T), "bq": i["L2_attn_bq"],
          "wkT": ct(i["L2_attn_Wk"].T), "bk": i["L2_attn_bk"],
          "wvT": ct(i["L2_attn_Wv"].T), "bv": i["L2_attn_bv"],
          "woT": ct(i["L2_attn_Wo"].T), "bo": i["L2_attn_bo"]}),
        ("l2f", build_l2f_program,
         {"WexpT": ct(i["L2_ffn_W_exp"].T), "b_exp": i["L2_ffn_b_exp"],
          "up_w": i["L2_ffn_up_weights"], "up_b": i["L2_ffn_up_bias"],
          "down_w": i["L2_ffn_down_weights"], "W_con": i["L2_ffn_W_con"],
          "b_con": i["L2_ffn_b_con"], "down_b": i["L2_ffn_down_bias"]}),
        ("l3a", build_l3a_program,
         {"wqkvT": ct(i["L3_attn_W_qkv"].T), "bqkv": i["L3_attn_b_qkv"],
          "woutT": ct(i["L3_attn_W_out"].T), "bout": i["L3_attn_b_out"]}),
        ("l3f", build_l3f_program,
         {"gwT": ct(i["L3_ffn_gate_w"].T), "fc1T": ct(i["L3_ffn_fc1_w"].transpose(0, 2, 1)),
          "fc1b": i["L3_ffn_fc1_b"], "gg": i["L3_ffn_geglu_mult"],
          "fc2T": ct(i["L3_ffn_fc2_w"].transpose(0, 2, 1)),
          "b16": ct(np.concatenate([i["L3_ffn_fc2_b"], i["L3_ffn_s_fc2_b"][None]], 0)),
          "s1T": ct(i["L3_ffn_s_fc1_w"].T), "s1b": i["L3_ffn_s_fc1_b"],
          "sgg": i["L3_ffn_s_geglu_mult"], "s2T": ct(i["L3_ffn_s_fc2_w"].T)}),
    ]
    return stages


def build_fused_program(dbg=False):
    nc = bass.Bass("TRN2", target_bir_lowering=False)
    x_in = nc.dram_tensor("x", [T, D], F32, kind="ExternalInput")
    xout = nc.dram_tensor("xout", [T, D], F32, kind="ExternalOutput")
    kindmid = {"kind": "ExternalOutput"} if dbg else {}
    xmid = [nc.dram_tensor(f"xmid{i}", [T, D], F32, **kindmid) for i in range(7)]
    xs = [x_in.ap()] + [t.ap() for t in xmid] + [xout.ap()]

    def din(stage, name, shape):
        return nc.dram_tensor(f"in_{stage}_{name}", shape, F32, kind="ExternalInput").ap()

    with ExitStack() as ctx:
        em = Em(nc, ctx)
        block = ctx.enter_context(nc.Block())
        B = alloc_common(nc, ctx, em)
        gctx = B.ctx

        def stage(i, fn):
            with ExitStack() as sctx:
                B.ctx = sctx
                fn(xs[i], xs[i + 1])
                B.ctx = gctx
            em.barrier()

        stage(0, lambda xi, xo: emit_l0a(
            B, xi, din("l0a", "wqkvT", [D, 3 * D]), din("l0a", "bqkv", [3 * D]),
            din("l0a", "wprojT", [D, D]), din("l0a", "bproj", [D]), xo))
        w1 = din("l0f", "w1", [8, D, 2048])
        w3 = din("l0f", "w3", [8, D, 2048])
        w2 = din("l0f", "w2", [8, 2048, D])
        stage(1, lambda xi, xo: emit_l0f(
            B, xi, din("l0f", "wg", [D, 8]), [w1[e] for e in range(8)],
            [w3[e] for e in range(8)], [w2[e] for e in range(8)], xo))
        stage(2, lambda xi, xo: emit_l1a(
            B, xi, din("l1a", "W_kv", [D, 256]), din("l1a", "g_kv", [256]),
            din("l1a", "W_kdec", [256, D]), din("l1a", "W_vdec", [256, D]),
            din("l1a", "W_q", [D, 512]), din("l1a", "W_qdec", [512, D]),
            din("l1a", "W_krope", [D, 512]), din("l1a", "W_qrope", [512, 512]),
            din("l1a", "W_o", [D, D]), din("l1a", "cos4", [128, T]),
            din("l1a", "sin4", [128, T]), din("l1a", "rot_permT", [128, 128]), xo))
        stage(3, lambda xi, xo: emit_l1f(B, xi, din("l1f", "wT", [D, D]), xo))
        stage(4, lambda xi, xo: emit_l2a(
            B, xi, din("l2a", "wqT", [D, D]), din("l2a", "bq", [D]),
            din("l2a", "wkT", [D, D]), din("l2a", "bk", [D]),
            din("l2a", "wvT", [D, D]), din("l2a", "bv", [D]),
            din("l2a", "woT", [D, D]), din("l2a", "bo", [D]), xo))
        stage(5, lambda xi, xo: emit_l2f(
            B, xi, din("l2f", "WexpT", [4096, D]), din("l2f", "b_exp", [4096]),
            din("l2f", "up_w", [12, 2048, 2, 2]), din("l2f", "up_b", [4096]),
            din("l2f", "down_w", [12, 2048, 2, 2]), din("l2f", "W_con", [4096, D]),
            din("l2f", "b_con", [D]), din("l2f", "down_b", [D]), xo))
        stage(6, lambda xi, xo: emit_l3a(
            B, xi, din("l3a", "wqkvT", [D, 3 * D]), din("l3a", "bqkv", [3 * D]),
            din("l3a", "woutT", [D, D]), din("l3a", "bout", [D]), xo))
        fc1T = din("l3f", "fc1T", [15, D, 2048])
        fc2T = din("l3f", "fc2T", [15, D, D])
        stage(7, lambda xi, xo: emit_l3f(
            B, xi, din("l3f", "gwT", [D, 15]), [fc1T[e] for e in range(15)],
            din("l3f", "fc1b", [15, 2048]), din("l3f", "gg", [15, D]),
            [fc2T[e] for e in range(15)], din("l3f", "b16", [16, D]),
            din("l3f", "s1T", [D, 4096]), din("l3f", "s1b", [4096]),
            din("l3f", "sgg", [2048]), din("l3f", "s2T", [2048, D]), xo))
        em.finalize(block)
    return nc


def kernel_fused(**inputs):
    x = np.ascontiguousarray(np.asarray(inputs["L0_attn_hidden_states"], dtype=np.float32))
    ncores = x.shape[0]
    wmap = {}
    for name, build, wm in _prep_stages(inputs):
        for k, v in wm.items():
            wmap[f"in_{name}_{k}"] = v
    in_maps = [dict(wmap, x=np.ascontiguousarray(x[c])) for c in range(ncores)]
    res = _launch_retry(build_fused_program, in_maps, ncores, attempts=2)
    return np.stack([np.asarray(res.results[c]["xout"]) for c in range(ncores)], axis=0)


def _launch_retry(build, in_maps, ncores, attempts=3):
    import time as _time
    last = None
    for att in range(attempts):
        nc = build()
        try:
            return run_bass_kernel_spmd(nc, in_maps, core_ids=list(range(ncores)))
        except Exception as ex:
            last = ex
            _time.sleep(10)
    raise last


def kernel(**inputs):
    return kernel_fused(**inputs)


def kernel_unfused(**inputs):
    x = np.ascontiguousarray(np.asarray(inputs["L0_attn_hidden_states"], dtype=np.float32))
    ncores = x.shape[0]
    shards = [np.ascontiguousarray(x[c]) for c in range(ncores)]
    for name, build, wmap in _prep_stages(inputs):
        in_maps = [dict(wmap, x=shards[c]) for c in range(ncores)]
        res = _launch_retry(build, in_maps, ncores)
        shards = [np.asarray(res.results[c]["xout"]) for c in range(ncores)]
    return np.stack(shards, axis=0)
